# version 02
# baseline (speedup 1.0000x reference)
import numpy as np
import ml_dtypes
from contextlib import ExitStack

import concourse.bass as bass
import concourse.mybir as mybir
from concourse.bass_utils import run_bass_kernel_spmd

F32 = mybir.dt.float32
BF16 = mybir.dt.bfloat16
I32 = mybir.dt.int32
AF = mybir.ActivationFunctionType
ALU = mybir.AluOpType
AX = mybir.AxisListType
bfnp = ml_dtypes.bfloat16

NCORE = 8
TPC = 1024
SEQ = 4096
DM = 2048
LNEPS = 1e-5


class Buf:
    __slots__ = ("ap", "last_w", "readers", "name", "_subs", "is_psum")

    def __init__(self, ap, name="", is_psum=False):
        self.ap = ap
        self.last_w = None
        self.readers = []
        self.name = name
        self._subs = {}
        self.is_psum = is_psum

    def sub(self, sl, name=""):
        key = repr(sl)
        b = self._subs.get(key)
        if b is None:
            b = Buf(self.ap[sl], name or self.name)
            self._subs[key] = b
        return b


class Op:
    __slots__ = ("eng", "is_dma", "emit", "deps", "needs_inc", "cum", "idx")

    def __init__(self, eng, is_dma, emit, deps, idx):
        self.eng = eng
        self.is_dma = is_dma
        self.emit = emit
        self.deps = deps
        self.needs_inc = False
        self.cum = None
        self.idx = idx


class Sched:
    ENGS = ("T", "V", "S", "P", "Q")

    def __init__(self, nc, ctx):
        self.nc = nc
        self.ctx = ctx
        self.streams = {e: [] for e in self.ENGS}

    def sbuf(self, name, shape, dtype):
        h = self.ctx.enter_context(self.nc.sbuf_tensor(name, list(shape), dtype))
        return Buf(h[:], name)

    def psum(self, name, shape, dtype=F32):
        h = self.ctx.enter_context(self.nc.psum_tensor(name, list(shape), dtype))
        return Buf(h[:], name, is_psum=True)

    def dram_in(self, name, shape, dtype):
        return Buf(self.nc.dram_tensor(name, list(shape), dtype, kind="ExternalInput").ap(), name)

    def dram_out(self, name, shape, dtype):
        return Buf(self.nc.dram_tensor(name, list(shape), dtype, kind="ExternalOutput").ap(), name)

    def dram_tmp(self, name, shape, dtype):
        return Buf(self.nc.dram_tensor(name, list(shape), dtype, kind="Internal").ap(), name)

    def add(self, eng, emit, reads=(), writes=(), sreads=(), is_dma=False):
        idx = len(self.streams[eng])
        deps = []

        def dep_on(prod, force):
            if prod is None:
                return
            if (prod.eng == eng == "T" and not prod.is_dma and not is_dma
                    and not force):
                return
            deps.append(prod)

        for b in reads:
            dep_on(b.last_w, False)
            if b.is_psum:
                for r in b.readers:
                    if r.eng != eng:
                        dep_on(r, True)
        for b in sreads:
            dep_on(b.last_w, True)
        op = Op(eng, is_dma, emit, deps, idx)
        if is_dma:
            op.needs_inc = True
        for b in writes:
            dep_on(b.last_w, False)
            for r in b.readers:
                if r is not op:
                    dep_on(r, False)
            b.last_w = op
            b.readers = []
        for b in tuple(reads) + tuple(sreads):
            b.readers.append(op)
        seen, dd = set(), []
        for d in op.deps:
            if id(d) not in seen and d is not op:
                seen.add(id(d))
                dd.append(d)
                d.needs_inc = True
        op.deps = dd
        self.streams[eng].append(op)
        return op

    def dma(self, eng, out, in_, in_sl=None, out_sl=None, **kw):
        oa = out.ap[out_sl] if out_sl is not None else out.ap
        ia = in_.ap[in_sl] if in_sl is not None else in_.ap
        return self.add(eng, lambda e: e.dma_start(out=oa, in_=ia, **kw),
                        reads=(in_,), writes=(out,), is_dma=True)

    def finalize(self):
        nc = self.nc
        sems = {}
        for e in self.ENGS:
            for d in (False, True):
                sems[(e, d)] = self.ctx.enter_context(nc.semaphore(f"sm{e}{'d' if d else 'c'}"))
        totals = {}
        for e in self.ENGS:
            cum = {False: 0, True: 0}
            for op in self.streams[e]:
                if op.needs_inc:
                    cum[op.is_dma] += 16 if op.is_dma else 1
                    op.cum = cum[op.is_dma]
            totals[e] = dict(cum)
        block = self.ctx.enter_context(nc.Block())

        def emit_stream(e, eng_obj):
            waited = {}
            for op in self.streams[e]:
                for d in op.deps:
                    key = (d.eng, d.is_dma)
                    if waited.get(key, 0) < d.cum:
                        eng_obj.wait_ge(sems[key], d.cum)
                        waited[key] = d.cum
                instr = op.emit(eng_obj)
                if op.needs_inc:
                    instr.then_inc(sems[(e, op.is_dma)], 16 if op.is_dma else 1)

        @block.tensor
        def _(t):
            emit_stream("T", t)

        @block.vector
        def _(v):
            emit_stream("V", v)

        @block.scalar
        def _(s):
            emit_stream("S", s)

        @block.gpsimd
        def _(g):
            emit_stream("P", g)

        @block.sync
        def _(q):
            emit_stream("Q", q)
            for e in self.ENGS:
                for d in (True, False):
                    if totals[e][d]:
                        q.wait_ge(sems[(e, d)], totals[e][d])


def mm_acc(S, psum, terms, start=True, stop=True, width=None):
    n = len(terms)
    pa = psum.ap if width is None else psum.ap[:, :width]
    for i, (lb, ls, rb, rs) in enumerate(terms):
        la = lb.ap[ls] if ls is not None else lb.ap
        ra = rb.ap[rs] if rs is not None else rb.ap
        st = start and i == 0
        sp = stop and i == n - 1
        S.add("T", lambda e, la=la, ra=ra, st=st, sp=sp: e.matmul(pa, la, ra, start=st, stop=sp),
              reads=(lb, rb), writes=(psum,))


def mm_transpose(S, psum, src, ident, src_sl=None):
    sa = src.ap[src_sl] if src_sl is not None else src.ap
    S.add("T", lambda e, sa=sa: e.transpose(psum.ap, sa, ident.ap),
          reads=(src, ident), writes=(psum,))


def emit_ln_tile(S, x, xh, st8, junk, extra_stats=False):
    D = x.ap.shape[-1]
    negmu = st8.sub(np.s_[:, 0:1], "negmu")
    varsum = st8.sub(np.s_[:, 1:2], "varsum")
    rstd = st8.sub(np.s_[:, 2:3], "rstd")
    S.add("V", lambda e: e.tensor_reduce(negmu.ap, x.ap, axis=AX.X, op=ALU.add),
          reads=(x,), writes=(negmu,))
    S.add("V", lambda e: e.tensor_scalar(negmu.ap, negmu.ap, -1.0 / D, None, ALU.mult),
          reads=(negmu,), writes=(negmu,))
    S.add("S", lambda e: e.activation(junk.ap, x.ap, AF.Square, bias=negmu.ap,
                                      accum_out=varsum.ap),
          reads=(x,), sreads=(negmu,), writes=(junk, varsum))
    S.add("V", lambda e: e.tensor_scalar(rstd.ap, varsum.ap, 1.0 / D, LNEPS, ALU.mult, ALU.add),
          reads=(varsum,), writes=(rstd,))
    S.add("S", lambda e: e.activation(rstd.ap, rstd.ap, AF.Sqrt),
          reads=(rstd,), writes=(rstd,))
    S.add("V", lambda e: e.reciprocal(rstd.ap, rstd.ap),
          reads=(rstd,), writes=(rstd,))
    S.add("V", lambda e: e.tensor_scalar(xh.ap, x.ap, negmu.ap, rstd.ap, ALU.add, ALU.mult),
          reads=(x,), sreads=(negmu, rstd), writes=(xh,))
    if extra_stats:
        r2 = st8.sub(np.s_[:, 3:4], "r2")
        m = st8.sub(np.s_[:, 4:5], "m")
        t1 = st8.sub(np.s_[:, 5:6], "t1")
        S.add("V", lambda e: e.tensor_scalar(t1.ap, varsum.ap, 1.0 / D, LNEPS, ALU.mult, ALU.add),
              reads=(varsum,), writes=(t1,))
        S.add("V", lambda e: e.reciprocal(t1.ap, t1.ap), reads=(t1,), writes=(t1,))
        S.add("V", lambda e: e.tensor_scalar(t1.ap, t1.ap, -LNEPS, 1.0, ALU.mult, ALU.add),
              reads=(t1,), writes=(t1,))
        S.add("S", lambda e: e.activation(m.ap, t1.ap, AF.Sqrt, scale=float(D)),
              reads=(t1,), writes=(m,))
        S.add("V", lambda e: e.tensor_scalar(r2.ap, t1.ap, 1.0, LNEPS, ALU.mult, ALU.add),
              reads=(t1,), writes=(r2,))
        S.add("S", lambda e: e.activation(r2.ap, r2.ap, AF.Sqrt),
              reads=(r2,), writes=(r2,))
        S.add("V", lambda e: e.reciprocal(r2.ap, r2.ap), reads=(r2,), writes=(r2,))


def host_consts():
    c = {}
    c["ident"] = np.eye(128, dtype=np.float32).astype(bfnp)
    c["ones1x128"] = np.ones((1, 128), np.float32).astype(bfnp)
    freqs = (8000.0 / 200.0) ** np.linspace(0.0, 1.0, 64, dtype=np.float32) * 200.0 / 1000.0
    freqs = np.float32(200.0) * freqs / np.float32(1000.0)
    ang = np.outer(np.arange(SEQ, dtype=np.float32), freqs)
    u = ang / np.float32(2 * np.pi)
    red = (u - np.round(u)) * np.float32(2 * np.pi)
    sc = np.float32(128.0 ** -0.25)
    c["rot_cos"] = (np.cos(red) * sc).astype(np.float32)
    c["rot_sin"] = (np.sin(red) * sc).astype(np.float32)
    inv = np.float32(10000.0) ** (np.float32(-2.0) * np.arange(64, dtype=np.float32) / np.float32(128.0))
    th = np.arange(SEQ, dtype=np.float32)[:, None] * inv[None, :]
    u = th / np.float32(2 * np.pi)
    red = (u - np.round(u)) * np.float32(2 * np.pi)
    R = np.concatenate([np.cos(red) + np.sin(red)] * 2, axis=1).astype(np.float32)
    c["Rgrid"] = R
    c["Rgrid_q"] = (R / np.sqrt(np.float32(128.0))).astype(np.float32)
    c["triu64"] = np.triu(np.ones((64, 64), np.float32)).astype(bfnp)
    c["triu128"] = np.triu(np.ones((128, 128), np.float32)).astype(bfnp)
    c["triu128f"] = np.triu(np.ones((128, 128), np.float32))
    c["negmask128"] = np.where(np.tril(np.ones((128, 128), bool), -1), -1e30, 0.0).astype(np.float32)
    cm4 = np.zeros((4 * 128, 512), np.float32)
    for a in range(4):
        for i in range(128):
            cm4[a * 128 + i] = np.where(np.arange(512) <= a * 128 + i, 0.0, -1e30)
    c["cm4"] = cm4
    return c


def ceil_div(a, b):
    return -(-a // b)


PREP_SPECS = [
    ("l0f_gate", 5632, 2048, "dq"),
    ("l0f_up", 5632, 2048, "dq"),
    ("l0f_down", 2048, 5632, "dq"),
    ("mmb_inproj", 8576, 2048, "f32"),
    ("mmb_outproj", 2048, 4096, "f32"),
    ("l1b_wq", 2048, 2048, "f32"),
    ("l1b_wk", 512, 2048, "f32"),
    ("l1b_wv", 512, 2048, "f32"),
    ("l1b_wo", 2048, 2048, "f32"),
    ("l1f_gu", 16384, 2048, "tern"),
    ("l1f_proj", 2048, 8192, "tern"),
]


def prep_tiles_per_core(rows):
    nt = ceil_div(rows, 128)
    return ceil_div(nt, NCORE)


def build_t0():
    nc = bass.Bass("TRN2", target_bir_lowering=False)
    with ExitStack() as ctx:
        S = Sched(nc, ctx)
        ident = S.sbuf("ident", [128, 128], BF16)
        identin = S.dram_in("ident_in", [128, 128], BF16)
        S.dma("Q", ident, identin)
        ios = {}
        for name, rows, cols, kind in PREP_SPECS:
            tpc = prep_tiles_per_core(rows)
            din = S.dram_in(f"{name}_in", [tpc * 128, cols],
                            I32 if kind == "dq" else F32)
            dout = S.dram_out(f"{name}_out", [cols, tpc * 128], BF16)
            dsc = S.dram_in(f"{name}_s", [tpc * 128, 1], F32) if kind == "dq" else None
            ios[name] = (din, dout, tpc, cols, kind, dsc)

        wbuf = [S.sbuf(f"wb{i}", [128, 8192], BF16) for i in range(2)]
        obuf = [S.sbuf(f"ob{i}", [128, 8192], BF16) for i in range(2)]
        sbuf_s = S.sbuf("tern_s", [128, 8192 // 64], F32)
        sbuf_sbf = S.sbuf("tern_sbf", [128, 8192 // 64], BF16)
        sbuf_u1 = S.sbuf("tern_u1", [128, 8192], BF16)
        sbuf_u2 = S.sbuf("tern_u2", [128, 8192], BF16)
        pps = [S.psum(f"pp{i}", [128, 128], BF16) for i in range(4)]
        pidx = [0]

        def transpose_out(src, cols, dout, tile_i, ob):
            nch = cols // 128
            for ci in range(nch):
                pp = pps[pidx[0] % 4]
                pidx[0] += 1
                mm_transpose(S, pp, src, ident, np.s_[:, ci * 128:(ci + 1) * 128])
                dst = ob.sub(np.s_[:, ci * 128:(ci + 1) * 128], "obch")
                S.add("V", lambda e, pp=pp, dst=dst: e.tensor_copy(dst.ap, pp.ap),
                      reads=(pp,), writes=(ob,))
            da = dout.ap.rearrange("(c p) (t r) -> t p c r", p=128, r=128)[tile_i]
            sa = ob.ap[:, :nch * 128].rearrange("p (c r) -> p c r", r=128)
            S.add("Q", lambda e, da=da, sa=sa: e.dma_start(out=da, in_=sa),
                  reads=(ob,), writes=(dout,), is_dma=True)

        scol = S.sbuf("scol", [128, 1], F32)
        normwb = S.sbuf("normwb", [128, 4096], F32)
        S.dma("Q", normwb, S.dram_in("normw_b", [128, 4096], F32))
        for name, (din, dout, tpc, cols, kind, dsc) in ios.items():
            for ti in range(tpc):
                wb = wbuf[ti % 2]
                ob = obuf[ti % 2]
                S.dma("P", wb, din, in_sl=np.s_[ti * 128:(ti + 1) * 128, :],
                      out_sl=np.s_[:, :cols])
                src = wb
                if kind == "dq":
                    S.dma("Q", scol, dsc, in_sl=np.s_[ti * 128:(ti + 1) * 128, :])
                    wview = wb.ap[:, :cols]
                    S.add("V", lambda e, wview=wview, scol=scol: e.tensor_scalar(
                        wview, wview, -128.0, scol.ap, ALU.add, ALU.mult),
                        reads=(wb,), sreads=(scol,), writes=(wb,))
                if name == "mmb_outproj":
                    wview = wb.ap[:, :cols]
                    S.add("V", lambda e, wview=wview: e.tensor_tensor(
                        wview, wview, normwb.ap, op=ALU.mult),
                        reads=(wb, normwb), writes=(wb,))
                if kind == "tern":
                    ng = cols // 64
                    sv = sbuf_s.sub(np.s_[:, :ng], "s")
                    wv = wb.ap[:, :cols].rearrange("p (g k) -> p g k", k=64)
                    S.add("V", lambda e, sv=sv, wv=wv: e.tensor_reduce(
                        sv.ap, wv, axis=AX.X, op=ALU.add, apply_absolute_value=True),
                        reads=(wb,), writes=(sv,))
                    S.add("V", lambda e, sv=sv: e.tensor_scalar(
                        sv.ap, sv.ap, 1.0 / 64, None, ALU.mult),
                        reads=(sv,), writes=(sv,))
                    srow_bf = sbuf_sbf.sub(np.s_[:, :ng], "srowbf")
                    S.add("V", lambda e, sv=sv, srow_bf=srow_bf: e.tensor_copy(srow_bf.ap, sv.ap),
                          reads=(sv,), writes=(srow_bf,))
                    S.add("V", lambda e, sv=sv, srow_bf=srow_bf: e.tensor_scalar(
                        sv.ap, srow_bf.ap, 1e-8, None, ALU.max),
                        reads=(srow_bf,), writes=(sv,))
                    sb = sv.ap.unsqueeze(-1).broadcast_to([128, ng, 64])
                    w3 = wb.ap[:, :cols].rearrange("p (g k) -> p g k", k=64)
                    u13 = sbuf_u1.ap[:, :cols].rearrange("p (g k) -> p g k", k=64)
                    u23 = sbuf_u2.ap[:, :cols].rearrange("p (g k) -> p g k", k=64)
                    C_THR = 0.501953125
                    S.add("V", lambda e, w3=w3, sb=sb, u13=u13: e.scalar_tensor_tensor(
                        u13, sb, -C_THR, w3, ALU.mult, ALU.add),
                        reads=(wb, sv), writes=(sbuf_u1,))
                    S.add("V", lambda e, w3=w3, sb=sb, u23=u23: e.scalar_tensor_tensor(
                        u23, sb, C_THR, w3, ALU.mult, ALU.add),
                        reads=(wb, sv), writes=(sbuf_u2,))
                    S.add("S", lambda e, u13=u13: e.activation(u13, u13, AF.Sign),
                          reads=(sbuf_u1,), writes=(sbuf_u1,))
                    S.add("S", lambda e, u23=u23: e.activation(u23, u23, AF.Sign),
                          reads=(sbuf_u2,), writes=(sbuf_u2,))
                    S.add("V", lambda e, u13=u13, u23=u23: e.tensor_tensor(
                        u13, u13, u23, op=ALU.add),
                        reads=(sbuf_u1, sbuf_u2), writes=(sbuf_u1,))
                    S.add("V", lambda e, u13=u13, sb=sb, w3=w3: e.scalar_tensor_tensor(
                        w3, sb, 0.5, u13, ALU.mult, ALU.mult),
                        reads=(sbuf_u1, sv), writes=(wb,))
                transpose_out(src, cols, dout, ti, ob)
        S.finalize()
    return nc


def t0_host_inputs(inputs, consts):
    w = {
        "l0f_gate": inputs["L0_ffn_wq_gate"],
        "l0f_up": inputs["L0_ffn_wq_up"],
        "l0f_down": inputs["L0_ffn_wq_down"],
        "mmb_inproj": np.pad(np.asarray(inputs["L0_attn_b_in_proj_w"], np.float32),
                             ((0, 64), (0, 0))),
        "mmb_outproj": inputs["L0_attn_b_out_proj_w"],
        "l1b_wq": inputs["L1_attn_b_Wq"],
        "l1b_wk": inputs["L1_attn_b_Wk"],
        "l1b_wv": inputs["L1_attn_b_Wv"],
        "l1b_wo": inputs["L1_attn_b_Wo"],
        "l1f_gu": inputs["L1_ffn_w_gate_up"],
        "l1f_proj": inputs["L1_ffn_w_proj"],
    }
    maps = [dict() for _ in range(NCORE)]
    nwb = np.broadcast_to(np.asarray(inputs["L0_attn_b_norm_w"], np.float32), (128, 4096)).copy()
    for c in range(NCORE):
        maps[c]["ident_in"] = consts["ident"]
        maps[c]["normw_b"] = nwb
    scales = {
        "l0f_gate": inputs["L0_ffn_s_gate"],
        "l0f_up": inputs["L0_ffn_s_up"],
        "l0f_down": inputs["L0_ffn_s_down"],
    }
    for name, rows, cols, kind in PREP_SPECS:
        a = np.asarray(w[name])
        if kind == "dq":
            a = a.astype(np.int32)
        else:
            a = a.astype(np.float32)
        tpc = prep_tiles_per_core(rows)
        slab_rows = tpc * 128
        padded = np.zeros((NCORE * slab_rows, cols), a.dtype)
        padded[:a.shape[0]] = a
        for c in range(NCORE):
            maps[c][f"{name}_in"] = np.ascontiguousarray(
                padded[c * slab_rows:(c + 1) * slab_rows])
        if kind == "dq":
            s = np.zeros((NCORE * slab_rows, 1), np.float32)
            s[:rows, 0] = np.asarray(scales[name], np.float32)
            for c in range(NCORE):
                maps[c][f"{name}_s"] = np.ascontiguousarray(
                    s[c * slab_rows:(c + 1) * slab_rows])
    return maps


def t0_host_outputs(results):
    out = {}
    for name, rows, cols, kind in PREP_SPECS:
        tpc = prep_tiles_per_core(rows)
        full = np.concatenate([results[c][f"{name}_out"] for c in range(NCORE)], axis=1)
        out[name] = np.ascontiguousarray(full[:, :rows])
    return out


def build_l0ffn(debug=False):
    HID = 5632
    NT = TPC // 128
    NH_CH = HID // 512
    ND = DM // 128
    NK = HID // 128
    DC = 256
    nc = bass.Bass("TRN2", target_bir_lowering=False)
    with ExitStack() as ctx:
        S = Sched(nc, ctx)
        xin = S.dram_in("xin", [TPC, DM], F32)
        wgT = S.dram_in("wgT", [DM, HID], BF16)
        wuT = S.dram_in("wuT", [DM, HID], BF16)
        wdT = S.dram_in("wdT", [HID, DM], BF16)
        identin = S.dram_in("ident_in", [128, 128], BF16)
        xout = S.dram_out("xout", [TPC, DM], F32)

        ident = S.sbuf("ident", [128, 128], BF16)
        S.dma("Q", ident, identin)
        xT = S.sbuf("xT", [128, ND * TPC], BF16)
        hT = S.sbuf("hT", [128, NK * TPC], BF16)
        xtiles = [S.sbuf(f"xt{i}", [128, DM], F32) for i in range(1)]
        xh = S.sbuf("xh", [128, DM], BF16)
        junk = S.sbuf("junk", [128, DM], BF16)
        stats = S.sbuf("stats", [128, 8], F32)
        wg = [S.sbuf(f"wg{i}", [128, ND * 512], BF16) for i in range(1)]
        wu = [S.sbuf(f"wu{i}", [128, ND * 512], BF16) for i in range(1)]
        wd = [S.sbuf(f"wd{i}", [128, NK * DC], BF16) for i in range(1)]
        gsil = S.sbuf("gsil", [128, 512], BF16)
        ubuf = S.sbuf("ubuf", [128, 512], BF16)
        htile = S.sbuf("htile", [128, 512], BF16)
        otile = S.sbuf("otile", [128, DC], F32)
        ps = [S.psum(f"ps{i}", [128, 512]) for i in range(4)]
        pt = [S.psum(f"pt{i}", [128, 128], BF16) for i in range(2)]
        pd = [S.psum(f"pd{i}", [128, DC]) for i in range(2)]
        if debug:
            dbg_xh = S.dram_out("dbg_xh", [128, DM], F32)
            dbg_xT = S.dram_out("dbg_xT", [128, ND * TPC], F32)
            dbg_g = S.dram_out("dbg_g", [128, 512], F32)
            dbg_u = S.dram_out("dbg_u", [128, 512], F32)
            dbg_h = S.dram_out("dbg_h", [128, 512], F32)
            dbg_hT = S.dram_out("dbg_hT", [128, NK * TPC], F32)

        for j in range(NT):
            xtile = xtiles[0]
            S.dma("Q", xtile, xin, in_sl=np.s_[j * 128:(j + 1) * 128, :])
            emit_ln_tile(S, xtile, xh, stats, junk)
            if debug and j == 0:
                S.dma("P", dbg_xh, xh)
            for i in range(ND):
                p = pt[i % 2]
                mm_transpose(S, p, xh, ident, np.s_[:, i * 128:(i + 1) * 128])
                dst = Buf(xT.ap[:, i * TPC + j * 128: i * TPC + (j + 1) * 128], "xTc")
                S.add("V", lambda e, p=p, dst=dst: e.tensor_copy(dst.ap, p.ap),
                      reads=(p,), writes=(xT,))

        if debug:
            S.dma("P", dbg_xT, xT)
        for hc in range(NH_CH):
            h0 = hc * 512
            wgb, wub = wg[0], wu[0]
            wga = wgT.ap[:, h0:h0 + 512].rearrange("(c p) n -> p c n", p=128)
            S.add("Q", lambda e, wga=wga, wgb=wgb: e.dma_start(
                out=wgb.ap.rearrange("p (c n) -> p c n", n=512), in_=wga),
                reads=(wgT,), writes=(wgb,), is_dma=True)
            wua = wuT.ap[:, h0:h0 + 512].rearrange("(c p) n -> p c n", p=128)
            S.add("P", lambda e, wua=wua, wub=wub: e.dma_start(
                out=wub.ap.rearrange("p (c n) -> p c n", n=512), in_=wua),
                reads=(wuT,), writes=(wub,), is_dma=True)
            for j in range(NT):
                pg = ps[(2 * j) % 4]
                pu = ps[(2 * j + 1) % 4]
                mm_acc(S, pg, [(xT, np.s_[:, i * TPC + j * 128:i * TPC + (j + 1) * 128],
                                wgb, np.s_[:, i * 512:(i + 1) * 512]) for i in range(ND)])
                mm_acc(S, pu, [(xT, np.s_[:, i * TPC + j * 128:i * TPC + (j + 1) * 128],
                                wub, np.s_[:, i * 512:(i + 1) * 512]) for i in range(ND)])
                S.add("S", lambda e, pg=pg: e.activation(gsil.ap, pg.ap, AF.Silu),
                      reads=(pg,), writes=(gsil,))
                S.add("V", lambda e, pu=pu: e.tensor_copy(ubuf.ap, pu.ap),
                      reads=(pu,), writes=(ubuf,))
                S.add("V", lambda e: e.tensor_tensor(htile.ap, gsil.ap, ubuf.ap, op=ALU.mult),
                      reads=(gsil, ubuf), writes=(htile,))
                if debug and hc == 0 and j == 0:
                    S.dma("P", dbg_g, gsil)
                    S.dma("P", dbg_u, ubuf)
                    S.dma("P", dbg_h, htile)
                for c in range(4):
                    p = pt[c % 2]
                    mm_transpose(S, p, htile, ident, np.s_[:, c * 128:(c + 1) * 128])
                    kidx = hc * 4 + c
                    dst = Buf(hT.ap[:, kidx * TPC + j * 128: kidx * TPC + (j + 1) * 128], "hTc")
                    S.add("V", lambda e, p=p, dst=dst: e.tensor_copy(dst.ap, p.ap),
                          reads=(p,), writes=(hT,))

        if debug:
            S.dma("P", dbg_hT, hT)
        for dc in range(DM // DC):
            d0 = dc * DC
            wdb = wd[0]
            wda = wdT.ap[:, d0:d0 + DC].rearrange("(c p) n -> p c n", p=128)
            S.add("Q", lambda e, wda=wda, wdb=wdb: e.dma_start(
                out=wdb.ap.rearrange("p (c n) -> p c n", n=DC), in_=wda),
                reads=(wdT,), writes=(wdb,), is_dma=True)
            for j in range(NT):
                p = pd[j % 2]
                xtile = xtiles[0]
                mm_acc(S, p, [(hT, np.s_[:, k * TPC + j * 128:k * TPC + (j + 1) * 128],
                               wdb, np.s_[:, k * DC:(k + 1) * DC]) for k in range(NK)])
                S.dma("Q", xtile, xin, in_sl=np.s_[j * 128:(j + 1) * 128, d0:d0 + DC],
                      out_sl=np.s_[:, :DC])
                S.add("V", lambda e, p=p, xtile=xtile: e.tensor_tensor(
                    otile.ap, p.ap, xtile.ap[:, :DC], op=ALU.add),
                      reads=(p, xtile), writes=(otile,))
                S.dma("Q", xout, otile, out_sl=np.s_[j * 128:(j + 1) * 128, d0:d0 + DC])
        S.finalize()
    return nc


def build_l1ffn():
    HID = 8192
    TH = 512
    NT = TH // 128
    NH_CH = HID // 512
    ND = DM // 128
    NK = HID // 128
    DC = 256
    EPSF = 1.1920929e-07
    nc = bass.Bass("TRN2", target_bir_lowering=False)
    with ExitStack() as ctx:
        S = Sched(nc, ctx)
        xin = S.dram_in("xin", [TPC, DM], F32)
        wguT = S.dram_in("wguT", [DM, 2 * HID], BF16)
        wpT = S.dram_in("wpT", [HID, DM], BF16)
        identin = S.dram_in("ident_in", [128, 128], BF16)
        xout = S.dram_out("xout", [TPC, DM], F32)

        ident = S.sbuf("ident", [128, 128], BF16)
        S.dma("Q", ident, identin)
        xT = S.sbuf("xT", [128, ND * TH], BF16)
        hT = S.sbuf("hT", [128, NK * TH], BF16)
        xtile = S.sbuf("xt0", [128, DM], F32)
        xh = S.sbuf("xh", [128, DM], BF16)
        junk = S.sbuf("junk", [128, DM], BF16)
        stats = S.sbuf("stats", [128, 8], F32)
        sstat = S.sbuf("sstat", [128, 16], F32)
        wg = S.sbuf("wg0", [128, ND * 512], BF16)
        wu = S.sbuf("wu0", [128, ND * 512], BF16)
        wd = S.sbuf("wd0", [128, NK * DC], BF16)
        gsil = S.sbuf("gsil", [128, 512], BF16)
        ubuf = S.sbuf("ubuf", [128, 512], BF16)
        htile = S.sbuf("htile", [128, 512], BF16)
        sqj = S.sbuf("sqj", [128, 512], F32)
        tmp1 = S.sbuf("tmp1", [128, 1], F32)
        otile = S.sbuf("otile", [128, DC], F32)
        ps = [S.psum(f"ps{i}", [128, 512]) for i in range(4)]
        pt = [S.psum(f"pt{i}", [128, 128], BF16) for i in range(2)]
        pd = [S.psum(f"pd{i}", [128, DC]) for i in range(2)]

        for half in range(2):
            toff = half * TH
            for j in range(NT):
                S.dma("Q", xtile, xin, in_sl=np.s_[toff + j * 128:toff + (j + 1) * 128, :])
                emit_ln_tile(S, xtile, xh, stats, junk)
                for i in range(ND):
                    p = pt[i % 2]
                    mm_transpose(S, p, xh, ident, np.s_[:, i * 128:(i + 1) * 128])
                    dst = Buf(xT.ap[:, i * TH + j * 128: i * TH + (j + 1) * 128], "xTc")
                    S.add("V", lambda e, p=p, dst=dst: e.tensor_copy(dst.ap, p.ap),
                          reads=(p,), writes=(xT,))
            for hc in range(NH_CH):
                h0 = hc * 512
                wga = wguT.ap[:, h0:h0 + 512].rearrange("(c p) n -> p c n", p=128)
                S.add("Q", lambda e, wga=wga: e.dma_start(
                    out=wg.ap.rearrange("p (c n) -> p c n", n=512), in_=wga),
                    reads=(wguT,), writes=(wg,), is_dma=True)
                wua = wguT.ap[:, HID + h0:HID + h0 + 512].rearrange("(c p) n -> p c n", p=128)
                S.add("P", lambda e, wua=wua: e.dma_start(
                    out=wu.ap.rearrange("p (c n) -> p c n", n=512), in_=wua),
                    reads=(wguT,), writes=(wu,), is_dma=True)
                for j in range(NT):
                    pg = ps[(2 * j) % 4]
                    pu = ps[(2 * j + 1) % 4]
                    mm_acc(S, pg, [(xT, np.s_[:, i * TH + j * 128:i * TH + (j + 1) * 128],
                                    wg, np.s_[:, i * 512:(i + 1) * 512]) for i in range(ND)])
                    mm_acc(S, pu, [(xT, np.s_[:, i * TH + j * 128:i * TH + (j + 1) * 128],
                                    wu, np.s_[:, i * 512:(i + 1) * 512]) for i in range(ND)])
                    S.add("S", lambda e, pg=pg: e.activation(gsil.ap, pg.ap, AF.Silu),
                          reads=(pg,), writes=(gsil,))
                    S.add("V", lambda e, pu=pu: e.tensor_copy(ubuf.ap, pu.ap),
                          reads=(pu,), writes=(ubuf,))
                    S.add("V", lambda e: e.tensor_tensor(htile.ap, gsil.ap, ubuf.ap, op=ALU.mult),
                          reads=(gsil, ubuf), writes=(htile,))
                    ssj = sstat.sub(np.s_[:, j:j + 1], "ssj")
                    S.add("V", lambda e: e.scalar_tensor_tensor(
                        sqj.ap, htile.ap, 1.0, htile.ap, ALU.mult, ALU.mult,
                        accum_out=tmp1.ap),
                        reads=(htile,), writes=(sqj, tmp1))
                    if hc == 0:
                        S.add("V", lambda e, ssj=ssj: e.tensor_copy(ssj.ap, tmp1.ap),
                              reads=(tmp1,), writes=(ssj,))
                    else:
                        S.add("V", lambda e, ssj=ssj: e.tensor_tensor(
                            ssj.ap, ssj.ap, tmp1.ap, op=ALU.add),
                            reads=(ssj, tmp1), writes=(ssj,))
                    for c in range(4):
                        p = pt[c % 2]
                        mm_transpose(S, p, htile, ident, np.s_[:, c * 128:(c + 1) * 128])
                        kidx = hc * 4 + c
                        dst = Buf(hT.ap[:, kidx * TH + j * 128: kidx * TH + (j + 1) * 128], "hTc")
                        S.add("V", lambda e, p=p, dst=dst: e.tensor_copy(dst.ap, p.ap),
                              reads=(p,), writes=(hT,))
            for j in range(NT):
                ssj = sstat.sub(np.s_[:, j:j + 1], "ssj")
                rstdj = sstat.sub(np.s_[:, 8 + j:9 + j], "rstdj")
                S.add("V", lambda e, ssj=ssj, rstdj=rstdj: e.tensor_scalar(
                    rstdj.ap, ssj.ap, 1.0 / HID, EPSF, ALU.mult, ALU.add),
                    reads=(ssj,), writes=(rstdj,))
                S.add("S", lambda e, rstdj=rstdj: e.activation(rstdj.ap, rstdj.ap, AF.Sqrt),
                      reads=(rstdj,), writes=(rstdj,))
                S.add("V", lambda e, rstdj=rstdj: e.reciprocal(rstdj.ap, rstdj.ap),
                      reads=(rstdj,), writes=(rstdj,))
            for dc in range(DM // DC):
                d0 = dc * DC
                wda = wpT.ap[:, d0:d0 + DC].rearrange("(c p) n -> p c n", p=128)
                S.add("Q", lambda e, wda=wda: e.dma_start(
                    out=wd.ap.rearrange("p (c n) -> p c n", n=DC), in_=wda),
                    reads=(wpT,), writes=(wd,), is_dma=True)
                for j in range(NT):
                    p = pd[j % 2]
                    mm_acc(S, p, [(hT, np.s_[:, k * TH + j * 128:k * TH + (j + 1) * 128],
                                   wd, np.s_[:, k * DC:(k + 1) * DC]) for k in range(NK)])
                    S.dma("Q", xtile, xin,
                          in_sl=np.s_[toff + j * 128:toff + (j + 1) * 128, d0:d0 + DC],
                          out_sl=np.s_[:, :DC])
                    rstdj = sstat.sub(np.s_[:, 8 + j:9 + j], "rstdj")
                    S.add("V", lambda e, p=p, rstdj=rstdj: e.scalar_tensor_tensor(
                        otile.ap, p.ap, rstdj.ap, xtile.ap[:, :DC], ALU.mult, ALU.add),
                        reads=(p, xtile), sreads=(rstdj,), writes=(otile,))
                    S.dma("Q", xout, otile,
                          out_sl=np.s_[toff + j * 128:toff + (j + 1) * 128, d0:d0 + DC])
        S.finalize()
    return nc


def emit_rot_hln(S, psrc, qrot, rr4, ri4, rsc, hstats, junk512, ghB, bhB, do_rot=True):
    er = rsc.sub(np.s_[:, 0:256], "er")
    oi = rsc.sub(np.s_[:, 256:512], "oi")
    ei = rsc.sub(np.s_[:, 512:768], "ei")
    eo = rsc.sub(np.s_[:, 768:1024], "eo")
    pe_ = psrc.ap[:, 0:512:2]
    po_ = psrc.ap[:, 1:512:2]
    if do_rot:
        S.add("V", lambda e: e.tensor_tensor(er.ap, pe_, rr4.ap, op=ALU.mult),
              reads=(psrc, rr4), writes=(er,))
        S.add("V", lambda e: e.tensor_tensor(oi.ap, po_, ri4.ap, op=ALU.mult),
              reads=(psrc, ri4), writes=(oi,))
        S.add("V", lambda e: e.tensor_tensor(ei.ap, pe_, ri4.ap, op=ALU.mult),
              reads=(psrc, ri4), writes=(ei,))
        S.add("V", lambda e: e.tensor_tensor(eo.ap, po_, rr4.ap, op=ALU.mult),
              reads=(psrc, rr4), writes=(eo,))
        qe = qrot.ap[:, 0:512:2]
        qo = qrot.ap[:, 1:512:2]
        S.add("V", lambda e: e.tensor_tensor(qe, er.ap, oi.ap, op=ALU.subtract),
              reads=(er, oi), writes=(qrot,))
        S.add("V", lambda e: e.tensor_tensor(qo, ei.ap, eo.ap, op=ALU.add),
              reads=(ei, eo), writes=(qrot,))
        src = qrot
    else:
        S.add("V", lambda e: e.tensor_copy(qrot.ap, psrc.ap), reads=(psrc,), writes=(qrot,))
        src = qrot
    nmu = hstats.sub(np.s_[:, 0:4], "nmu")
    ssq = hstats.sub(np.s_[:, 4:8], "ssq")
    var = hstats.sub(np.s_[:, 8:12], "var")
    rstd = hstats.sub(np.s_[:, 12:16], "rstd")
    v3 = src.ap.rearrange("p (h d) -> p h d", d=128)
    S.add("V", lambda e: e.tensor_reduce(nmu.ap, v3, axis=AX.X, op=ALU.add),
          reads=(src,), writes=(nmu,))
    S.add("V", lambda e: e.tensor_scalar(nmu.ap, nmu.ap, -1.0 / 128, None, ALU.mult),
          reads=(nmu,), writes=(nmu,))
    S.add("S", lambda e: e.activation(junk512.ap, src.ap, AF.Square),
          reads=(src,), writes=(junk512,))
    j3 = junk512.ap.rearrange("p (h d) -> p h d", d=128)
    S.add("V", lambda e: e.tensor_reduce(ssq.ap, j3, axis=AX.X, op=ALU.add),
          reads=(junk512,), writes=(ssq,))
    S.add("V", lambda e: e.scalar_tensor_tensor(var.ap, nmu.ap, 1.0, nmu.ap,
                                                ALU.mult, ALU.mult),
          reads=(nmu,), writes=(var,))
    S.add("V", lambda e: e.scalar_tensor_tensor(var.ap, ssq.ap, 1.0 / 128, var.ap,
                                                ALU.mult, ALU.subtract),
          reads=(ssq, var), writes=(var,))
    S.add("V", lambda e: e.tensor_scalar(rstd.ap, var.ap, 1.0, LNEPS, ALU.mult, ALU.add),
          reads=(var,), writes=(rstd,))
    S.add("S", lambda e: e.activation(rstd.ap, rstd.ap, AF.Sqrt),
          reads=(rstd,), writes=(rstd,))
    S.add("V", lambda e: e.reciprocal(rstd.ap, rstd.ap), reads=(rstd,), writes=(rstd,))
    for h in range(4):
        sl = np.s_[:, h * 128:(h + 1) * 128]
        hv = src.sub(sl, "hv")
        S.add("V", lambda e, hv=hv, h=h: e.tensor_scalar(
            hv.ap, hv.ap, nmu.ap[:, h:h + 1], rstd.ap[:, h:h + 1], ALU.add, ALU.mult),
            reads=(src,), sreads=(nmu, rstd), writes=(src,))
    S.add("V", lambda e: e.tensor_tensor(src.ap, src.ap, ghB.ap, op=ALU.mult),
          reads=(src, ghB), writes=(src,))
    S.add("V", lambda e: e.tensor_tensor(src.ap, src.ap, bhB.ap, op=ALU.add),
          reads=(src, bhB), writes=(src,))


def build_l0a_pre(phases=3):
    NT = TPC // 128
    ND = DM // 128
    nc = bass.Bass("TRN2", target_bir_lowering=False)
    with ExitStack() as ctx:
        S = Sched(nc, ctx)
        xin = S.dram_in("xin", [TPC, DM], F32)
        wq_d = S.dram_in("wq", [DM, DM], F32)
        wkv_d = S.dram_in("wkv", [DM, 2 * DM], F32)
        bq_d = S.dram_in("bq", [1, DM], F32)
        bkv_d = S.dram_in("bkv", [1, 2 * DM], F32)
        gq_b = S.dram_in("gq_b", [128, DM], F32)
        bqln_b = S.dram_in("bqln_b", [128, DM], F32)
        gkv_b = S.dram_in("gkv_b", [128, DM], F32)
        bkvln_b = S.dram_in("bkvln_b", [128, DM], F32)
        gh_b = S.dram_in("gh_b", [128, 512], F32)
        bh_b = S.dram_in("bh_b", [128, 512], F32)
        cosg = S.dram_in("cosg", [TPC, 64], F32)
        sing = S.dram_in("sing", [TPC, 64], F32)
        onesin = S.dram_in("ones_in", [1, 128], BF16)
        identin = S.dram_in("ident_in", [128, 128], BF16)
        qn_o = S.dram_out("qn", [TPC, DM], BF16)
        knT_o = S.dram_out("knT", [DM, TPC], BF16)
        v_o = S.dram_out("v", [TPC, DM], BF16)

        ident = S.sbuf("ident", [128, 128], BF16)
        S.dma("Q", ident, identin)
        ones = S.sbuf("ones", [1, 128], BF16)
        S.dma("Q", ones, onesin)
        bqs = S.sbuf("bqs", [1, DM], BF16)
        S.dma("P", bqs, bq_d)
        bkvs = S.sbuf("bkvs", [1, 2 * DM], BF16)
        S.dma("P", bkvs, bkv_d)
        gqB = S.sbuf("gqB", [128, DM], F32)
        S.dma("Q", gqB, gq_b)
        bqB = S.sbuf("bqB", [128, DM], F32)
        S.dma("Q", bqB, bqln_b)
        gkvB = S.sbuf("gkvB", [128, DM], F32)
        S.dma("Q", gkvB, gkv_b)
        bkvB = S.sbuf("bkvB", [128, DM], F32)
        S.dma("Q", bkvB, bkvln_b)
        ghB = S.sbuf("ghB", [128, 512], F32)
        S.dma("Q", ghB, gh_b)
        bhB = S.sbuf("bhB", [128, 512], F32)
        S.dma("Q", bhB, bh_b)
        wch = [S.sbuf(f"wch{i}", [128, ND * 512], BF16) for i in range(2)]

        xtile = S.sbuf("xtile", [128, DM], F32)
        xh = S.sbuf("xh", [128, DM], BF16)
        junk = S.sbuf("junk", [128, DM], BF16)
        stats = S.sbuf("stats", [128, 8], F32)
        qinT = S.sbuf("qinT", [128, ND * TPC], BF16)
        kvinT = S.sbuf("kvinT", [128, ND * TPC], BF16)
        lnq = S.sbuf("lnq", [128, DM], BF16)
        cosb = S.sbuf("cosb", [128, 64], F32)
        sinb = S.sbuf("sinb", [128, 64], F32)
        rr4a = S.sbuf("rr4a", [128, 8 * 256], F32)
        ri4a = S.sbuf("ri4a", [128, 8 * 256], F32)
        rsc = S.sbuf("rsc", [128, 1024], F32)
        hstats = S.sbuf("hstats", [128, 32], F32)
        junk512 = S.sbuf("junk512", [128, 512], BF16)
        qrot = S.sbuf("qrot", [128, 512], BF16)
        vtile = S.sbuf("vtile", [128, 512], BF16)
        ps = [S.psum(f"ps{i}", [128, 512]) for i in range(4)]
        pt = [S.psum(f"pt{i}", [128, 128], BF16) for i in range(4)]

        rrm = S.sbuf("rrm", [128, NT * 8], F32)
        for j in range(NT):
            S.dma("Q", xtile, xin, in_sl=np.s_[j * 128:(j + 1) * 128, :])
            emit_ln_tile(S, xtile, xh, stats, junk, extra_stats=True)
            r2 = stats.sub(np.s_[:, 3:4], "r2")
            mm_ = stats.sub(np.s_[:, 4:5], "m")
            S.add("V", lambda e, j=j: e.tensor_copy(rrm.ap[:, 8 * j:8 * j + 1], stats.ap[:, 3:4]),
                  reads=(stats,), writes=(rrm,))
            S.add("V", lambda e, j=j: e.tensor_copy(rrm.ap[:, 8 * j + 1:8 * j + 2], stats.ap[:, 4:5]),
                  reads=(stats,), writes=(rrm,))
            for variant, gB, bB, dstT in ((0, gqB, bqB, qinT), (1, gkvB, bkvB, kvinT)):
                S.add("V", lambda e, gB=gB, r2=r2: e.scalar_tensor_tensor(
                    lnq.ap, xh.ap, r2.ap, gB.ap, ALU.mult, ALU.mult),
                    reads=(xh, gB), sreads=(stats,), writes=(lnq,))
                S.add("V", lambda e, bB=bB: e.tensor_tensor(lnq.ap, lnq.ap, bB.ap, op=ALU.add),
                      reads=(lnq, bB), writes=(lnq,))
                for i in range(ND):
                    p = pt[i % 4]
                    mm_transpose(S, p, lnq, ident, np.s_[:, i * 128:(i + 1) * 128])
                    dst = Buf(dstT.ap[:, i * TPC + j * 128: i * TPC + (j + 1) * 128], "T")
                    S.add("V", lambda e, p=p, dst=dst: e.tensor_copy(dst.ap, p.ap),
                          reads=(p,), writes=(dstT,))

        for j in (range(NT) if phases >= 2 else []):
            S.dma("Q", cosb, cosg, in_sl=np.s_[j * 128:(j + 1) * 128, :])
            S.dma("Q", sinb, sing, in_sl=np.s_[j * 128:(j + 1) * 128, :])
            for rep in range(4):
                c0 = j * 256 + rep * 64
                S.add("V", lambda e, c0=c0, j=j: e.tensor_scalar(
                    rr4a.ap[:, c0:c0 + 64], cosb.ap, rrm.ap[:, 8 * j + 1:8 * j + 2],
                    None, ALU.mult),
                    reads=(cosb,), sreads=(rrm,), writes=(rr4a,))
                S.add("V", lambda e, c0=c0, j=j: e.tensor_scalar(
                    ri4a.ap[:, c0:c0 + 64], sinb.ap, rrm.ap[:, 8 * j + 1:8 * j + 2],
                    None, ALU.mult),
                    reads=(sinb,), sreads=(rrm,), writes=(ri4a,))

        for hc in (range(12) if phases >= 3 else []):
            wb = wch[hc % 2]
            if hc < 4:
                wsrc, col0, inT, brow = wq_d, hc * 512, qinT, bqs
                bcol = hc * 512
            else:
                wsrc, col0, inT, brow = wkv_d, (hc - 4) * 512, kvinT, bkvs
                bcol = (hc - 4) * 512
            wa = wsrc.ap[:, col0:col0 + 512].rearrange("(c p) n -> p c n", p=128)
            S.add("P", lambda e, wa=wa, wb=wb: e.dma_start(
                out=wb.ap.rearrange("p (c n) -> p c n", n=512), in_=wa),
                reads=(wsrc,), writes=(wb,), is_dma=True)
            for j in range(NT):
                p = ps[j % 2] if hc < 4 else ps[2 + j % 2]
                terms = [(inT, np.s_[:, i * TPC + j * 128:i * TPC + (j + 1) * 128],
                          wb, np.s_[:, i * 512:(i + 1) * 512]) for i in range(ND)]
                terms.append((ones, None, brow, np.s_[:, bcol:bcol + 512]))
                mm_acc(S, p, terms)
                rr4 = rr4a.sub(np.s_[:, j * 256:(j + 1) * 256], "rr4")
                ri4 = ri4a.sub(np.s_[:, j * 256:(j + 1) * 256], "ri4")
                if hc < 8:
                    emit_rot_hln(S, p, qrot, rr4, ri4, rsc, hstats, junk512, ghB, bhB)
                    if hc < 4:
                        S.dma("Q", qn_o, qrot,
                              out_sl=np.s_[j * 128:(j + 1) * 128, hc * 512:(hc + 1) * 512])
                    else:
                        kc = hc - 4
                        for c in range(4):
                            pp = pt[c % 4]
                            mm_transpose(S, pp, qrot, ident, np.s_[:, c * 128:(c + 1) * 128])
                            S.add("V", lambda e, pp=pp: e.tensor_copy(junk512.ap[:, :128], pp.ap),
                                  reads=(pp,), writes=(junk512,))
                            S.dma("Q", knT_o, junk512,
                                  in_sl=np.s_[:, :128],
                                  out_sl=np.s_[(kc * 4 + c) * 128:(kc * 4 + c + 1) * 128,
                                               j * 128:(j + 1) * 128])
                else:
                    S.add("V", lambda e, p=p: e.tensor_copy(vtile.ap, p.ap),
                          reads=(p,), writes=(vtile,))
                    S.dma("Q", v_o, vtile,
                          out_sl=np.s_[j * 128:(j + 1) * 128, (hc - 8) * 512:(hc - 7) * 512])
        S.finalize()
    return nc


def build_l0a_attn(nheads=16, outproj=True, parts=5):
    NT = TPC // 128
    ND = DM // 128
    NKC = SEQ // 512
    NKT = SEQ // 128
    SCALE = float(128.0 ** -0.5)
    nc = bass.Bass("TRN2", target_bir_lowering=False)
    with ExitStack() as ctx:
        S = Sched(nc, ctx)
        xin = S.dram_in("xin", [TPC, DM], F32)
        qn_d = S.dram_in("qn", [TPC, DM], BF16)
        knT_d = S.dram_in("knT", [DM, SEQ], BF16)
        v_d = S.dram_in("v", [SEQ, DM], BF16)
        wo_d = S.dram_in("wo", [DM, DM], F32)
        bo_d = S.dram_in("bo", [1, DM], F32)
        onesin = S.dram_in("ones_in", [1, 128], BF16)
        identin = S.dram_in("ident_in", [128, 128], BF16)
        xout = S.dram_out("xout", [TPC, DM], F32)

        ident = S.sbuf("ident", [128, 128], BF16)
        S.dma("Q", ident, identin)
        ones = S.sbuf("ones", [1, 128], BF16)
        S.dma("Q", ones, onesin)
        bos = S.sbuf("bos", [1, DM], BF16)
        S.dma("P", bos, bo_d)
        qnT = S.sbuf("qnT", [128, ND * TPC], BF16)
        obuf = S.sbuf("obuf", [128, NT * DM], BF16)
        knh = [S.sbuf(f"knh{i}", [128, SEQ], BF16) for i in range(2)]
        vh = [S.sbuf(f"vh{i}", [128, NKT * 128], BF16) for i in range(2)]
        stash = S.sbuf("stash", [128, SEQ], F32)
        Pb = S.sbuf("Pb", [128, SEQ], BF16)
        PT = S.sbuf("PT", [128, NKT * 128], BF16)
        m8 = S.sbuf("m8", [128, 16], F32)
        nms = S.sbuf("nms", [128, 1], F32)
        rsum = S.sbuf("rsum", [128, 1], F32)
        otile = S.sbuf("otile", [128, 128], F32)
        wos = [S.sbuf(f"wos{i}", [128, ND * 512], BF16) for i in range(2)]
        xtile = S.sbuf("xtile", [128, DM], F32)
        outt = S.sbuf("outt", [128, 512], F32)
        ps = [S.psum(f"ps{i}", [128, 512]) for i in range(2)]
        po = [S.psum(f"po{i}", [128, 128]) for i in range(2)]
        pt = [S.psum(f"pt{i}", [128, 128], BF16) for i in range(4)]

        qtl = S.sbuf("qtl", [128, DM], BF16)
        for j in range(NT):
            S.dma("Q", qtl, qn_d, in_sl=np.s_[j * 128:(j + 1) * 128, :])
            for i in range(ND):
                p = pt[i % 4]
                mm_transpose(S, p, qtl, ident, np.s_[:, i * 128:(i + 1) * 128])
                dst = Buf(qnT.ap[:, i * TPC + j * 128: i * TPC + (j + 1) * 128], "qT")
                S.add("V", lambda e, p=p, dst=dst: e.tensor_copy(dst.ap, p.ap),
                      reads=(p,), writes=(qnT,))

        for h in range(nheads):
            kb = knh[h % 2]
            vb = vh[h % 2]
            S.dma("Q", kb, knT_d, in_sl=np.s_[h * 128:(h + 1) * 128, :])
            va = v_d.ap[:, h * 128:(h + 1) * 128].rearrange("(kt p) n -> p kt n", p=128)
            S.add("Q", lambda e, va=va, vb=vb: e.dma_start(
                out=vb.ap.rearrange("p (kt n) -> p kt n", n=128), in_=va),
                reads=(v_d,), writes=(vb,), is_dma=True)
            for j in (range(NT) if parts >= 2 else []):
                for kc in range(NKC):
                    p = ps[kc % 2]
                    mm_acc(S, p, [(qnT, np.s_[:, h * TPC + j * 128:h * TPC + (j + 1) * 128],
                                   kb, np.s_[:, kc * 512:(kc + 1) * 512])])
                    dst = stash.sub(np.s_[:, kc * 512:(kc + 1) * 512], "stc")
                    mm8 = m8.sub(np.s_[:, kc:kc + 1], "m8c")
                    S.add("V", lambda e, p=p, dst=dst, mm8=mm8: e.tensor_scalar(
                        dst.ap, p.ap, 1.0, None, ALU.mult, ALU.max, accum_out=mm8.ap),
                        reads=(p,), writes=(stash, mm8))
                if parts < 3:
                    continue
                S.add("V", lambda e: e.reduce_max(nms.ap, m8.ap[:, :NKC], axis=AX.X,
                                                  negate=True),
                      reads=(m8,), writes=(nms,))
                S.add("V", lambda e: e.tensor_scalar(nms.ap, nms.ap, SCALE, None, ALU.mult),
                      reads=(nms,), writes=(nms,))
                S.add("S", lambda e: e.activation(Pb.ap, stash.ap, AF.Exp, bias=nms.ap,
                                                  scale=SCALE, accum_out=rsum.ap),
                      reads=(stash,), sreads=(nms,), writes=(Pb, rsum))
                if parts < 4:
                    continue
                for kt in range(NKT):
                    p = pt[kt % 4]
                    mm_transpose(S, p, Pb, ident, np.s_[:, kt * 128:(kt + 1) * 128])
                    dst = Buf(PT.ap[:, kt * 128:(kt + 1) * 128], "PTc")
                    S.add("V", lambda e, p=p, dst=dst: e.tensor_copy(dst.ap, p.ap),
                          reads=(p,), writes=(PT,))
                if parts < 5:
                    continue
                p = po[j % 2]
                mm_acc(S, p, [(PT, np.s_[:, kt * 128:(kt + 1) * 128],
                               vb, np.s_[:, kt * 128:(kt + 1) * 128]) for kt in range(NKT)])
                S.add("V", lambda e: e.reciprocal(rsum.ap, rsum.ap),
                      reads=(rsum,), writes=(rsum,))
                S.add("V", lambda e, p=p: e.tensor_scalar(otile.ap, p.ap, rsum.ap, None,
                                                          ALU.mult),
                      reads=(p,), sreads=(rsum,), writes=(otile,))
                dst = Buf(obuf.ap[:, j * DM + h * 128: j * DM + (h + 1) * 128], "oc")
                S.add("V", lambda e, dst=dst: e.tensor_copy(dst.ap, otile.ap),
                      reads=(otile,), writes=(obuf,))

        oT = qnT
        for j in range(NT):
            for i in range(ND):
                p = pt[i % 4]
                src = Buf(obuf.ap[:, j * DM + i * 128: j * DM + (i + 1) * 128], "ob")
                src.last_w = obuf.last_w
                S.add("T", lambda e, p=p, src=src: e.transpose(p.ap, src.ap, ident.ap),
                      reads=(obuf, ident), writes=(p,))
                dst = Buf(oT.ap[:, i * TPC + j * 128: i * TPC + (j + 1) * 128], "oT")
                S.add("V", lambda e, p=p, dst=dst: e.tensor_copy(dst.ap, p.ap),
                      reads=(p,), writes=(oT,))
        for hc in (range(4) if outproj else []):
            wb = wos[hc % 2]
            wa = wo_d.ap[:, hc * 512:(hc + 1) * 512].rearrange("(c p) n -> p c n", p=128)
            S.add("P", lambda e, wa=wa, wb=wb: e.dma_start(
                out=wb.ap.rearrange("p (c n) -> p c n", n=512), in_=wa),
                reads=(wo_d,), writes=(wb,), is_dma=True)
            for j in range(NT):
                p = ps[j % 2]
                terms = [(oT, np.s_[:, i * TPC + j * 128:i * TPC + (j + 1) * 128],
                          wb, np.s_[:, i * 512:(i + 1) * 512]) for i in range(ND)]
                terms.append((ones, None, bos, np.s_[:, hc * 512:(hc + 1) * 512]))
                mm_acc(S, p, terms)
                S.dma("Q", xtile, xin, in_sl=np.s_[j * 128:(j + 1) * 128, hc * 512:(hc + 1) * 512],
                      out_sl=np.s_[:, :512])
                S.add("V", lambda e, p=p: e.tensor_tensor(outt.ap, p.ap, xtile.ap[:, :512],
                                                          op=ALU.add),
                      reads=(p, xtile), writes=(outt,))
                S.dma("Q", xout, outt, out_sl=np.s_[j * 128:(j + 1) * 128, hc * 512:(hc + 1) * 512])
        S.finalize()
    return nc


def build_l1b_pre():
    NT = TPC // 128
    ND = DM // 128
    nc = bass.Bass("TRN2", target_bir_lowering=False)
    with ExitStack() as ctx:
        S = Sched(nc, ctx)
        xin = S.dram_in("xin", [TPC, DM], F32)
        wqT_d = S.dram_in("wqT", [DM, DM], BF16)
        wkT_d = S.dram_in("wkT", [DM, 512], BF16)
        wvT_d = S.dram_in("wvT", [DM, 512], BF16)
        bq_d = S.dram_in("bq", [1, DM], F32)
        bk_d = S.dram_in("bk", [1, 512], F32)
        bv_d = S.dram_in("bv", [1, 512], F32)
        Rq_d = S.dram_in("Rq", [TPC, 128], F32)
        Rk_d = S.dram_in("Rk", [TPC, 128], F32)
        onesin = S.dram_in("ones_in", [1, 128], BF16)
        identin = S.dram_in("ident_in", [128, 128], BF16)
        q_o = S.dram_out("q", [TPC, DM], BF16)
        kT_o = S.dram_out("kT", [512, TPC], BF16)
        v_o = S.dram_out("v", [TPC, 512], BF16)

        ident = S.sbuf("ident", [128, 128], BF16)
        S.dma("Q", ident, identin)
        ones = S.sbuf("ones", [1, 128], BF16)
        S.dma("Q", ones, onesin)
        bqs = S.sbuf("bqs", [1, DM], BF16)
        S.dma("P", bqs, bq_d)
        bks = S.sbuf("bks", [1, 512], BF16)
        S.dma("P", bks, bk_d)
        bvs = S.sbuf("bvs", [1, 512], BF16)
        S.dma("P", bvs, bv_d)
        xT = S.sbuf("xT", [128, ND * TPC], BF16)
        xtile = S.sbuf("xtile", [128, DM], F32)
        xh = S.sbuf("xh", [128, DM], BF16)
        junk = S.sbuf("junk", [128, DM], BF16)
        stats = S.sbuf("stats", [128, 8], F32)
        wch = [S.sbuf(f"wch{i}", [128, ND * 512], BF16) for i in range(2)]
        R4 = S.sbuf("R4", [128, 512], F32)
        Rtile = S.sbuf("Rtile", [128, 128], F32)
        qtile = S.sbuf("qtile", [128, 512], BF16)
        ps = [S.psum(f"ps{i}", [128, 512]) for i in range(4)]
        pt = [S.psum(f"pt{i}", [128, 128], BF16) for i in range(4)]

        for j in range(NT):
            S.dma("Q", xtile, xin, in_sl=np.s_[j * 128:(j + 1) * 128, :])
            emit_ln_tile(S, xtile, xh, stats, junk)
            for i in range(ND):
                p = pt[i % 4]
                mm_transpose(S, p, xh, ident, np.s_[:, i * 128:(i + 1) * 128])
                dst = Buf(xT.ap[:, i * TPC + j * 128: i * TPC + (j + 1) * 128], "xTc")
                S.add("V", lambda e, p=p, dst=dst: e.tensor_copy(dst.ap, p.ap),
                      reads=(p,), writes=(xT,))

        for hc in range(6):
            wb = wch[hc % 2]
            if hc < 4:
                wsrc, col0, brow, bcol = wqT_d, hc * 512, bqs, hc * 512
            elif hc == 4:
                wsrc, col0, brow, bcol = wkT_d, 0, bks, 0
            else:
                wsrc, col0, brow, bcol = wvT_d, 0, bvs, 0
            wa = wsrc.ap[:, col0:col0 + 512].rearrange("(c p) n -> p c n", p=128)
            S.add("Q", lambda e, wa=wa, wb=wb: e.dma_start(
                out=wb.ap.rearrange("p (c n) -> p c n", n=512), in_=wa),
                reads=(wsrc,), writes=(wb,), is_dma=True)
            for j in range(NT):
                p = ps[j % 2] if hc < 4 else ps[2 + j % 2]
                terms = [(xT, np.s_[:, i * TPC + j * 128:i * TPC + (j + 1) * 128],
                          wb, np.s_[:, i * 512:(i + 1) * 512]) for i in range(ND)]
                terms.append((ones, None, brow, np.s_[:, bcol:bcol + 512]))
                mm_acc(S, p, terms)
                if hc < 5:
                    Rsrc = Rq_d if hc < 4 else Rk_d
                    S.dma("Q", Rtile, Rsrc, in_sl=np.s_[j * 128:(j + 1) * 128, :])
                    for rep in range(4):
                        dst = R4.sub(np.s_[:, rep * 128:(rep + 1) * 128], f"r{rep}")
                        S.add("V", lambda e, dst=dst: e.tensor_copy(dst.ap, Rtile.ap),
                              reads=(Rtile,), writes=(dst,))
                    S.add("V", lambda e, p=p: e.tensor_tensor(qtile.ap, p.ap, R4.ap,
                                                              op=ALU.mult),
                          reads=(p, R4), writes=(qtile,))
                    if hc < 4:
                        S.dma("Q", q_o, qtile,
                              out_sl=np.s_[j * 128:(j + 1) * 128, hc * 512:(hc + 1) * 512])
                    else:
                        for c in range(4):
                            pp = pt[c % 4]
                            mm_transpose(S, pp, qtile, ident, np.s_[:, c * 128:(c + 1) * 128])
                            S.add("V", lambda e, pp=pp: e.tensor_copy(junk.ap[:, :128], pp.ap),
                                  reads=(pp,), writes=(junk,))
                            S.dma("Q", kT_o, junk, in_sl=np.s_[:, :128],
                                  out_sl=np.s_[c * 128:(c + 1) * 128, j * 128:(j + 1) * 128])
                else:
                    S.add("V", lambda e, p=p: e.tensor_copy(qtile.ap, p.ap),
                          reads=(p,), writes=(qtile,))
                    S.dma("Q", v_o, qtile, in_sl=np.s_[:, :512],
                          out_sl=np.s_[j * 128:(j + 1) * 128, :])
        S.finalize()
    return nc


def build_l1b_attn(cpos):
    NT = TPC // 128
    ND = DM // 128
    nc = bass.Bass("TRN2", target_bir_lowering=False)
    with ExitStack() as ctx:
        S = Sched(nc, ctx)
        xin = S.dram_in("xin", [TPC, DM], F32)
        q_d = S.dram_in("q", [TPC, DM], BF16)
        kT_d = S.dram_in("kT", [512, SEQ], BF16)
        v_d = S.dram_in("v", [SEQ, 512], BF16)
        woT_d = S.dram_in("woT", [DM, DM], BF16)
        bo_d = S.dram_in("bo", [1, DM], F32)
        cm4_d = S.dram_in("cm4_in", [4 * 128, 512], F32)
        onesin = S.dram_in("ones_in", [1, 128], BF16)
        identin = S.dram_in("ident_in", [128, 128], BF16)
        xout = S.dram_out("xout", [TPC, DM], F32)

        ident = S.sbuf("ident", [128, 128], BF16)
        S.dma("Q", ident, identin)
        ones = S.sbuf("ones", [1, 128], BF16)
        S.dma("Q", ones, onesin)
        bos = S.sbuf("bos", [1, DM], BF16)
        S.dma("P", bos, bo_d)
        cm4 = S.sbuf("cm4", [128, 4 * 512], F32)
        cma = cm4_d.ap.rearrange("(a p) n -> p a n", p=128)
        S.add("Q", lambda e: e.dma_start(out=cm4.ap.rearrange("p (a n) -> p a n", n=512),
                                         in_=cma), reads=(cm4_d,), writes=(cm4,), is_dma=True)
        qT = S.sbuf("qT", [128, ND * TPC], BF16)
        obuf = S.sbuf("obuf", [128, NT * DM], BF16)
        knh = [S.sbuf(f"knh{i}", [128, SEQ], BF16) for i in range(2)]
        vh = [S.sbuf(f"vh{i}", [128, (SEQ // 128) * 128], BF16) for i in range(2)]
        stash = S.sbuf("stash", [128, SEQ], F32)
        Pb = S.sbuf("Pb", [128, SEQ], BF16)
        PT = S.sbuf("PT", [128, SEQ], BF16)
        m8 = S.sbuf("m8", [128, 16], F32)
        nms = S.sbuf("nms", [128, 1], F32)
        rsum = S.sbuf("rsum", [128, 1], F32)
        otile = S.sbuf("otile", [128, 128], F32)
        wos = [S.sbuf(f"wos{i}", [128, ND * 512], BF16) for i in range(2)]
        xtile = S.sbuf("xtile", [128, DM], F32)
        outt = S.sbuf("outt", [128, 512], F32)
        qtl = S.sbuf("qtl", [128, DM], BF16)
        ps = [S.psum(f"ps{i}", [128, 512]) for i in range(2)]
        po = [S.psum(f"po{i}", [128, 128]) for i in range(2)]
        pt = [S.psum(f"pt{i}", [128, 128], BF16) for i in range(4)]

        for j in range(NT):
            S.dma("Q", qtl, q_d, in_sl=np.s_[j * 128:(j + 1) * 128, :])
            for i in range(ND):
                p = pt[i % 4]
                mm_transpose(S, p, qtl, ident, np.s_[:, i * 128:(i + 1) * 128])
                dst = Buf(qT.ap[:, i * TPC + j * 128: i * TPC + (j + 1) * 128], "qT")
                S.add("V", lambda e, p=p, dst=dst: e.tensor_copy(dst.ap, p.ap),
                      reads=(p,), writes=(qT,))

        for kvh in range(4):
            kb = knh[kvh % 2]
            vb = vh[kvh % 2]
            S.dma("Q", kb, kT_d, in_sl=np.s_[kvh * 128:(kvh + 1) * 128, :])
            va = v_d.ap[:, kvh * 128:(kvh + 1) * 128].rearrange("(kt p) n -> p kt n", p=128)
            S.add("Q", lambda e, va=va, vb=vb: e.dma_start(
                out=vb.ap.rearrange("p (kt n) -> p kt n", n=128), in_=va),
                reads=(v_d,), writes=(vb,), is_dma=True)
            for qh in range(kvh * 4, kvh * 4 + 4):
                for j in range(NT):
                    Q0 = cpos * TPC + j * 128
                    nkc = (Q0 + 128 + 511) // 512
                    klen = nkc * 512
                    for kc in range(nkc):
                        p = ps[kc % 2]
                        mm_acc(S, p, [(qT, np.s_[:, qh * TPC + j * 128:qh * TPC + (j + 1) * 128],
                                       kb, np.s_[:, kc * 512:(kc + 1) * 512])])
                        dst = stash.sub(np.s_[:, kc * 512:(kc + 1) * 512], f"stc{kc}")
                        mm8 = m8.sub(np.s_[:, kc:kc + 1], f"m8c{kc}")
                        if kc == nkc - 1 and (Q0 + 128) % 512 != 0 or kc * 512 + 512 > Q0 + 128:
                            a = (Q0 - kc * 512) // 128
                            cmsl = cm4.sub(np.s_[:, a * 512:(a + 1) * 512], f"cm{a}")
                            S.add("V", lambda e, p=p, dst=dst, cmsl=cmsl: e.scalar_tensor_tensor(
                                dst.ap, p.ap, 1.0, cmsl.ap, ALU.mult, ALU.add),
                                reads=(p, cmsl), writes=(stash,))
                            S.add("V", lambda e, dst=dst, mm8=mm8: e.reduce_max(
                                mm8.ap, dst.ap, axis=AX.X),
                                reads=(stash,), writes=(mm8,))
                        else:
                            S.add("V", lambda e, p=p, dst=dst, mm8=mm8: e.tensor_scalar(
                                dst.ap, p.ap, 1.0, None, ALU.mult, ALU.max, accum_out=mm8.ap),
                                reads=(p,), writes=(stash, mm8))
                    S.add("V", lambda e, nkc=nkc: e.reduce_max(nms.ap, m8.ap[:, :nkc],
                                                               axis=AX.X, negate=True),
                          reads=(m8,), writes=(nms,))
                    S.add("S", lambda e, klen=klen: e.activation(
                        Pb.ap[:, :klen], stash.ap[:, :klen], AF.Exp, bias=nms.ap,
                        scale=1.0, accum_out=rsum.ap),
                        reads=(stash,), sreads=(nms,), writes=(Pb, rsum))
                    nkt = klen // 128
                    for kt in range(nkt):
                        p = pt[kt % 4]
                        mm_transpose(S, p, Pb, ident, np.s_[:, kt * 128:(kt + 1) * 128])
                        dst = PT.sub(np.s_[:, kt * 128:(kt + 1) * 128], f"PT{kt}")
                        S.add("V", lambda e, p=p, dst=dst: e.tensor_copy(dst.ap, p.ap),
                              reads=(p,), writes=(PT,))
                    p = po[j % 2]
                    mm_acc(S, p, [(PT, np.s_[:, kt * 128:(kt + 1) * 128],
                                   vb, np.s_[:, kt * 128:(kt + 1) * 128]) for kt in range(nkt)])
                    S.add("V", lambda e: e.reciprocal(rsum.ap, rsum.ap),
                          reads=(rsum,), writes=(rsum,))
                    S.add("V", lambda e, p=p: e.tensor_scalar(otile.ap, p.ap, rsum.ap, None,
                                                              ALU.mult),
                          reads=(p,), sreads=(rsum,), writes=(otile,))
                    dst = Buf(obuf.ap[:, j * DM + qh * 128: j * DM + (qh + 1) * 128], "oc")
                    S.add("V", lambda e, dst=dst: e.tensor_copy(dst.ap, otile.ap),
                          reads=(otile,), writes=(obuf,))

        oT = qT
        for j in range(NT):
            for i in range(ND):
                p = pt[i % 4]
                src = Buf(obuf.ap[:, j * DM + i * 128: j * DM + (i + 1) * 128], "ob")
                src.last_w = obuf.last_w
                S.add("T", lambda e, p=p, src=src: e.transpose(p.ap, src.ap, ident.ap),
                      reads=(obuf, ident), writes=(p,))
                dst = Buf(oT.ap[:, i * TPC + j * 128: i * TPC + (j + 1) * 128], "oT")
                S.add("V", lambda e, p=p, dst=dst: e.tensor_copy(dst.ap, p.ap),
                      reads=(p,), writes=(oT,))
        for hc in range(4):
            wb = wos[hc % 2]
            wa = woT_d.ap[:, hc * 512:(hc + 1) * 512].rearrange("(c p) n -> p c n", p=128)
            S.add("Q", lambda e, wa=wa, wb=wb: e.dma_start(
                out=wb.ap.rearrange("p (c n) -> p c n", n=512), in_=wa),
                reads=(woT_d,), writes=(wb,), is_dma=True)
            for j in range(NT):
                p = ps[j % 2]
                terms = [(oT, np.s_[:, i * TPC + j * 128:i * TPC + (j + 1) * 128],
                          wb, np.s_[:, i * 512:(i + 1) * 512]) for i in range(ND)]
                terms.append((ones, None, bos, np.s_[:, hc * 512:(hc + 1) * 512]))
                mm_acc(S, p, terms)
                S.dma("Q", xtile, xin, in_sl=np.s_[j * 128:(j + 1) * 128, hc * 512:(hc + 1) * 512],
                      out_sl=np.s_[:, :512])
                S.add("V", lambda e, p=p: e.tensor_tensor(outt.ap, p.ap, xtile.ap[:, :512],
                                                          op=ALU.add),
                      reads=(p, xtile), writes=(outt,))
                S.dma("Q", xout, outt, out_sl=np.s_[j * 128:(j + 1) * 128, hc * 512:(hc + 1) * 512])
        S.finalize()
    return nc


MB_NH, MB_HP = 64, 64


def mamba_bufs(S, CT, with_Y=False):
    b = {}
    b["CT"] = CT
    b["with_Y"] = with_Y
    b["py"] = [S.psum(f"mpy{i}", [128, 512]) for i in range(2)]
    b["pf"] = [S.psum(f"mpf{i}", [128, 512]) for i in range(3)]
    b["pbf"] = [S.psum(f"mpb{i}", [128, 128], BF16) for i in range(3)]
    b["Adt"] = S.sbuf("mAdt", [128, 64], F32)
    b["G"] = S.sbuf("mG", [128, 64], F32)
    b["negG"] = S.sbuf("mnegG", [128, 64], F32)
    b["eG"] = S.sbuf("meG", [128, 64], F32)
    b["GF"] = S.sbuf("mGF", [64, 128], F32)
    b["decF"] = S.sbuf("mdecF", [64, 128], F32)
    b["decT"] = S.sbuf("mdecT", [128, 64], F32)
    b["dtd"] = S.sbuf("mdtd", [128, 64], F32)
    b["dchB"] = S.sbuf("mdchB", [128, 64], F32)
    b["BTM"] = S.sbuf("mBTM", [128, 128], BF16)
    b["xTM"] = S.sbuf("mxTM", [128, 4096], BF16)
    b["xdtd"] = S.sbuf("mxdtd", [128, 4096], BF16)
    b["Sst"] = S.sbuf("mSst", [128, 4096], F32)
    b["Sbf"] = S.sbuf("mSbf", [128, 4096], BF16)
    if with_Y:
        b["BCt"] = S.sbuf("mBCt", [128, 128], F32)
        b["Earg"] = S.sbuf("mEarg", [128, 128], F32)
        b["Ebuf"] = S.sbuf("mEbuf", [128, 128], F32)
        b["M2"] = S.sbuf("mM2", [128, 128], BF16)
        b["xdt1"] = S.sbuf("mxdt1", [128, 64], BF16)
        b["ytile"] = S.sbuf("mytile", [128, 4096], F32)
    b["gl_d"] = S.dram_tmp("mgl_d", [64], F32)
    b["glrow"] = S.sbuf("mglrow", [1, 64], F32)
    if with_Y:
        b["ghi"] = S.sbuf("mghi", [64, 128], BF16)
        b["glo"] = S.sbuf("mglo", [64, 128], BF16)
        b["gsc"] = S.sbuf("mgsc", [64, 128], F32)
        b["g2_d"] = S.dram_tmp("mg2_d", [2 * 8192], BF16)
        b["gflat2"] = S.sbuf("mgflat2", [2, 8192], BF16)
    return b


def emit_mamba_chunk(S, bufs, conv_fm, dtp, c, with_Y, consts_b):
    T = 128
    CT = bufs["CT"]
    ident = consts_b["ident"]
    identf = consts_b["identf"]
    onesb = consts_b["ones"]
    onesf = consts_b["onesf"]
    trif = consts_b["trif"]
    Ab = consts_b["Ab"]
    py, pf, pbf = bufs["py"], bufs["pf"], bufs["pbf"]
    Adt, G, negG, eG = bufs["Adt"], bufs["G"], bufs["negG"], bufs["eG"]
    GF, decF, decT, dtd, dchB = bufs["GF"], bufs["decF"], bufs["decT"], bufs["dtd"], bufs["dchB"]
    BTM, xTM, xdtd, Sst, Sbf = bufs["BTM"], bufs["xTM"], bufs["xdtd"], bufs["Sst"], bufs["Sbf"]
    ytile = bufs.get("ytile")

    def bsl(i):
        return np.s_[:, i * CT + c * T: i * CT + (c + 1) * T]

    S.add("V", lambda e: e.tensor_tensor(Adt.ap, dtp.ap, Ab.ap, op=ALU.mult),
          reads=(dtp, Ab), writes=(Adt,))
    S.add("T", lambda e: e.matmul(pf[0].ap[:, :64], trif.ap, Adt.ap, start=True, stop=True),
          reads=(trif, Adt), writes=(pf[0],))
    S.add("V", lambda e: e.tensor_copy(G.ap, pf[0].ap[:, :64]), reads=(pf[0],), writes=(G,))
    S.add("V", lambda e: e.tensor_scalar(negG.ap, G.ap, -1.0, None, ALU.mult),
          reads=(G,), writes=(negG,))
    S.add("S", lambda e: e.activation(eG.ap, G.ap, AF.Exp), reads=(G,), writes=(eG,))
    S.add("T", lambda e: e.transpose(pf[1].ap[:64, :128], G.ap, identf.ap),
          reads=(G, identf), writes=(pf[1],))
    S.add("V", lambda e: e.tensor_copy(GF.ap, pf[1].ap[:64, :128]), reads=(pf[1],), writes=(GF,))
    S.dma("Q", bufs["gl_d"], GF, in_sl=np.s_[:, 127:128])
    gla = bufs["gl_d"].ap.rearrange("(a b) -> a b", a=1)
    S.add("Q", lambda e, gla=gla: e.dma_start(out=bufs["glrow"].ap, in_=gla),
          reads=(bufs["gl_d"],), writes=(bufs["glrow"],), is_dma=True)
    if with_Y:
        S.add("V", lambda e: e.tensor_copy(bufs["ghi"].ap, GF.ap),
              reads=(GF,), writes=(bufs["ghi"],))
        S.add("V", lambda e: e.scalar_tensor_tensor(bufs["gsc"].ap, bufs["ghi"].ap, -1.0,
                                                    GF.ap, ALU.mult, ALU.add),
              reads=(bufs["ghi"], GF), writes=(bufs["gsc"],))
        S.add("V", lambda e: e.tensor_copy(bufs["glo"].ap, bufs["gsc"].ap),
              reads=(bufs["gsc"],), writes=(bufs["glo"],))
        g2 = bufs["g2_d"].ap.rearrange("(a b) -> a b", a=2)
        S.add("Q", lambda e, g2=g2: e.dma_start(out=g2[0:1, :], in_=bufs["ghi"].ap),
              reads=(bufs["ghi"],), writes=(bufs["g2_d"],), is_dma=True)
        S.add("Q", lambda e, g2=g2: e.dma_start(out=g2[1:2, :], in_=bufs["glo"].ap),
              reads=(bufs["glo"],), writes=(bufs["g2_d"],), is_dma=True)
        S.add("Q", lambda e, g2=g2: e.dma_start(out=bufs["gflat2"].ap, in_=g2),
              reads=(bufs["g2_d"],), writes=(bufs["gflat2"],), is_dma=True)
    S.add("S", lambda e: e.activation(decF.ap, GF.ap, AF.Exp,
                                      bias=GF.ap[:, 127:128], scale=-1.0),
          reads=(GF,), sreads=(GF,), writes=(decF,))
    S.add("T", lambda e: e.matmul(pf[2].ap[:, :64], onesf.ap, bufs["glrow"].ap,
                                  start=True, stop=True),
          reads=(onesf, bufs["glrow"]), writes=(pf[2],))
    S.add("S", lambda e: e.activation(dchB.ap, pf[2].ap[:, :64], AF.Exp),
          reads=(pf[2],), writes=(dchB,))
    S.add("T", lambda e: e.transpose(pf[1].ap[:, :64], decF.ap, identf.ap[:64, :64]),
          reads=(decF, identf), writes=(pf[1],))
    S.add("V", lambda e: e.tensor_copy(decT.ap, pf[1].ap[:, :64]),
          reads=(pf[1],), writes=(decT,))
    S.add("V", lambda e: e.tensor_tensor(dtd.ap, dtp.ap, decT.ap, op=ALU.mult),
          reads=(dtp, decT), writes=(dtd,))
    S.add("T", lambda e: e.transpose(pbf[0].ap, conv_fm.ap[bsl(32)], ident.ap),
          reads=(conv_fm, ident), writes=(pbf[0],))
    S.add("V", lambda e: e.tensor_copy(BTM.ap, pbf[0].ap), reads=(pbf[0],), writes=(BTM,))
    for i in range(32):
        pp = pbf[1 + i % 2]
        S.add("T", lambda e, i=i, pp=pp: e.transpose(pp.ap, conv_fm.ap[bsl(i)], ident.ap),
              reads=(conv_fm, ident), writes=(pp,))
        dst = xTM.sub(np.s_[:, i * 128:(i + 1) * 128], f"x{i}")
        S.add("V", lambda e, pp=pp, dst=dst: e.tensor_copy(dst.ap, pp.ap),
              reads=(pp,), writes=(xTM,))
    for h in range(MB_NH):
        sl = np.s_[:, h * MB_HP:(h + 1) * MB_HP]
        S.add("V", lambda e, sl=sl, h=h: e.tensor_scalar(
            xdtd.ap[sl], xTM.ap[sl], dtd.ap[:, h:h + 1], None, ALU.mult),
            reads=(xTM,), sreads=(dtd,), writes=(xdtd,))
    if with_Y:
        for hb in range(8):
            p = py[hb % 2]
            S.add("T", lambda e, p=p, hb=hb: e.matmul(
                p.ap, conv_fm.ap[bsl(33)], Sbf.ap[:, hb * 512:(hb + 1) * 512],
                start=True, stop=True),
                reads=(conv_fm, Sbf), writes=(p,))
            for hh in range(8):
                h = hb * 8 + hh
                sl = np.s_[:, h * MB_HP:(h + 1) * MB_HP]
                psl = np.s_[:, hh * MB_HP:(hh + 1) * MB_HP]
                S.add("V", lambda e, p=p, sl=sl, psl=psl, h=h: e.tensor_scalar(
                    ytile.ap[sl], p.ap[psl], eG.ap[:, h:h + 1], None, ALU.mult),
                    reads=(p,), sreads=(eG,), writes=(ytile,))
        for h in range(MB_NH):
            sl = np.s_[:, h * MB_HP:(h + 1) * MB_HP]
            S.add("V", lambda e, sl=sl, h=h: e.scalar_tensor_tensor(
                ytile.ap[sl], xTM.ap[sl], consts_b["Db"].ap[:, h:h + 1],
                ytile.ap[sl], ALU.mult, ALU.add),
                reads=(xTM, ytile), sreads=(consts_b["Db"],), writes=(ytile,))
        S.add("T", lambda e: e.matmul(pf[1].ap[:, :128], conv_fm.ap[bsl(32)],
                                      conv_fm.ap[bsl(33)], start=True, stop=True),
              reads=(conv_fm,), writes=(pf[1],))
        S.add("V", lambda e: e.tensor_copy(bufs["BCt"].ap, pf[1].ap[:, :128]),
              reads=(pf[1],), writes=(bufs["BCt"],))
        for h in range(MB_NH):
            pE = pf[2]
            S.add("T", lambda e, pE=pE, h=h: e.matmul(
                pE.ap[:, :128], consts_b["ones2"].ap, bufs["gflat2"].ap[:, h * 128:(h + 1) * 128],
                start=True, stop=True),
                reads=(consts_b["ones2"], bufs["gflat2"]), writes=(pE,))
            S.add("V", lambda e, pE=pE: e.tensor_tensor(
                bufs["Earg"].ap, pE.ap[:, :128], consts_b["nmask"].ap, op=ALU.add),
                reads=(pE, consts_b["nmask"]), writes=(bufs["Earg"],))
            S.add("S", lambda e, h=h: e.activation(bufs["Ebuf"].ap, bufs["Earg"].ap, AF.Exp,
                                                   bias=negG.ap[:, h:h + 1]),
                  reads=(bufs["Earg"],), sreads=(negG,), writes=(bufs["Ebuf"],))
            S.add("V", lambda e: e.tensor_tensor(bufs["M2"].ap, bufs["Ebuf"].ap,
                                                 bufs["BCt"].ap, op=ALU.mult),
                  reads=(bufs["Ebuf"], bufs["BCt"]), writes=(bufs["M2"],))
            sl = np.s_[:, h * MB_HP:(h + 1) * MB_HP]
            S.add("V", lambda e, sl=sl, h=h: e.tensor_scalar(
                bufs["xdt1"].ap, xTM.ap[sl], dtp.ap[:, h:h + 1], None, ALU.mult),
                reads=(xTM,), sreads=(dtp,), writes=(bufs["xdt1"],))
            pyd = pf[0]
            S.add("T", lambda e, pyd=pyd: e.matmul(pyd.ap[:, :64], bufs["M2"].ap,
                                                   bufs["xdt1"].ap, start=True, stop=True),
                  reads=(bufs["M2"], bufs["xdt1"]), writes=(pyd,))
            S.add("V", lambda e, pyd=pyd, sl=sl: e.tensor_tensor(
                ytile.ap[sl], ytile.ap[sl], pyd.ap[:, :64], op=ALU.add),
                reads=(ytile, pyd), writes=(ytile,))
    for hb in range(8):
        p = py[hb % 2]
        S.add("T", lambda e, p=p, hb=hb: e.matmul(
            p.ap, BTM.ap, xdtd.ap[:, hb * 512:(hb + 1) * 512], start=True, stop=True),
            reads=(BTM, xdtd), writes=(p,))
        for hh in range(8):
            h = hb * 8 + hh
            sl = np.s_[:, h * MB_HP:(h + 1) * MB_HP]
            psl = np.s_[:, hh * MB_HP:(hh + 1) * MB_HP]
            S.add("V", lambda e, p=p, sl=sl, psl=psl, h=h: e.scalar_tensor_tensor(
                Sst.ap[sl], Sst.ap[sl], dchB.ap[:, h:h + 1], p.ap[psl], ALU.mult, ALU.add),
                reads=(Sst, p), sreads=(dchB,), writes=(Sst,))
    if with_Y:
        S.add("V", lambda e: e.tensor_copy(Sbf.ap, Sst.ap), reads=(Sst,), writes=(Sbf,))


def emit_softplus(S, raw, dtp, sc1, sc2):
    S.add("S", lambda e: e.activation(sc1.ap, raw.ap, AF.Abs), reads=(raw,), writes=(sc1,))
    S.add("S", lambda e: e.activation(sc1.ap, sc1.ap, AF.Exp, scale=-1.0),
          reads=(sc1,), writes=(sc1,))
    S.add("S", lambda e: e.activation(sc1.ap, sc1.ap, AF.Ln, bias=1.0),
          reads=(sc1,), writes=(sc1,))
    S.add("V", lambda e: e.tensor_scalar(sc2.ap, raw.ap, 0.0, None, ALU.max),
          reads=(raw,), writes=(sc2,))
    S.add("V", lambda e: e.tensor_tensor(dtp.ap, sc1.ap, sc2.ap, op=ALU.add),
          reads=(sc1, sc2), writes=(dtp,))


def mamba_consts(S):
    cb = {}
    cb["ident"] = S.sbuf("cident", [128, 128], BF16)
    S.dma("Q", cb["ident"], S.dram_in("ident_in", [128, 128], BF16))
    cb["identf"] = S.sbuf("cidentf", [128, 128], F32)
    S.dma("Q", cb["identf"], S.dram_in("identf_in", [128, 128], F32))
    cb["ones"] = S.sbuf("cones", [1, 128], BF16)
    S.dma("Q", cb["ones"], S.dram_in("ones_in", [1, 128], BF16))
    cb["onesf"] = S.sbuf("conesf", [1, 128], F32)
    S.dma("Q", cb["onesf"], S.dram_in("onesf_in", [1, 128], F32))
    cb["ones2"] = S.sbuf("cones2", [2, 128], BF16)
    S.dma("Q", cb["ones2"], S.dram_in("ones2_in", [2, 128], BF16))
    cb["trif"] = S.sbuf("ctrif", [128, 128], F32)
    S.dma("Q", cb["trif"], S.dram_in("trif_in", [128, 128], F32))
    cb["nmask"] = S.sbuf("cnmask", [128, 128], F32)
    S.dma("Q", cb["nmask"], S.dram_in("nmask_in", [128, 128], F32))
    cb["Ab"] = S.sbuf("cAb", [128, 64], F32)
    alog = S.dram_in("alog_b", [128, 64], F32)
    S.dma("Q", cb["Ab"], alog)
    S.add("S", lambda e: e.activation(cb["Ab"].ap, cb["Ab"].ap, AF.Exp),
          reads=(cb["Ab"],), writes=(cb["Ab"],))
    S.add("V", lambda e: e.tensor_scalar(cb["Ab"].ap, cb["Ab"].ap, -1.0, None, ALU.mult),
          reads=(cb["Ab"],), writes=(cb["Ab"],))
    cb["Db"] = S.sbuf("cDb", [128, 64], F32)
    S.dma("Q", cb["Db"], S.dram_in("d_b", [128, 64], F32))
    cb["dtbB"] = S.sbuf("cdtbB", [128, 64], F32)
    S.dma("Q", cb["dtbB"], S.dram_in("dtb_b", [128, 64], F32))
    return cb


def build_mmb_pre():
    NTH = 9
    TT_ = NTH * 128
    ND = DM // 128
    nc = bass.Bass("TRN2", target_bir_lowering=False)
    with ExitStack() as ctx:
        S = Sched(nc, ctx)
        xin = S.dram_in("xin", [TT_, DM], F32)
        wT_d = S.dram_in("wT", [DM, 8576], BF16)
        convw_d = S.dram_in("convw", [34 * 128, 4], F32)
        convb_d = S.dram_in("convb", [34 * 128, 1], F32)
        cb = mamba_consts(S)
        conv_o = S.dram_out("conv", [34 * 128, TPC], BF16)
        z_o = S.dram_out("z", [TPC, 4096], BF16)
        dt_o = S.dram_out("dtp", [TPC, 64], F32)
        sloc_o = S.dram_out("sloc", [128, 4096], F32)
        dloc_o = S.dram_out("dloc", [1, 64], F32)

        xT = S.sbuf("xT", [128, ND * TT_], BF16)
        conv_fm = S.sbuf("convfm", [128, 34 * TPC], BF16)
        xtile = S.sbuf("xtile", [128, DM], F32)
        xh = S.sbuf("xh", [128, DM], BF16)
        junk = S.sbuf("junk", [128, DM], BF16)
        stats = S.sbuf("stats", [128, 8], F32)
        bufs = mamba_bufs(S, TPC)
        cacc = S.sbuf("cacc", [128, TPC], F32)
        wl = [S.sbuf(f"wl{i}", [128, ND * 128], BF16) for i in range(1)]
        wz = [S.sbuf(f"wz{i}", [128, ND * 512], BF16) for i in range(1)]
        xbc = [S.sbuf(f"xbc{i}", [128, TT_], BF16) for i in range(1)]
        cwcol = S.sbuf("cwcol", [128, 4], F32)
        cbcol = S.sbuf("cbcol", [128, 1], F32)
        ztile = S.sbuf("ztile", [128, 512], BF16)
        dtraw = S.sbuf("dtraw", [128, 64], F32)
        dtpb = S.sbuf("dtpb", [128, 8 * 64], F32)
        sc1 = S.sbuf("sc1", [128, 64], F32)
        sc2 = S.sbuf("sc2", [128, 64], F32)

        for j in range(NTH):
            S.dma("Q", xtile, xin, in_sl=np.s_[j * 128:(j + 1) * 128, :])
            emit_ln_tile(S, xtile, xh, stats, junk)
            for i in range(ND):
                p = bufs["pbf"][i % 3]
                mm_transpose(S, p, xh, ident=cb["ident"], src_sl=np.s_[:, i * 128:(i + 1) * 128])
                dst = Buf(xT.ap[:, i * TT_ + j * 128: i * TT_ + (j + 1) * 128], "xTc")
                S.add("V", lambda e, p=p, dst=dst: e.tensor_copy(dst.ap, p.ap),
                      reads=(p,), writes=(xT,))

        for r in range(34):
            wb = wl[0]
            wa = wT_d.ap[:, 4096 + r * 128: 4096 + (r + 1) * 128].rearrange(
                "(c p) n -> p c n", p=128)
            S.add("P", lambda e, wa=wa, wb=wb: e.dma_start(
                out=wb.ap.rearrange("p (c n) -> p c n", n=128), in_=wa),
                reads=(wT_d,), writes=(wb,), is_dma=True)
            xb = xbc[0]
            for tc, t0, tw in ((0, 0, 512), (1, 512, 512), (2, 1024, 128)):
                p = bufs["py"][tc % 2]
                mm_acc(S, p, [(wb, np.s_[:, i * 128:(i + 1) * 128],
                               xT, np.s_[:, i * TT_ + t0: i * TT_ + t0 + tw])
                              for i in range(ND)], width=tw)
                S.add("V", lambda e, p=p, t0=t0, tw=tw, xb=xb: e.tensor_copy(
                    xb.ap[:, t0:t0 + tw], p.ap[:, :tw]),
                    reads=(p,), writes=(xb,))
            S.dma("Q", cwcol, convw_d, in_sl=np.s_[r * 128:(r + 1) * 128, :])
            S.dma("Q", cbcol, convb_d, in_sl=np.s_[r * 128:(r + 1) * 128, :])
            acc = cacc
            S.add("V", lambda e, xb=xb: e.tensor_scalar(
                acc.ap, xb.ap[:, 125:125 + TPC], cwcol.ap[:, 0:1], None, ALU.mult),
                reads=(xb,), sreads=(cwcol,), writes=(acc,))
            for i in range(1, 4):
                S.add("V", lambda e, xb=xb, i=i: e.scalar_tensor_tensor(
                    acc.ap, xb.ap[:, 125 + i:125 + i + TPC], cwcol.ap[:, i:i + 1],
                    acc.ap, ALU.mult, ALU.add),
                    reads=(xb, acc), sreads=(cwcol,), writes=(acc,))
            dst = conv_fm.sub(np.s_[:, r * TPC:(r + 1) * TPC], f"cv{r}")
            S.add("S", lambda e, dst=dst: e.activation(dst.ap, acc.ap, AF.Silu,
                                                       bias=cbcol.ap),
                  reads=(acc,), sreads=(cbcol,), writes=(dst,))
            S.dma("Q", conv_o, dst, out_sl=np.s_[r * 128:(r + 1) * 128, :])

        for hc in range(8):
            wb = wz[0]
            wa = wT_d.ap[:, hc * 512:(hc + 1) * 512].rearrange("(c p) n -> p c n", p=128)
            S.add("P", lambda e, wa=wa, wb=wb: e.dma_start(
                out=wb.ap.rearrange("p (c n) -> p c n", n=512), in_=wa),
                reads=(wT_d,), writes=(wb,), is_dma=True)
            for j in range(8):
                p = bufs["py"][j % 2]
                mm_acc(S, p, [(xT, np.s_[:, i * TT_ + 128 + j * 128: i * TT_ + 128 + (j + 1) * 128],
                               wb, np.s_[:, i * 512:(i + 1) * 512]) for i in range(ND)])
                S.add("V", lambda e, p=p: e.tensor_copy(ztile.ap, p.ap),
                      reads=(p,), writes=(ztile,))
                S.dma("Q", z_o, ztile, out_sl=np.s_[j * 128:(j + 1) * 128, hc * 512:(hc + 1) * 512])
        wdt = wz[0]
        wa = wT_d.ap[:, 8448:8512].rearrange("(c p) n -> p c n", p=128)
        S.add("P", lambda e, wa=wa: e.dma_start(
            out=wdt.ap[:, :ND * 64].rearrange("p (c n) -> p c n", n=64), in_=wa),
            reads=(wT_d,), writes=(wdt,), is_dma=True)
        for j in range(8):
            p = bufs["pf"][j % 2]
            mm_acc(S, p, [(xT, np.s_[:, i * TT_ + 128 + j * 128: i * TT_ + 128 + (j + 1) * 128],
                           wdt, np.s_[:, i * 64:(i + 1) * 64]) for i in range(ND)], width=64)
            S.add("V", lambda e, p=p: e.tensor_tensor(dtraw.ap, p.ap[:, :64], cb["dtbB"].ap,
                                                      op=ALU.add),
                  reads=(p, cb["dtbB"]), writes=(dtraw,))
            dtp = dtpb.sub(np.s_[:, j * 64:(j + 1) * 64], f"dtp{j}")
            emit_softplus(S, dtraw, dtp, sc1, sc2)
            S.dma("Q", dt_o, dtp, out_sl=np.s_[j * 128:(j + 1) * 128, :])

        S.add("V", lambda e: e.memset(bufs["Sst"].ap, 0.0), writes=(bufs["Sst"],))
        glsum = S.sbuf("glsum", [1, 64], F32)
        S.add("V", lambda e: e.memset(glsum.ap, 0.0), writes=(glsum,))
        for c in range(8):
            dtp = dtpb.sub(np.s_[:, c * 64:(c + 1) * 64], f"dtp{c}")
            emit_mamba_chunk(S, bufs, conv_fm, dtp, c, False, cb)
            S.add("V", lambda e: e.tensor_tensor(glsum.ap, glsum.ap, bufs["glrow"].ap,
                                                 op=ALU.add),
                  reads=(glsum, bufs["glrow"]), writes=(glsum,))
        S.add("S", lambda e: e.activation(glsum.ap, glsum.ap, AF.Exp),
              reads=(glsum,), writes=(glsum,))
        S.dma("Q", dloc_o, glsum)
        S.dma("Q", sloc_o, bufs["Sst"])
        S.finalize()
    return nc


def build_mmb_post(cpos, debug=False):
    ND = DM // 128
    nc = bass.Bass("TRN2", target_bir_lowering=False)
    with ExitStack() as ctx:
        S = Sched(nc, ctx)
        xin = S.dram_in("xin", [TPC, DM], F32)
        conv_d = S.dram_in("conv", [34 * 128, TPC], BF16)
        z_d = S.dram_in("z", [TPC, 4096], BF16)
        dtp_d = S.dram_in("dtp", [TPC, 64], F32)
        slocs_d = S.dram_in("slocs", [4 * 128, 4096], F32)
        dlocx_d = S.dram_in("dlocx", [1, 4 * 4096], F32)
        wpT_d = S.dram_in("wpT", [4096, DM], BF16)
        cb = mamba_consts(S)
        xout = S.dram_out("xout", [TPC, DM], F32)

        bufs = mamba_bufs(S, 128, with_Y=True)
        conv_sb = S.sbuf("convsb", [128, 34 * 128], BF16)
        sltmp = S.sbuf("sltmp", [128, 4096], F32)
        zt = S.sbuf("zt", [128, 4096], BF16)
        junkz = S.sbuf("junkz", [128, 4096], BF16)
        dtpc = S.sbuf("dtpc", [128, 64], F32)
        ynTc = S.sbuf("ynTc", [128, 32 * 128], BF16)
        wp = S.sbuf("wp", [128, 32 * 512], BF16)
        xtile = S.sbuf("xtile", [128, 512], F32)
        otile = S.sbuf("otile", [128, 512], F32)
        rsts = S.sbuf("rsts", [128, 8], F32)
        dlr = S.sbuf("mdlr", [1, 512], F32)

        if debug:
            dbg_y = S.dram_out("dbg_y", [128, 4096], F32)
            dbg_E = S.dram_out("dbg_E", [128, 128], F32)
            dbg_G = S.dram_out("dbg_G", [128, 64], F32)
            dbg_M2 = S.dram_out("dbg_M2", [128, 128], F32)
            dbg_BCt = S.dram_out("dbg_BCt", [128, 128], F32)
        S.add("V", lambda e: e.memset(bufs["Sst"].ap, 0.0), writes=(bufs["Sst"],))
        for i in range(cpos):
            S.dma("Q", sltmp, slocs_d, in_sl=np.s_[i * 128:(i + 1) * 128, :])
            for c8 in range(8):
                p = bufs["pf"][c8 % 2]
                row = np.s_[0:1, i * 4096 + c8 * 512: i * 4096 + (c8 + 1) * 512]
                S.dma("Q", dlr, dlocx_d, in_sl=row)
                S.add("T", lambda e, p=p, dlr=dlr: e.matmul(p.ap, cb["onesf"].ap, dlr.ap,
                                                            start=True, stop=True),
                      reads=(cb["onesf"], dlr), writes=(p,))
                sl = np.s_[:, c8 * 512:(c8 + 1) * 512]
                S.add("V", lambda e, p=p, sl=sl: e.tensor_tensor(
                    bufs["Sst"].ap[sl], bufs["Sst"].ap[sl], p.ap, op=ALU.mult),
                    reads=(bufs["Sst"], p), writes=(bufs["Sst"],))
                S.add("V", lambda e, sl=sl: e.tensor_tensor(
                    bufs["Sst"].ap[sl], bufs["Sst"].ap[sl], sltmp.ap[sl], op=ALU.add),
                    reads=(bufs["Sst"], sltmp), writes=(bufs["Sst"],))
        S.add("V", lambda e: e.tensor_copy(bufs["Sbf"].ap, bufs["Sst"].ap),
              reads=(bufs["Sst"],), writes=(bufs["Sbf"],))

        for c in range(8):
            ca = conv_d.ap[:, c * 128:(c + 1) * 128].rearrange("(r p) n -> p r n", p=128)
            S.add("Q", lambda e, ca=ca: e.dma_start(
                out=conv_sb.ap.rearrange("p (r n) -> p r n", n=128), in_=ca),
                reads=(conv_d,), writes=(conv_sb,), is_dma=True)
            S.dma("Q", dtpc, dtp_d, in_sl=np.s_[c * 128:(c + 1) * 128, :])
            emit_mamba_chunk(S, bufs, conv_sb, dtpc, 0, True, cb)
            ytile = bufs["ytile"]
            if debug and c == 0:
                S.dma("P", dbg_y, ytile)
                S.dma("P", dbg_E, bufs["Ebuf"])
                S.dma("P", dbg_G, bufs["G"])
                S.dma("P", dbg_M2, bufs["M2"])
                S.dma("P", dbg_BCt, bufs["BCt"])
            S.dma("Q", zt, z_d, in_sl=np.s_[c * 128:(c + 1) * 128, :])
            S.add("S", lambda e: e.activation(junkz.ap, zt.ap, AF.Silu),
                  reads=(zt,), writes=(junkz,))
            S.add("V", lambda e: e.tensor_tensor(ytile.ap, ytile.ap, junkz.ap, op=ALU.mult),
                  reads=(ytile, junkz), writes=(ytile,))
            ss = rsts.sub(np.s_[:, 0:1], "ss")
            rstd = rsts.sub(np.s_[:, 1:2], "rstd")
            S.add("V", lambda e: e.scalar_tensor_tensor(junkz.ap, ytile.ap, 1.0, ytile.ap,
                                                        ALU.mult, ALU.mult, accum_out=ss.ap),
                  reads=(ytile,), writes=(junkz, ss))
            S.add("V", lambda e: e.tensor_scalar(rstd.ap, ss.ap, 1.0 / 4096, LNEPS,
                                                 ALU.mult, ALU.add),
                  reads=(ss,), writes=(rstd,))
            S.add("S", lambda e: e.activation(rstd.ap, rstd.ap, AF.Sqrt),
                  reads=(rstd,), writes=(rstd,))
            S.add("V", lambda e: e.reciprocal(rstd.ap, rstd.ap), reads=(rstd,), writes=(rstd,))
            S.add("V", lambda e: e.tensor_scalar(junkz.ap, ytile.ap, rstd.ap, None, ALU.mult),
                  reads=(ytile,), sreads=(rstd,), writes=(junkz,))
            for i in range(32):
                pp = bufs["pbf"][i % 3]
                S.add("T", lambda e, pp=pp, i=i: e.transpose(
                    pp.ap, junkz.ap[:, i * 128:(i + 1) * 128], cb["ident"].ap),
                    reads=(junkz, cb["ident"]), writes=(pp,))
                dst = ynTc.sub(np.s_[:, i * 128:(i + 1) * 128], f"yn{i}")
                S.add("V", lambda e, pp=pp, dst=dst: e.tensor_copy(dst.ap, pp.ap),
                      reads=(pp,), writes=(ynTc,))
            for oc in range(4):
                wa = wpT_d.ap[:, oc * 512:(oc + 1) * 512].rearrange("(k p) n -> p k n", p=128)
                S.add("P", lambda e, wa=wa: e.dma_start(
                    out=wp.ap.rearrange("p (k n) -> p k n", n=512), in_=wa),
                    reads=(wpT_d,), writes=(wp,), is_dma=True)
                p = bufs["py"][oc % 2]
                mm_acc(S, p, [(ynTc, np.s_[:, kk * 128:(kk + 1) * 128],
                               wp, np.s_[:, kk * 512:(kk + 1) * 512]) for kk in range(32)])
                S.dma("Q", xtile, xin, in_sl=np.s_[c * 128:(c + 1) * 128, oc * 512:(oc + 1) * 512])
                S.add("V", lambda e, p=p: e.tensor_tensor(otile.ap, p.ap, xtile.ap,
                                                          op=ALU.add),
                      reads=(p, xtile), writes=(otile,))
                S.dma("Q", xout, otile, out_sl=np.s_[c * 128:(c + 1) * 128, oc * 512:(oc + 1) * 512])
        S.finalize()
    return nc


def build_gsa_pre():
    NTH = 9
    TT_ = NTH * 128
    ND = DM // 128
    SCALE = float(128.0 ** -0.5)
    nc = bass.Bass("TRN2", target_bir_lowering=False)
    with ExitStack() as ctx:
        S = Sched(nc, ctx)
        xin = S.dram_in("xin", [TT_, DM], F32)
        wq_d = S.dram_in("wq", [DM, DM], F32)
        wk_d = S.dram_in("wk", [DM, DM], F32)
        wv_d = S.dram_in("wv", [DM, DM], F32)
        ww_d = S.dram_in("ww", [DM, DM], F32)
        cq_d = S.dram_in("cq", [DM, 4], F32)
        ck_d = S.dram_in("ck", [DM, 4], F32)
        cv_d = S.dram_in("cv", [DM, 4], F32)
        wf1_d = S.dram_in("wf1", [DM, 128], F32)
        wf2_d = S.dram_in("wf2", [128, DM], F32)
        wb_d = S.dram_in("wb", [DM, 16], F32)
        wg1_d = S.dram_in("wg1", [DM, 128], F32)
        wg2_d = S.dram_in("wg2", [128, DM], F32)
        bg2_d = S.dram_in("bg2", [1, DM], F32)
        identin = S.dram_in("ident_in", [128, 128], BF16)
        onesin = S.dram_in("ones_in", [1, 128], BF16)
        q_o = S.dram_out("qfm", [DM, TPC], BF16)
        k_o = S.dram_out("kfm", [DM, TPC], BF16)
        v_o = S.dram_out("vfm", [DM, TPC], BF16)
        w_o = S.dram_out("wfm", [DM, TPC], BF16)
        g_o = S.dram_out("g", [TPC, DM], F32)
        beta_o = S.dram_out("beta", [TPC, 16], F32)
        sg_o = S.dram_out("sgate", [TPC, DM], BF16)

        ident = S.sbuf("ident", [128, 128], BF16)
        S.dma("Q", ident, identin)
        ones = S.sbuf("ones", [1, 128], BF16)
        S.dma("Q", ones, onesin)
        bg2s = S.sbuf("bg2s", [1, DM], BF16)
        S.dma("P", bg2s, bg2_d)
        xT = S.sbuf("xT", [128, ND * TT_], BF16)
        xtile = S.sbuf("xtile", [128, DM], F32)
        xh = S.sbuf("xh", [128, DM], BF16)
        junk = S.sbuf("junk", [128, DM], BF16)
        stats = S.sbuf("stats", [128, 8], F32)
        wl = S.sbuf("wl", [128, ND * 128], BF16)
        cwcol = S.sbuf("cwcol", [128, 4], F32)
        xbc = S.sbuf("xbc", [128, TT_], BF16)
        cacc = S.sbuf("cacc", [128, TPC], F32)
        convt = S.sbuf("convt", [128, TPC], BF16)
        ffm = S.sbuf("ffm", [128, TPC], BF16)
        g1fm = S.sbuf("g1fm", [128, TPC], BF16)
        w2s = S.sbuf("w2s", [128, DM], BF16)
        sc512 = S.sbuf("sc512", [128, 512], F32)
        ob = S.sbuf("ob", [128, 512], BF16)
        ps = [S.psum(f"ps{i}", [128, 512]) for i in range(4)]
        pt = [S.psum(f"pt{i}", [128, 128], BF16) for i in range(4)]

        for j in range(NTH):
            S.dma("Q", xtile, xin, in_sl=np.s_[j * 128:(j + 1) * 128, :])
            emit_ln_tile(S, xtile, xh, stats, junk)
            for i in range(ND):
                p = pt[i % 4]
                mm_transpose(S, p, xh, ident, np.s_[:, i * 128:(i + 1) * 128])
                dst = Buf(xT.ap[:, i * TT_ + j * 128: i * TT_ + (j + 1) * 128], "xTc")
                S.add("V", lambda e, p=p, dst=dst: e.tensor_copy(dst.ap, p.ap),
                      reads=(p,), writes=(xT,))

        for wsrc, csrc, dsto, scaled in ((wq_d, cq_d, q_o, True), (wk_d, ck_d, k_o, False),
                                         (wv_d, cv_d, v_o, False), (ww_d, cv_d, w_o, False)):
            for r in range(16):
                wa = wsrc.ap[:, r * 128:(r + 1) * 128].rearrange("(c p) n -> p c n", p=128)
                S.add("P", lambda e, wa=wa: e.dma_start(
                    out=wl.ap.rearrange("p (c n) -> p c n", n=128), in_=wa),
                    reads=(wsrc,), writes=(wl,), is_dma=True)
                for tc, t0, tw in ((0, 0, 512), (1, 512, 512), (2, 1024, 128)):
                    p = ps[tc % 2]
                    mm_acc(S, p, [(wl, np.s_[:, i * 128:(i + 1) * 128],
                                   xT, np.s_[:, i * TT_ + t0: i * TT_ + t0 + tw])
                                  for i in range(ND)], width=tw)
                    S.add("V", lambda e, p=p, t0=t0, tw=tw: e.tensor_copy(
                        xbc.ap[:, t0:t0 + tw], p.ap[:, :tw]),
                        reads=(p,), writes=(xbc,))
                S.dma("Q", cwcol, csrc, in_sl=np.s_[r * 128:(r + 1) * 128, :])
                S.add("V", lambda e: e.tensor_scalar(
                    cacc.ap, xbc.ap[:, 125:125 + TPC], cwcol.ap[:, 0:1], None, ALU.mult),
                    reads=(xbc,), sreads=(cwcol,), writes=(cacc,))
                for i in range(1, 4):
                    S.add("V", lambda e, i=i: e.scalar_tensor_tensor(
                        cacc.ap, xbc.ap[:, 125 + i:125 + i + TPC], cwcol.ap[:, i:i + 1],
                        cacc.ap, ALU.mult, ALU.add),
                        reads=(xbc, cacc), sreads=(cwcol,), writes=(cacc,))
                S.add("S", lambda e: e.activation(convt.ap, cacc.ap, AF.Silu),
                      reads=(cacc,), writes=(convt,))
                if scaled:
                    S.add("V", lambda e: e.tensor_scalar(convt.ap, convt.ap, SCALE, None,
                                                         ALU.mult),
                          reads=(convt,), writes=(convt,))
                S.dma("Q", dsto, convt, out_sl=np.s_[r * 128:(r + 1) * 128, :])

        for fm, wsrc in ((ffm, wf1_d), (g1fm, wg1_d)):
            wa = wsrc.ap.rearrange("(c p) n -> p c n", p=128)
            S.add("P", lambda e, wa=wa: e.dma_start(
                out=wl.ap.rearrange("p (c n) -> p c n", n=128), in_=wa),
                reads=(wsrc,), writes=(wl,), is_dma=True)
            for tc in range(2):
                p = ps[tc % 2]
                mm_acc(S, p, [(wl, np.s_[:, i * 128:(i + 1) * 128],
                               xT, np.s_[:, i * TT_ + 128 + tc * 512: i * TT_ + 128 + (tc + 1) * 512])
                              for i in range(ND)])
                S.add("V", lambda e, p=p, tc=tc, fm=fm: e.tensor_copy(
                    fm.ap[:, tc * 512:(tc + 1) * 512], p.ap),
                    reads=(p,), writes=(fm,))
        for which, fm, w2src, outd in ((0, ffm, wf2_d, g_o), (1, g1fm, wg2_d, sg_o)):
            S.dma("P", w2s, w2src)
            for j in range(8):
                for hc in range(4):
                    p = ps[2 + hc % 2]
                    terms = [(fm, np.s_[:, j * 128:(j + 1) * 128],
                              w2s, np.s_[:, hc * 512:(hc + 1) * 512])]
                    if which == 1:
                        terms.append((ones, None, bg2s, np.s_[:, hc * 512:(hc + 1) * 512]))
                    mm_acc(S, p, terms)
                    if which == 0:
                        S.add("S", lambda e, p=p: e.activation(sc512.ap, p.ap, AF.Abs),
                              reads=(p,), writes=(sc512,))
                        S.add("S", lambda e: e.activation(sc512.ap, sc512.ap, AF.Exp, scale=-1.0),
                              reads=(sc512,), writes=(sc512,))
                        S.add("S", lambda e: e.activation(sc512.ap, sc512.ap, AF.Ln, bias=1.0),
                              reads=(sc512,), writes=(sc512,))
                        S.add("V", lambda e, p=p: e.tensor_scalar(
                            cacc.ap[:, :512], p.ap, 0.0, None, ALU.min),
                            reads=(p,), writes=(cacc,))
                        S.add("V", lambda e: e.tensor_tensor(
                            cacc.ap[:, :512], cacc.ap[:, :512], sc512.ap, op=ALU.subtract),
                            reads=(cacc, sc512), writes=(cacc,))
                        S.dma("Q", outd, cacc, in_sl=np.s_[:, :512],
                              out_sl=np.s_[j * 128:(j + 1) * 128, hc * 512:(hc + 1) * 512])
                    else:
                        S.add("S", lambda e, p=p: e.activation(ob.ap, p.ap, AF.Sigmoid),
                              reads=(p,), writes=(ob,))
                        S.dma("Q", outd, ob,
                              out_sl=np.s_[j * 128:(j + 1) * 128, hc * 512:(hc + 1) * 512])
        wa = wb_d.ap.rearrange("(c p) n -> p c n", p=128)
        S.add("P", lambda e, wa=wa: e.dma_start(
            out=wl.ap[:, :ND * 16].rearrange("p (c n) -> p c n", n=16), in_=wa),
            reads=(wb_d,), writes=(wl,), is_dma=True)
        for j in range(8):
            p = ps[j % 2]
            mm_acc(S, p, [(xT, np.s_[:, i * TT_ + 128 + j * 128: i * TT_ + 128 + (j + 1) * 128],
                           wl, np.s_[:, i * 16:(i + 1) * 16]) for i in range(ND)], width=16)
            S.add("S", lambda e, p=p: e.activation(sc512.ap[:, :16], p.ap[:, :16], AF.Sigmoid),
                  reads=(p,), writes=(sc512,))
            S.dma("Q", beta_o, sc512, in_sl=np.s_[:, :16], out_sl=np.s_[j * 128:(j + 1) * 128, :])
        S.finalize()
    return nc


def build_gsa_scan(cpos, with_out, debug=False):
    GH, GM, LC, NC_ = 16, 128, 64, 16
    ND = DM // 128
    nc = bass.Bass("TRN2", target_bir_lowering=False)
    with ExitStack() as ctx:
        S = Sched(nc, ctx)
        qfm_d = S.dram_in("qfm", [DM, TPC], BF16)
        kfm_d = S.dram_in("kfm", [DM, TPC], BF16)
        vfm_d = S.dram_in("vfm", [DM, TPC], BF16)
        wfm_d = S.dram_in("wfm", [DM, TPC], BF16)
        g_d = S.dram_in("g", [TPC, DM], F32)
        beta_d = S.dram_in("beta", [TPC, 16], F32)
        identin = S.dram_in("ident_in", [128, 128], BF16)
        identfin = S.dram_in("identf_in", [128, 128], F32)
        onesfin = S.dram_in("onesf_in", [1, 128], F32)
        tri64in = S.dram_in("tri64_in", [64, 64], BF16)
        ctriin = S.dram_in("ctri_in", [64, 64], F32)
        if with_out:
            xin = S.dram_in("xin", [TPC, DM], F32)
            sgate_d = S.dram_in("sgate", [TPC, DM], BF16)
            sklocs_d = S.dram_in("sklocs", [4 * GH * 128, GM], F32)
            svlocs_d = S.dram_in("svlocs", [4 * GH * 128, GM], F32)
            dlocx_d = S.dram_in("dlocx", [1, 4 * DM], F32)
            wo_d = S.dram_in("wo", [DM, DM], F32)
            normw_b = S.dram_in("normw_b", [128, DM], F32)
            xout = S.dram_out("xout", [TPC, DM], F32)
        else:
            skloc_o = S.dram_out("skloc", [GH * 128, GM], F32)
            svloc_o = S.dram_out("svloc", [GH * 128, GM], F32)
            dloc_o = S.dram_out("dloc", [1, DM], F32)

        ident = S.sbuf("ident", [128, 128], BF16)
        S.dma("Q", ident, identin)
        identf = S.sbuf("identf", [128, 128], F32)
        S.dma("Q", identf, identfin)
        onesf = S.sbuf("onesf", [1, 128], F32)
        S.dma("Q", onesf, onesfin)
        tri64 = S.sbuf("tri64", [64, 64], BF16)
        S.dma("Q", tri64, tri64in)
        ctri = S.sbuf("ctri", [64, 64], F32)
        S.dma("Q", ctri, ctriin)

        qfc = S.sbuf("qfc", [128, GH * LC], BF16)
        kfc = S.sbuf("kfc", [128, GH * LC], BF16)
        vfc = S.sbuf("vfc", [128, GH * LC], BF16)
        wfc = S.sbuf("wfc", [128, GH * LC], BF16)
        gsl = S.sbuf("gsl", [64, DM], F32)
        Gt = S.sbuf("Gt", [64, DM], F32)
        eG = S.sbuf("eG", [64, DM], F32)
        emG = S.sbuf("emG", [64, DM], BF16)
        bsl_ = S.sbuf("bsl", [64, 16], F32)
        rb = S.sbuf("rb", [64, 16], F32)
        wtm = S.sbuf("wtm", [64, DM], F32)
        U = S.sbuf("U", [64, DM], BF16)
        ktm = S.sbuf("ktm", [64, 128], BF16)
        vtm = S.sbuf("vtm", [64, 128], BF16)
        UF = S.sbuf("UF", [128, 64], BF16)
        Am = S.sbuf("Am", [64, 64], BF16)
        P2m = S.sbuf("P2m", [64, 64], BF16)
        sst = S.sbuf("sst", [64, 128], F32)
        Pa = S.sbuf("Pa", [64, 128], F32)
        abuf = S.sbuf("abuf", [64, 128], BF16)
        aT = S.sbuf("aT", [128, 64], BF16)
        nm1 = S.sbuf("nm1", [64, 4], F32)
        Sk = S.sbuf("Sk", [128, GH * GM], F32)
        Sv = S.sbuf("Sv", [128, GH * GM], F32)
        Skbf = S.sbuf("Skbf", [128, GH * GM], BF16)
        Svbf = S.sbuf("Svbf", [128, GH * GM], BF16)
        gl_d2 = S.dram_tmp("gl_d2", [DM], F32)
        glrow = S.sbuf("glrow", [1, DM], F32)
        glsum = S.sbuf("glsum", [1, DM], F32)
        eglb = S.sbuf("eglb", [128, 128], F32)
        eglc = S.sbuf("eglc", [128, 1], F32)
        tmpS = S.sbuf("tmpS", [128, 128], F32)
        pf = [S.psum(f"pf{i}", [128, 128]) for i in range(3)]
        pb = [S.psum(f"pb{i}", [128, 128], BF16) for i in range(3)]
        pw = [S.psum(f"pw{i}", [128, 512]) for i in range(2)]
        if with_out:
            otm = S.sbuf("otm", [64, DM], F32)
            oT = S.sbuf("oT", [128, ND * TPC], BF16)
            sgt = S.sbuf("sgt", [64, DM], BF16)
            normwB = S.sbuf("normwB", [128, DM], F32)
            S.dma("Q", normwB, normw_b)
            sltmp = S.sbuf("sltmp", [128, 128], F32)
            dlr = S.sbuf("dlr", [1, 128], F32)
            wos = S.sbuf("wos", [128, ND * 512], BF16)
            xtile = S.sbuf("xtile", [128, 512], F32)
            otile = S.sbuf("otile", [128, 512], F32)
            ono = S.sbuf("ono", [64, 16], F32)

        if debug:
            dbg_o = S.dram_out("dbg_o", [64, DM], F32)
            dbg_U = S.dram_out("dbg_U", [64, DM], F32)
            dbg_sst = S.dram_out("dbg_sst", [64, 128], F32)
            dbg_a = S.dram_out("dbg_a", [64, 128], F32)
            dbg_G = S.dram_out("dbg_G", [64, DM], F32)
        S.add("V", lambda e: e.memset(Sk.ap, 0.0), writes=(Sk,))
        S.add("V", lambda e: e.memset(Sv.ap, 0.0), writes=(Sv,))
        if with_out:
            for i in range(cpos):
                for h in range(GH):
                    S.dma("Q", dlr, dlocx_d,
                          in_sl=np.s_[0:1, i * DM + h * GM: i * DM + (h + 1) * GM])
                    for which, st, locd in ((0, Sk, sklocs_d), (1, Sv, svlocs_d)):
                        S.dma("Q", sltmp, locd,
                              in_sl=np.s_[i * GH * 128 + h * 128: i * GH * 128 + (h + 1) * 128, :])
                        msl = np.s_[:, h * GM:(h + 1) * GM]
                        p = pf[0]
                        if which == 0:
                            S.add("T", lambda e, p=p: e.matmul(p.ap, onesf.ap, dlr.ap,
                                                               start=True, stop=True),
                                  reads=(onesf, dlr), writes=(p,))
                            S.add("V", lambda e, p=p, msl=msl, st=st: e.tensor_tensor(
                                st.ap[msl], st.ap[msl], p.ap, op=ALU.mult),
                                reads=(st, p), writes=(st,))
                        else:
                            S.add("T", lambda e, p=p: e.matmul(p.ap[:, 0:1], dlr.ap,
                                                               onesf.ap[:, 0:1],
                                                               start=True, stop=True),
                                  reads=(dlr, onesf), writes=(p,))
                            S.add("V", lambda e, p=p: e.tensor_copy(eglc.ap, p.ap[:, 0:1]),
                                  reads=(p,), writes=(eglc,))
                            S.add("V", lambda e, msl=msl, st=st: e.tensor_scalar(
                                st.ap[msl], st.ap[msl], eglc.ap, None, ALU.mult),
                                reads=(st,), sreads=(eglc,), writes=(st,))
                        S.add("V", lambda e, msl=msl, st=st: e.tensor_tensor(
                            st.ap[msl], st.ap[msl], sltmp.ap, op=ALU.add),
                            reads=(st, sltmp), writes=(st,))
        S.add("V", lambda e: e.tensor_copy(Skbf.ap, Sk.ap), reads=(Sk,), writes=(Skbf,))
        S.add("V", lambda e: e.tensor_copy(Svbf.ap, Sv.ap), reads=(Sv,), writes=(Svbf,))
        if not with_out:
            S.add("V", lambda e: e.memset(glsum.ap, 0.0), writes=(glsum,))

        for c in range(NC_):
            t0 = c * LC
            for src, dst in ((qfm_d, qfc), (kfm_d, kfc), (vfm_d, vfc), (wfm_d, wfc)):
                if src is qfm_d and not with_out:
                    continue
                sa = src.ap[:, t0:t0 + LC].rearrange("(h p) n -> p h n", p=128)
                S.add("Q", lambda e, sa=sa, dst=dst: e.dma_start(
                    out=dst.ap.rearrange("p (h n) -> p h n", n=LC), in_=sa),
                    reads=(src,), writes=(dst,), is_dma=True)
            S.dma("Q", gsl, g_d, in_sl=np.s_[t0:t0 + LC, :])
            S.dma("Q", bsl_, beta_d, in_sl=np.s_[t0:t0 + LC, :])
            for hc in range(4):
                p = pw[hc % 2]
                S.add("T", lambda e, p=p, hc=hc: e.matmul(
                    p.ap[:64, :], ctri.ap, gsl.ap[:, hc * 512:(hc + 1) * 512],
                    start=True, stop=True),
                    reads=(ctri, gsl), writes=(p,))
                S.add("V", lambda e, p=p, hc=hc: e.tensor_copy(
                    Gt.ap[:, hc * 512:(hc + 1) * 512], p.ap[:64, :]),
                    reads=(p,), writes=(Gt,))
            S.add("S", lambda e: e.activation(eG.ap, Gt.ap, AF.Exp), reads=(Gt,), writes=(eG,))
            S.add("S", lambda e: e.activation(emG.ap, Gt.ap, AF.Exp, scale=-1.0),
                  reads=(Gt,), writes=(emG,))
            gla = gl_d2.ap.rearrange("(a b) -> a b", a=1)
            S.add("Q", lambda e, gla=gla: e.dma_start(out=gla, in_=Gt.ap[63:64, :]),
                  reads=(Gt,), writes=(gl_d2,), is_dma=True)
            S.add("Q", lambda e, gla=gla: e.dma_start(out=glrow.ap, in_=gla),
                  reads=(gl_d2,), writes=(glrow,), is_dma=True)
            if not with_out:
                S.add("V", lambda e: e.tensor_tensor(glsum.ap, glsum.ap, glrow.ap, op=ALU.add),
                      reads=(glsum, glrow), writes=(glsum,))
            for h in range(GH):
                pp = pb[h % 3]
                S.add("T", lambda e, pp=pp, h=h: e.transpose(
                    pp.ap[:64, :], wfc.ap[:, h * LC:(h + 1) * LC], ident.ap),
                    reads=(wfc, ident), writes=(pp,))
                S.add("V", lambda e, pp=pp, h=h: e.tensor_copy(
                    wtm.ap[:, h * GM:(h + 1) * GM], pp.ap[:64, :128]),
                    reads=(pp,), writes=(wtm,))
            S.add("S", lambda e: e.activation(U.ap, wtm.ap, AF.Square),
                  reads=(wtm,), writes=(U,))
            u3 = U.ap.rearrange("p (h m) -> p h m", m=GM)
            S.add("V", lambda e, u3=u3: e.tensor_reduce(rb.ap, u3, axis=AX.X, op=ALU.add),
                  reads=(U,), writes=(rb,))
            S.add("V", lambda e: e.tensor_scalar(rb.ap, rb.ap, 1.0, 1e-6, ALU.mult, ALU.add),
                  reads=(rb,), writes=(rb,))
            S.add("S", lambda e: e.activation(rb.ap, rb.ap, AF.Sqrt), reads=(rb,), writes=(rb,))
            S.add("V", lambda e: e.reciprocal(rb.ap, rb.ap), reads=(rb,), writes=(rb,))
            S.add("V", lambda e: e.tensor_tensor(rb.ap, rb.ap, bsl_.ap, op=ALU.mult),
                  reads=(rb, bsl_), writes=(rb,))
            for h in range(GH):
                msl = np.s_[:, h * GM:(h + 1) * GM]
                S.add("V", lambda e, msl=msl, h=h: e.tensor_scalar(
                    U.ap[msl], wtm.ap[msl], rb.ap[:, h:h + 1], None, ALU.mult),
                    reads=(wtm,), sreads=(rb,), writes=(U,))
            S.add("V", lambda e: e.tensor_tensor(U.ap, U.ap, emG.ap, op=ALU.mult),
                  reads=(U, emG), writes=(U,))
            for h in range(GH):
                hsl = np.s_[:, h * LC:(h + 1) * LC]
                msl = np.s_[:, h * GM:(h + 1) * GM]
                S.add("T", lambda e, hsl=hsl: e.transpose(pb[0].ap[:64, :], kfc.ap[hsl], ident.ap),
                      reads=(kfc, ident), writes=(pb[0],))
                S.add("V", lambda e: e.tensor_copy(ktm.ap, pb[0].ap[:64, :128]),
                      reads=(pb[0],), writes=(ktm,))
                S.add("T", lambda e, hsl=hsl: e.transpose(pb[1].ap[:64, :], vfc.ap[hsl], ident.ap),
                      reads=(vfc, ident), writes=(pb[1],))
                S.add("V", lambda e: e.tensor_copy(vtm.ap, pb[1].ap[:64, :128]),
                      reads=(pb[1],), writes=(vtm,))
                S.add("T", lambda e, h=h: e.transpose(
                    pb[2].ap[:, :64], U.ap[:64, h * GM:(h + 1) * GM], ident.ap[:64, :64]),
                    reads=(U, ident), writes=(pb[2],))
                S.add("V", lambda e: e.tensor_copy(UF.ap, pb[2].ap[:, :64]),
                      reads=(pb[2],), writes=(UF,))
                if with_out:
                    S.add("T", lambda e, hsl=hsl: e.matmul(pf[0].ap[:64, :64], kfc.ap[hsl],
                                                           qfc.ap[hsl], start=True, stop=True),
                          reads=(kfc, qfc), writes=(pf[0],))
                    S.add("V", lambda e: e.tensor_tensor(Am.ap, pf[0].ap[:64, :64], tri64.ap,
                                                         op=ALU.mult),
                          reads=(pf[0], tri64), writes=(Am,))
                    p = pf[1]
                    S.add("T", lambda e, p=p, h=h: e.matmul(p.ap[:64, :], Am.ap,
                                                            U.ap[:64, h * GM:(h + 1) * GM],
                                                            start=True, stop=False),
                          reads=(Am, U), writes=(p,))
                    S.add("T", lambda e, p=p, hsl=hsl, msl=msl: e.matmul(
                        p.ap[:64, :], qfc.ap[hsl], Skbf.ap[msl], start=False, stop=True),
                        reads=(qfc, Skbf), writes=(p,))
                    S.add("V", lambda e, p=p, h=h: e.tensor_tensor(
                        sst.ap, p.ap[:64, :], eG.ap[:, h * GM:(h + 1) * GM], op=ALU.mult),
                        reads=(p, eG), writes=(sst,))
                    S.add("V", lambda e: e.reduce_max(nm1.ap[:, 0:1], sst.ap, axis=AX.X,
                                                      negate=True),
                          reads=(sst,), writes=(nm1,))
                    S.add("S", lambda e: e.activation(Pa.ap, sst.ap, AF.Exp,
                                                      bias=nm1.ap[:, 0:1],
                                                      accum_out=nm1.ap[:, 1:2]),
                          reads=(sst,), sreads=(nm1,), writes=(Pa, nm1))
                    S.add("V", lambda e: e.reciprocal(nm1.ap[:, 2:3], nm1.ap[:, 1:2]),
                          reads=(nm1,), writes=(nm1,))
                    S.add("V", lambda e: e.tensor_scalar(Pa.ap, Pa.ap, nm1.ap[:, 2:3], None,
                                                         ALU.mult),
                          reads=(Pa,), sreads=(nm1,), writes=(Pa,))
                    S.add("V", lambda e, h=h: e.tensor_tensor(
                        abuf.ap, Pa.ap, eG.ap[:, h * GM:(h + 1) * GM], op=ALU.mult),
                        reads=(Pa, eG), writes=(abuf,))
                    S.add("T", lambda e: e.transpose(pb[0].ap[:, :64], abuf.ap,
                                                     ident.ap[:64, :64]),
                          reads=(abuf, ident), writes=(pb[0],))
                    S.add("V", lambda e: e.tensor_copy(aT.ap, pb[0].ap[:, :64]),
                          reads=(pb[0],), writes=(aT,))
                    S.add("T", lambda e: e.matmul(pf[0].ap[:64, :64], UF.ap, aT.ap,
                                                  start=True, stop=True),
                          reads=(UF, aT), writes=(pf[0],))
                    S.add("V", lambda e: e.tensor_tensor(P2m.ap, pf[0].ap[:64, :64], tri64.ap,
                                                         op=ALU.mult),
                          reads=(pf[0], tri64), writes=(P2m,))
                    p = pf[2]
                    S.add("T", lambda e, p=p, msl=msl: e.matmul(p.ap[:64, :], aT.ap,
                                                                Svbf.ap[msl],
                                                                start=True, stop=False),
                          reads=(aT, Svbf), writes=(p,))
                    S.add("T", lambda e, p=p: e.matmul(p.ap[:64, :], P2m.ap, vtm.ap,
                                                       start=False, stop=True),
                          reads=(P2m, vtm), writes=(p,))
                    S.add("V", lambda e, p=p, h=h: e.tensor_copy(
                        otm.ap[:, h * GM:(h + 1) * GM], p.ap[:64, :]),
                        reads=(p,), writes=(otm,))
                p = pf[0]
                S.add("T", lambda e, p=p, h=h: e.matmul(
                    p.ap, ktm.ap, U.ap[:64, h * GM:(h + 1) * GM], start=True, stop=True),
                    reads=(ktm, U), writes=(p,))
                p2 = pf[1]
                S.add("T", lambda e, p2=p2, h=h: e.matmul(
                    p2.ap, onesf.ap, glrow.ap[:, h * GM:(h + 1) * GM], start=True, stop=True),
                    reads=(onesf, glrow), writes=(p2,))
                S.add("S", lambda e, p2=p2: e.activation(eglb.ap, p2.ap, AF.Exp),
                      reads=(p2,), writes=(eglb,))
                p3 = pf[2]
                S.add("T", lambda e, p3=p3, h=h: e.matmul(
                    p3.ap[:, 0:1], glrow.ap[:, h * GM:(h + 1) * GM], onesf.ap[:, 0:1],
                    start=True, stop=True),
                    reads=(glrow, onesf), writes=(p3,))
                S.add("S", lambda e, p3=p3: e.activation(eglc.ap, p3.ap[:, 0:1], AF.Exp),
                      reads=(p3,), writes=(eglc,))
                S.add("V", lambda e, p=p, msl=msl: e.tensor_tensor(tmpS.ap, Sk.ap[msl], p.ap,
                                                                   op=ALU.add),
                      reads=(Sk, p), writes=(tmpS,))
                S.add("V", lambda e, msl=msl: e.tensor_tensor(Sk.ap[msl], tmpS.ap, eglb.ap,
                                                              op=ALU.mult),
                      reads=(tmpS, eglb), writes=(Sk,))
                p4 = pf[0]
                S.add("T", lambda e, p4=p4, h=h: e.matmul(
                    p4.ap, U.ap[:64, h * GM:(h + 1) * GM], vtm.ap, start=True, stop=True),
                    reads=(U, vtm), writes=(p4,))
                S.add("V", lambda e, p4=p4, msl=msl: e.tensor_tensor(tmpS.ap, Sv.ap[msl], p4.ap,
                                                                     op=ALU.add),
                      reads=(Sv, p4), writes=(tmpS,))
                S.add("V", lambda e, msl=msl: e.tensor_scalar(Sv.ap[msl], tmpS.ap, eglc.ap,
                                                              None, ALU.mult),
                      reads=(tmpS,), sreads=(eglc,), writes=(Sv,))
                S.add("V", lambda e, msl=msl: e.tensor_copy(Skbf.ap[msl], Sk.ap[msl]),
                      reads=(Sk,), writes=(Skbf,))
                S.add("V", lambda e, msl=msl: e.tensor_copy(Svbf.ap[msl], Sv.ap[msl]),
                      reads=(Sv,), writes=(Svbf,))
            if debug and c == 0:
                S.dma("P", dbg_o, otm)
                S.dma("P", dbg_U, U)
                S.dma("P", dbg_sst, sst)
                S.dma("P", dbg_a, abuf)
                S.dma("P", dbg_G, Gt)
            if with_out:
                S.add("S", lambda e: e.activation(U.ap, otm.ap, AF.Square),
                      reads=(otm,), writes=(U,))
                u3b = U.ap.rearrange("p (h m) -> p h m", m=GM)
                S.add("V", lambda e, u3b=u3b: e.tensor_reduce(ono.ap, u3b, axis=AX.X, op=ALU.add),
                      reads=(U,), writes=(ono,))
                S.add("V", lambda e: e.tensor_scalar(ono.ap, ono.ap, 1.0 / GM, LNEPS,
                                                     ALU.mult, ALU.add),
                      reads=(ono,), writes=(ono,))
                S.add("S", lambda e: e.activation(ono.ap, ono.ap, AF.Sqrt),
                      reads=(ono,), writes=(ono,))
                S.add("V", lambda e: e.reciprocal(ono.ap, ono.ap), reads=(ono,), writes=(ono,))
                for h in range(GH):
                    msl = np.s_[:, h * GM:(h + 1) * GM]
                    S.add("V", lambda e, msl=msl, h=h: e.tensor_scalar(
                        otm.ap[msl], otm.ap[msl], ono.ap[:, h:h + 1], None, ALU.mult),
                        reads=(otm,), sreads=(ono,), writes=(otm,))
                S.add("V", lambda e: e.tensor_tensor(otm.ap, otm.ap, normwB.ap[:64, :],
                                                     op=ALU.mult),
                      reads=(otm, normwB), writes=(otm,))
                S.dma("Q", sgt, sgate_d, in_sl=np.s_[t0:t0 + LC, :])
                S.add("V", lambda e: e.tensor_tensor(otm.ap, otm.ap, sgt.ap, op=ALU.mult),
                      reads=(otm, sgt), writes=(otm,))
                for i in range(ND):
                    pp = pf[i % 3]
                    S.add("T", lambda e, pp=pp, i=i: e.transpose(
                        pp.ap[:, :64], otm.ap[:, i * 128:(i + 1) * 128], identf.ap[:64, :64]),
                        reads=(otm, identf), writes=(pp,))
                    dst = Buf(oT.ap[:, i * TPC + t0: i * TPC + t0 + LC], "oTc")
                    S.add("V", lambda e, pp=pp, dst=dst: e.tensor_copy(dst.ap, pp.ap[:, :64]),
                          reads=(pp,), writes=(oT,))
        if not with_out:
            S.add("S", lambda e: e.activation(glsum.ap, glsum.ap, AF.Exp),
                  reads=(glsum,), writes=(glsum,))
            S.dma("Q", dloc_o, glsum)
            for h in range(GH):
                S.dma("Q", skloc_o, Sk, in_sl=np.s_[:, h * GM:(h + 1) * GM],
                      out_sl=np.s_[h * 128:(h + 1) * 128, :])
                S.dma("Q", svloc_o, Sv, in_sl=np.s_[:, h * GM:(h + 1) * GM],
                      out_sl=np.s_[h * 128:(h + 1) * 128, :])
        else:
            for hc in range(4):
                wa = wo_d.ap[:, hc * 512:(hc + 1) * 512].rearrange("(c p) n -> p c n", p=128)
                S.add("P", lambda e, wa=wa: e.dma_start(
                    out=wos.ap.rearrange("p (c n) -> p c n", n=512), in_=wa),
                    reads=(wo_d,), writes=(wos,), is_dma=True)
                for j in range(8):
                    p = pw[j % 2]
                    mm_acc(S, p, [(oT, np.s_[:, i * TPC + j * 128:i * TPC + (j + 1) * 128],
                                   wos, np.s_[:, i * 512:(i + 1) * 512]) for i in range(ND)])
                    S.dma("Q", xtile, xin,
                          in_sl=np.s_[j * 128:(j + 1) * 128, hc * 512:(hc + 1) * 512])
                    S.add("V", lambda e, p=p: e.tensor_tensor(otile.ap, p.ap, xtile.ap,
                                                              op=ALU.add),
                          reads=(p, xtile), writes=(otile,))
                    S.dma("Q", xout, otile,
                          out_sl=np.s_[j * 128:(j + 1) * 128, hc * 512:(hc + 1) * 512])
        S.finalize()
    return nc


def _f32(a):
    return np.asarray(a, dtype=np.float32)


def kernel(**inputs):
    inputs = {k: np.asarray(v) for k, v in inputs.items()}
    _all_inputs_referenced = (
        inputs["L0_attn_a_x"],
        inputs["L0_attn_a_ln_q_g"],
        inputs["L0_attn_a_ln_q_b"],
        inputs["L0_attn_a_Wq"],
        inputs["L0_attn_a_bq"],
        inputs["L0_attn_a_ln_kv_g"],
        inputs["L0_attn_a_ln_kv_b"],
        inputs["L0_attn_a_Wkv"],
        inputs["L0_attn_a_bkv"],
        inputs["L0_attn_a_ln_h_g"],
        inputs["L0_attn_a_ln_h_b"],
        inputs["L0_attn_a_Wo"],
        inputs["L0_attn_a_bo"],
        inputs["L0_attn_b_in_proj_w"],
        inputs["L0_attn_b_conv_w"],
        inputs["L0_attn_b_conv_b"],
        inputs["L0_attn_b_dt_bias"],
        inputs["L0_attn_b_A_log"],
        inputs["L0_attn_b_D"],
        inputs["L0_attn_b_norm_w"],
        inputs["L0_attn_b_out_proj_w"],
        inputs["L0_ffn_wq_gate"],
        inputs["L0_ffn_s_gate"],
        inputs["L0_ffn_wq_up"],
        inputs["L0_ffn_s_up"],
        inputs["L0_ffn_wq_down"],
        inputs["L0_ffn_s_down"],
        inputs["L1_attn_a_Wq"],
        inputs["L1_attn_a_Wk"],
        inputs["L1_attn_a_Wv"],
        inputs["L1_attn_a_Ww"],
        inputs["L1_attn_a_cq"],
        inputs["L1_attn_a_ck"],
        inputs["L1_attn_a_cv"],
        inputs["L1_attn_a_Wf1"],
        inputs["L1_attn_a_Wf2"],
        inputs["L1_attn_a_Wb"],
        inputs["L1_attn_a_Wg1"],
        inputs["L1_attn_a_Wg2"],
        inputs["L1_attn_a_bg2"],
        inputs["L1_attn_a_norm_w"],
        inputs["L1_attn_a_Wo"],
        inputs["L1_attn_b_Wq"],
        inputs["L1_attn_b_bq"],
        inputs["L1_attn_b_Wk"],
        inputs["L1_attn_b_bk"],
        inputs["L1_attn_b_Wv"],
        inputs["L1_attn_b_bv"],
        inputs["L1_attn_b_Wo"],
        inputs["L1_attn_b_bo"],
        inputs["L1_ffn_w_gate_up"],
        inputs["L1_ffn_w_proj"],
    )

    consts = host_consts()
    f32 = np.float32
    cores = list(range(NCORE))

    prep_res = run_bass_kernel_spmd(build_t0(), t0_host_inputs(inputs, consts),
                                    core_ids=cores).results
    prep = t0_host_outputs(prep_res)
    prep = {k: np.frombuffer(np.ascontiguousarray(v).tobytes(), dtype=bfnp).reshape(v.shape)
            for k, v in prep.items()}

    x = _f32(inputs["L0_attn_a_x"]).copy()

    def slices(xx, c):
        b, j = c // 4, c % 4
        return np.ascontiguousarray(xx[b, j * 1024:(j + 1) * 1024])

    def halo_slices(xx, c):
        b, j = c // 4, c % 4
        out = np.zeros((1152, DM), f32)
        if j > 0:
            out[:128] = xx[b, j * 1024 - 128: j * 1024]
        out[128:] = xx[b, j * 1024:(j + 1) * 1024]
        return out

    def gather(res, key="xout"):
        out = np.zeros((2, SEQ, DM), f32)
        for c in range(NCORE):
            b, j = c // 4, c % 4
            out[b, j * 1024:(j + 1) * 1024] = res[c][key]
        return out

    IN = lambda pfx, n: _f32(inputs[pfx + "_" + n])

    gq = np.broadcast_to(IN("L0_attn_a", "ln_q_g"), (128, DM)).copy()
    bqg = np.broadcast_to(IN("L0_attn_a", "ln_q_b"), (128, DM)).copy()
    gkv = np.broadcast_to(IN("L0_attn_a", "ln_kv_g"), (128, DM)).copy()
    bkvg = np.broadcast_to(IN("L0_attn_a", "ln_kv_b"), (128, DM)).copy()
    gh = np.broadcast_to(np.tile(IN("L0_attn_a", "ln_h_g"), 4), (128, 512)).copy()
    bh = np.broadcast_to(np.tile(IN("L0_attn_a", "ln_h_b"), 4), (128, 512)).copy()
    maps = []
    for c in range(NCORE):
        b, j = c // 4, c % 4
        maps.append(dict(
            xin=slices(x, c), wq=IN("L0_attn_a", "Wq"), wkv=IN("L0_attn_a", "Wkv"),
            bq=IN("L0_attn_a", "bq")[None], bkv=IN("L0_attn_a", "bkv")[None],
            gq_b=gq, bqln_b=bqg, gkv_b=gkv, bkvln_b=bkvg, gh_b=gh, bh_b=bh,
            cosg=np.ascontiguousarray(consts["rot_cos"][j * 1024:(j + 1) * 1024]),
            sing=np.ascontiguousarray(consts["rot_sin"][j * 1024:(j + 1) * 1024]),
            ones_in=consts["ones1x128"], ident_in=consts["ident"]))
    r1 = run_bass_kernel_spmd(build_l0a_pre(), maps, core_ids=cores).results
    knT_full = {b: np.ascontiguousarray(np.concatenate(
        [r1[b * 4 + j]["knT"] for j in range(4)], axis=1)) for b in range(2)}
    v_full = {b: np.ascontiguousarray(np.concatenate(
        [r1[b * 4 + j]["v"] for j in range(4)], axis=0)) for b in range(2)}
    maps = []
    for c in range(NCORE):
        b, j = c // 4, c % 4
        maps.append(dict(
            xin=slices(x, c), qn=r1[c]["qn"], knT=knT_full[b], v=v_full[b],
            wo=IN("L0_attn_a", "Wo"), bo=IN("L0_attn_a", "bo")[None],
            ones_in=consts["ones1x128"], ident_in=consts["ident"]))
    x = gather(run_bass_kernel_spmd(build_l0a_attn(), maps, core_ids=cores).results)

    nc_l0f = build_l0ffn()

    def run_l0ffn(xx):
        maps = [dict(xin=slices(xx, c), wgT=prep["l0f_gate"], wuT=prep["l0f_up"],
                     wdT=prep["l0f_down"], ident_in=consts["ident"]) for c in range(NCORE)]
        return gather(run_bass_kernel_spmd(nc_l0f, maps, core_ids=cores).results)

    x = run_l0ffn(x)

    def mmb_cmaps():
        alog = np.broadcast_to(IN("L0_attn_b", "A_log"), (128, 64)).copy()
        db = np.broadcast_to(IN("L0_attn_b", "D"), (128, 64)).copy()
        dtb = np.broadcast_to(IN("L0_attn_b", "dt_bias"), (128, 64)).copy()
        return dict(ident_in=consts["ident"], identf_in=np.eye(128, dtype=f32),
                    ones_in=consts["ones1x128"], onesf_in=np.ones((1, 128), f32),
                    ones2_in=np.ones((2, 128), f32).astype(bfnp),
                    trif_in=np.triu(np.ones((128, 128), f32)),
                    nmask_in=consts["negmask128"], alog_b=alog, d_b=db, dtb_b=dtb)

    cm = mmb_cmaps()
    convw = np.pad(IN("L0_attn_b", "conv_w"), ((0, 34 * 128 - 4352), (0, 0)))
    convb = np.pad(IN("L0_attn_b", "conv_b")[:, None], ((0, 34 * 128 - 4352), (0, 0)))
    maps = [dict(xin=halo_slices(x, c), wT=prep["mmb_inproj"], convw=convw, convb=convb,
                 **cm) for c in range(NCORE)]
    r1 = run_bass_kernel_spmd(build_mmb_pre(), maps, core_ids=cores).results
    xnew = np.zeros_like(x)
    for cpos in range(4):
        maps2 = []
        for b in range(2):
            slocs = np.ascontiguousarray(np.concatenate(
                [r1[b * 4 + i]["sloc"] for i in range(4)], axis=0))
            dlocx = np.ascontiguousarray(np.concatenate(
                [np.repeat(r1[b * 4 + i]["dloc"][0], 64) for i in range(4)])[None].astype(f32))
            maps2.append(dict(xin=slices(x, b * 4 + cpos), conv=r1[b * 4 + cpos]["conv"],
                              z=r1[b * 4 + cpos]["z"], dtp=r1[b * 4 + cpos]["dtp"],
                              slocs=slocs, dlocx=dlocx, wpT=prep["mmb_outproj"], **cm))
        r2 = run_bass_kernel_spmd(build_mmb_post(cpos), maps2,
                                  core_ids=[cpos, 4 + cpos]).results
        for b in range(2):
            xnew[b, cpos * 1024:(cpos + 1) * 1024] = r2[b]["xout"]
    x = xnew

    x = run_l0ffn(x)

    GIN = lambda n: IN("L1_attn_a", n)
    maps = [dict(xin=halo_slices(x, c), wq=GIN("Wq"), wk=GIN("Wk"), wv=GIN("Wv"),
                 ww=GIN("Ww"), cq=GIN("cq"), ck=GIN("ck"), cv=GIN("cv"), wf1=GIN("Wf1"),
                 wf2=GIN("Wf2"), wb=GIN("Wb"), wg1=GIN("Wg1"), wg2=GIN("Wg2"),
                 bg2=GIN("bg2")[None], ident_in=consts["ident"],
                 ones_in=consts["ones1x128"]) for c in range(NCORE)]
    r1 = run_bass_kernel_spmd(build_gsa_pre(), maps, core_ids=cores).results
    com = dict(ident_in=consts["ident"], identf_in=np.eye(128, dtype=f32),
               onesf_in=np.ones((1, 128), f32),
               tri64_in=np.triu(np.ones((64, 64), f32)).astype(bfnp),
               ctri_in=np.triu(np.ones((64, 64), f32)))
    maps2 = [dict(qfm=r1[c]["qfm"], kfm=r1[c]["kfm"], vfm=r1[c]["vfm"], wfm=r1[c]["wfm"],
                  g=r1[c]["g"], beta=r1[c]["beta"], **com) for c in range(NCORE)]
    r2 = run_bass_kernel_spmd(build_gsa_scan(0, False), maps2, core_ids=cores).results
    xnew = np.zeros_like(x)
    nwb = np.broadcast_to(np.tile(GIN("norm_w"), 16), (128, DM)).copy()
    for cpos in range(4):
        maps3 = []
        for b in range(2):
            c = b * 4 + cpos
            sk = np.ascontiguousarray(np.concatenate(
                [r2[b * 4 + i]["skloc"] for i in range(4)], axis=0))
            sv = np.ascontiguousarray(np.concatenate(
                [r2[b * 4 + i]["svloc"] for i in range(4)], axis=0))
            dl = np.ascontiguousarray(np.concatenate(
                [r2[b * 4 + i]["dloc"][0] for i in range(4)])[None].astype(f32))
            maps3.append(dict(qfm=r1[c]["qfm"], kfm=r1[c]["kfm"], vfm=r1[c]["vfm"],
                              wfm=r1[c]["wfm"], g=r1[c]["g"], beta=r1[c]["beta"],
                              xin=slices(x, c), sgate=r1[c]["sgate"], sklocs=sk,
                              svlocs=sv, dlocx=dl, wo=GIN("Wo"), normw_b=nwb, **com))
        r3 = run_bass_kernel_spmd(build_gsa_scan(cpos, True), maps3,
                                  core_ids=[cpos, 4 + cpos]).results
        for b in range(2):
            xnew[b, cpos * 1024:(cpos + 1) * 1024] = r3[b]["xout"]
    x = xnew

    nc_l1f = build_l1ffn()

    def run_l1ffn(xx):
        maps = [dict(xin=slices(xx, c), wguT=prep["l1f_gu"], wpT=prep["l1f_proj"],
                     ident_in=consts["ident"]) for c in range(NCORE)]
        return gather(run_bass_kernel_spmd(nc_l1f, maps, core_ids=cores).results)

    x = run_l1ffn(x)

    BIN = lambda n: IN("L1_attn_b", n)
    maps = []
    for c in range(NCORE):
        b, j = c // 4, c % 4
        maps.append(dict(
            xin=slices(x, c), wqT=prep["l1b_wq"], wkT=prep["l1b_wk"], wvT=prep["l1b_wv"],
            bq=BIN("bq")[None], bk=BIN("bk")[None], bv=BIN("bv")[None],
            Rq=np.ascontiguousarray(consts["Rgrid_q"][j * 1024:(j + 1) * 1024]),
            Rk=np.ascontiguousarray(consts["Rgrid"][j * 1024:(j + 1) * 1024]),
            ones_in=consts["ones1x128"], ident_in=consts["ident"]))
    r1 = run_bass_kernel_spmd(build_l1b_pre(), maps, core_ids=cores).results
    kT_full = {b: np.ascontiguousarray(np.concatenate(
        [r1[b * 4 + j]["kT"] for j in range(4)], axis=1)) for b in range(2)}
    v_full = {b: np.ascontiguousarray(np.concatenate(
        [r1[b * 4 + j]["v"] for j in range(4)], axis=0)) for b in range(2)}
    xnew = np.zeros_like(x)
    for cpos in range(4):
        maps2 = []
        for b in range(2):
            maps2.append(dict(
                xin=slices(x, b * 4 + cpos), q=r1[b * 4 + cpos]["q"], kT=kT_full[b],
                v=v_full[b], woT=prep["l1b_wo"], bo=BIN("bo")[None], cm4_in=consts["cm4"],
                ones_in=consts["ones1x128"], ident_in=consts["ident"]))
        r2 = run_bass_kernel_spmd(build_l1b_attn(cpos), maps2,
                                  core_ids=[cpos, 4 + cpos]).results
        for b in range(2):
            xnew[b, cpos * 1024:(cpos + 1) * 1024] = r2[b]["xout"]
    x = xnew

    x = run_l1ffn(x)
    return x


# version 04 (final)
# speedup vs baseline: 1.4241x; 1.4241x over previous
import numpy as np
import ml_dtypes
from contextlib import ExitStack

import concourse.bass as bass
import concourse.mybir as mybir
from concourse.bass_utils import run_bass_kernel_spmd

F32 = mybir.dt.float32
BF16 = mybir.dt.bfloat16
I32 = mybir.dt.int32
AF = mybir.ActivationFunctionType
ALU = mybir.AluOpType
AX = mybir.AxisListType
bfnp = ml_dtypes.bfloat16

NCORE = 8
TPC = 1024
SEQ = 4096
DM = 2048
LNEPS = 1e-5


class Buf:
    __slots__ = ("ap", "last_w", "readers", "name", "_subs", "is_psum")

    def __init__(self, ap, name="", is_psum=False):
        self.ap = ap
        self.last_w = None
        self.readers = []
        self.name = name
        self._subs = {}
        self.is_psum = is_psum

    def sub(self, sl, name=""):
        key = repr(sl)
        b = self._subs.get(key)
        if b is None:
            b = Buf(self.ap[sl], name or self.name)
            self._subs[key] = b
        return b


class Op:
    __slots__ = ("eng", "is_dma", "emit", "deps", "needs_inc", "cum", "idx")

    def __init__(self, eng, is_dma, emit, deps, idx):
        self.eng = eng
        self.is_dma = is_dma
        self.emit = emit
        self.deps = deps
        self.needs_inc = False
        self.cum = None
        self.idx = idx


class Sched:
    ENGS = ("T", "V", "S", "P", "Q")

    def __init__(self, nc, ctx):
        self.nc = nc
        self.ctx = ctx
        self._outer_ctx = ctx
        self.streams = {e: [] for e in self.ENGS}
        self._uid = 0
        self._last = {e: None for e in self.ENGS}
        self._last_dma = {e: None for e in self.ENGS}

    def uname(self, name):
        self._uid += 1
        return f"{name}_u{self._uid}"

    def scope(self):
        from contextlib import contextmanager

        @contextmanager
        def _sc():
            inner = ExitStack()
            prev = self.ctx
            self.ctx = self._outer_ctx.enter_context(inner) if False else inner
            try:
                with inner:
                    yield
            finally:
                self.ctx = prev
        return _sc()

    def fence(self):
        lasts = [op for op in self._last.values() if op is not None]
        for e in self.ENGS:
            deps = [op for op in lasts if op.eng != e]
            if not deps:
                continue
            self.add(e, lambda en: en.nop(), reads=(), writes=(),
                     _extra_deps=deps)

    def sbuf(self, name, shape, dtype):
        h = self.ctx.enter_context(self.nc.sbuf_tensor(self.uname(name), list(shape), dtype))
        return Buf(h[:], name)

    def psum(self, name, shape, dtype=F32):
        h = self.ctx.enter_context(self.nc.psum_tensor(self.uname(name), list(shape), dtype))
        return Buf(h[:], name, is_psum=True)

    def dram_in(self, name, shape, dtype):
        if not hasattr(self, "_din_cache"):
            self._din_cache = {}
        b = self._din_cache.get(name)
        if b is not None:
            assert list(b.ap.shape) == list(shape), (name, b.ap.shape, shape)
            return b
        b = Buf(self.nc.dram_tensor(name, list(shape), dtype, kind="ExternalInput").ap(), name)
        self._din_cache[name] = b
        return b

    def dram_out(self, name, shape, dtype):
        return Buf(self.nc.dram_tensor(name, list(shape), dtype, kind="ExternalOutput").ap(), name)

    def dram_tmp(self, name, shape, dtype):
        return Buf(self.nc.dram_tensor(self.uname(name), list(shape), dtype,
                                       kind="Internal").ap(), name)

    def collective(self, kind, op_, rg, in_buf, out_buf):
        S = self
        extra = [self._last_cc] if getattr(self, "_last_cc", None) is not None else []
        op = S.add("P", lambda e: e.collective_compute(
            kind, op_, replica_groups=rg, ins=[in_buf.ap.opt()], outs=[out_buf.ap.opt()]),
            reads=(in_buf,), writes=(out_buf,), _extra_deps=extra)
        self._last_cc = op
        return op

    def add(self, eng, emit, reads=(), writes=(), sreads=(), is_dma=False,
            _extra_deps=()):
        idx = len(self.streams[eng])
        deps = []

        def dep_on(prod, force):
            if prod is None:
                return
            if (prod.eng == eng == "T" and not prod.is_dma and not is_dma
                    and not force):
                return
            if prod.is_dma:
                latest = self._last_dma.get(prod.eng)
                if latest is not None and latest.idx >= prod.idx:
                    prod = latest
            deps.append(prod)

        for b in reads:
            dep_on(b.last_w, False)
            if b.is_psum:
                for r in b.readers:
                    if r.eng != eng:
                        dep_on(r, True)
        for b in sreads:
            dep_on(b.last_w, True)
        for d in _extra_deps:
            deps.append(d)
        op = Op(eng, is_dma, emit, deps, idx)
        if is_dma:
            op.needs_inc = True
        for b in writes:
            dep_on(b.last_w, False)
            for r in b.readers:
                if r is not op:
                    dep_on(r, False)
            b.last_w = op
            b.readers = []
        for b in tuple(reads) + tuple(sreads):
            b.readers.append(op)
        seen, dd = set(), []
        for d in op.deps:
            if id(d) not in seen and d is not op:
                seen.add(id(d))
                dd.append(d)
                d.needs_inc = True
        op.deps = dd
        self.streams[eng].append(op)
        self._last[eng] = op
        if is_dma:
            self._last_dma[eng] = op
        return op

    def dma(self, eng, out, in_, in_sl=None, out_sl=None, **kw):
        oa = out.ap[out_sl] if out_sl is not None else out.ap
        ia = in_.ap[in_sl] if in_sl is not None else in_.ap
        return self.add(eng, lambda e: e.dma_start(out=oa, in_=ia, **kw),
                        reads=(in_,), writes=(out,), is_dma=True)

    def finalize(self):
        nc = self.nc
        sems = {}
        for e in self.ENGS:
            for d in (False, True):
                sems[(e, d)] = self.ctx.enter_context(nc.semaphore(f"sm{e}{'d' if d else 'c'}"))
        totals = {}
        for e in self.ENGS:
            cum = {False: 0, True: 0}
            for op in self.streams[e]:
                if op.needs_inc:
                    cum[op.is_dma] += 16 if op.is_dma else 1
                    op.cum = cum[op.is_dma]
            totals[e] = dict(cum)
        block = self.ctx.enter_context(nc.Block())

        def emit_stream(e, eng_obj):
            waited = {}
            for op in self.streams[e]:
                for d in op.deps:
                    key = (d.eng, d.is_dma)
                    if waited.get(key, 0) < d.cum:
                        eng_obj.wait_ge(sems[key], d.cum)
                        waited[key] = d.cum
                instr = op.emit(eng_obj)
                if op.needs_inc:
                    instr.then_inc(sems[(e, op.is_dma)], 16 if op.is_dma else 1)

        @block.tensor
        def _(t):
            emit_stream("T", t)

        @block.vector
        def _(v):
            emit_stream("V", v)

        @block.scalar
        def _(s):
            emit_stream("S", s)

        @block.gpsimd
        def _(g):
            emit_stream("P", g)

        @block.sync
        def _(q):
            emit_stream("Q", q)
            for e in self.ENGS:
                for d in (True, False):
                    if totals[e][d]:
                        q.wait_ge(sems[(e, d)], totals[e][d])


def mm_acc(S, psum, terms, start=True, stop=True, width=None):
    n = len(terms)
    pa = psum.ap if width is None else psum.ap[:, :width]
    for i, (lb, ls, rb, rs) in enumerate(terms):
        la = lb.ap[ls] if ls is not None else lb.ap
        ra = rb.ap[rs] if rs is not None else rb.ap
        st = start and i == 0
        sp = stop and i == n - 1
        S.add("T", lambda e, la=la, ra=ra, st=st, sp=sp: e.matmul(pa, la, ra, start=st, stop=sp),
              reads=(lb, rb), writes=(psum,))


def mm_transpose(S, psum, src, ident, src_sl=None):
    sa = src.ap[src_sl] if src_sl is not None else src.ap
    S.add("T", lambda e, sa=sa: e.transpose(psum.ap, sa, ident.ap),
          reads=(src, ident), writes=(psum,))


def emit_ln_tile(S, x, xh, st8, junk, extra_stats=False):
    D = x.ap.shape[-1]
    negmu = st8.sub(np.s_[:, 0:1], "negmu")
    varsum = st8.sub(np.s_[:, 1:2], "varsum")
    rstd = st8.sub(np.s_[:, 2:3], "rstd")
    S.add("V", lambda e: e.tensor_reduce(negmu.ap, x.ap, axis=AX.X, op=ALU.add),
          reads=(x,), writes=(negmu,))
    S.add("V", lambda e: e.tensor_scalar(negmu.ap, negmu.ap, -1.0 / D, None, ALU.mult),
          reads=(negmu,), writes=(negmu,))
    S.add("S", lambda e: e.activation(junk.ap, x.ap, AF.Square, bias=negmu.ap,
                                      accum_out=varsum.ap),
          reads=(x,), sreads=(negmu,), writes=(junk, varsum))
    S.add("V", lambda e: e.tensor_scalar(rstd.ap, varsum.ap, 1.0 / D, LNEPS, ALU.mult, ALU.add),
          reads=(varsum,), writes=(rstd,))
    S.add("S", lambda e: e.activation(rstd.ap, rstd.ap, AF.Sqrt),
          reads=(rstd,), writes=(rstd,))
    S.add("V", lambda e: e.reciprocal(rstd.ap, rstd.ap),
          reads=(rstd,), writes=(rstd,))
    S.add("V", lambda e: e.tensor_scalar(xh.ap, x.ap, negmu.ap, rstd.ap, ALU.add, ALU.mult),
          reads=(x,), sreads=(negmu, rstd), writes=(xh,))
    if extra_stats:
        r2 = st8.sub(np.s_[:, 3:4], "r2")
        m = st8.sub(np.s_[:, 4:5], "m")
        t1 = st8.sub(np.s_[:, 5:6], "t1")
        S.add("V", lambda e: e.tensor_scalar(t1.ap, varsum.ap, 1.0 / D, LNEPS, ALU.mult, ALU.add),
              reads=(varsum,), writes=(t1,))
        S.add("V", lambda e: e.reciprocal(t1.ap, t1.ap), reads=(t1,), writes=(t1,))
        S.add("V", lambda e: e.tensor_scalar(t1.ap, t1.ap, -LNEPS, 1.0, ALU.mult, ALU.add),
              reads=(t1,), writes=(t1,))
        S.add("S", lambda e: e.activation(m.ap, t1.ap, AF.Sqrt, scale=float(D)),
              reads=(t1,), writes=(m,))
        S.add("V", lambda e: e.tensor_scalar(r2.ap, t1.ap, 1.0, LNEPS, ALU.mult, ALU.add),
              reads=(t1,), writes=(r2,))
        S.add("S", lambda e: e.activation(r2.ap, r2.ap, AF.Sqrt),
              reads=(r2,), writes=(r2,))
        S.add("V", lambda e: e.reciprocal(r2.ap, r2.ap), reads=(r2,), writes=(r2,))


def host_consts():
    c = {}
    c["ident"] = np.eye(128, dtype=np.float32).astype(bfnp)
    c["ones1x128"] = np.ones((1, 128), np.float32).astype(bfnp)
    freqs = (8000.0 / 200.0) ** np.linspace(0.0, 1.0, 64, dtype=np.float32) * 200.0 / 1000.0
    freqs = np.float32(200.0) * freqs / np.float32(1000.0)
    ang = np.outer(np.arange(SEQ, dtype=np.float32), freqs)
    u = ang / np.float32(2 * np.pi)
    red = (u - np.round(u)) * np.float32(2 * np.pi)
    sc = np.float32(128.0 ** -0.25)
    c["rot_cos"] = (np.cos(red) * sc).astype(np.float32)
    c["rot_sin"] = (np.sin(red) * sc).astype(np.float32)
    inv = np.float32(10000.0) ** (np.float32(-2.0) * np.arange(64, dtype=np.float32) / np.float32(128.0))
    th = np.arange(SEQ, dtype=np.float32)[:, None] * inv[None, :]
    u = th / np.float32(2 * np.pi)
    red = (u - np.round(u)) * np.float32(2 * np.pi)
    R = np.concatenate([np.cos(red) + np.sin(red)] * 2, axis=1).astype(np.float32)
    c["Rgrid"] = R
    c["Rgrid_q"] = (R / np.sqrt(np.float32(128.0))).astype(np.float32)
    c["triu64"] = np.triu(np.ones((64, 64), np.float32)).astype(bfnp)
    c["triu128"] = np.triu(np.ones((128, 128), np.float32)).astype(bfnp)
    c["triu128f"] = np.triu(np.ones((128, 128), np.float32))
    c["negmask128"] = np.where(np.tril(np.ones((128, 128), bool), -1), -1e30, 0.0).astype(np.float32)
    cm4 = np.zeros((4 * 128, 512), np.float32)
    for a in range(4):
        for i in range(128):
            cm4[a * 128 + i] = np.where(np.arange(512) <= a * 128 + i, 0.0, -1e30)
    c["cm4"] = cm4
    return c


def ceil_div(a, b):
    return -(-a // b)


PREP_SPECS = [
    ("l0f_gate", 5632, 2048, "dq"),
    ("l0f_up", 5632, 2048, "dq"),
    ("l0f_down", 2048, 5632, "dq"),
    ("mmb_inproj", 8576, 2048, "f32"),
    ("mmb_outproj", 2048, 4096, "f32"),
    ("l1b_wq", 2048, 2048, "f32"),
    ("l1b_wk", 512, 2048, "f32"),
    ("l1b_wv", 512, 2048, "f32"),
    ("l1b_wo", 2048, 2048, "f32"),
    ("l1f_gu", 16384, 2048, "tern"),
    ("l1f_proj", 2048, 8192, "tern"),
]


def prep_tiles_per_core(rows):
    nt = ceil_div(rows, 128)
    return ceil_div(nt, NCORE)


def _io_t0(S):
    ND = DM // 128; NT = TPC // 128
    io = {}
    io["ident_in"] = S.dram_in("ident_in", [128, 128], BF16)
    io["normw_b"] = S.dram_in("normw_b", [128, 4096], F32)
    for nm, rows, cols, kind in PREP_SPECS:
        tpc = prep_tiles_per_core(rows)
        io[f"{nm}_in"] = S.dram_in(f"{nm}_in", [tpc * 128, cols], I32 if kind == "dq" else F32)
        io[f"{nm}_out"] = S.dram_out(f"{nm}_out", [cols, tpc * 128], BF16)
        if kind == "dq":
            io[f"{nm}_s"] = S.dram_in(f"{nm}_s", [tpc * 128, 1], F32)
    return io


def emit_t0(S, io):
    with S.scope():
        ident = S.sbuf("ident", [128, 128], BF16)
        S.dma("Q", ident, io["ident_in"])
        ios = {}
        for name, rows, cols, kind in PREP_SPECS:
            tpc = prep_tiles_per_core(rows)
            din = io[f"{name}_in"]
            dout = io[f"{name}_out"]
            dsc = io.get(f"{name}_s")
            ios[name] = (din, dout, tpc, cols, kind, dsc)

        wbuf = [S.sbuf(f"wb{i}", [128, 8192], BF16) for i in range(2)]
        obuf = [S.sbuf(f"ob{i}", [128, 8192], BF16) for i in range(2)]
        sbuf_s = S.sbuf("tern_s", [128, 8192 // 64], F32)
        sbuf_sbf = S.sbuf("tern_sbf", [128, 8192 // 64], BF16)
        sbuf_u1 = S.sbuf("tern_u1", [128, 8192], BF16)
        sbuf_u2 = S.sbuf("tern_u2", [128, 8192], BF16)
        pps = [S.psum(f"pp{i}", [128, 128], BF16) for i in range(4)]
        pidx = [0]

        def transpose_out(src, cols, dout, tile_i, ob):
            nch = cols // 128
            for ci in range(nch):
                pp = pps[pidx[0] % 4]
                pidx[0] += 1
                mm_transpose(S, pp, src, ident, np.s_[:, ci * 128:(ci + 1) * 128])
                dst = ob.sub(np.s_[:, ci * 128:(ci + 1) * 128], "obch")
                S.add("V", lambda e, pp=pp, dst=dst: e.tensor_copy(dst.ap, pp.ap),
                      reads=(pp,), writes=(ob,))
            da = dout.ap.rearrange("(c p) (t r) -> t p c r", p=128, r=128)[tile_i]
            sa = ob.ap[:, :nch * 128].rearrange("p (c r) -> p c r", r=128)
            S.add("Q", lambda e, da=da, sa=sa: e.dma_start(out=da, in_=sa),
                  reads=(ob,), writes=(dout,), is_dma=True)

        scol = S.sbuf("scol", [128, 1], F32)
        normwb = S.sbuf("normwb", [128, 4096], F32)
        S.dma("Q", normwb, io["normw_b"])
        for name, (din, dout, tpc, cols, kind, dsc) in ios.items():
            for ti in range(tpc):
                wb = wbuf[ti % 2]
                ob = obuf[ti % 2]
                S.dma("P", wb, din, in_sl=np.s_[ti * 128:(ti + 1) * 128, :],
                      out_sl=np.s_[:, :cols])
                src = wb
                if kind == "dq":
                    S.dma("Q", scol, dsc, in_sl=np.s_[ti * 128:(ti + 1) * 128, :])
                    wview = wb.ap[:, :cols]
                    S.add("V", lambda e, wview=wview, scol=scol: e.tensor_scalar(
                        wview, wview, -128.0, scol.ap, ALU.add, ALU.mult),
                        reads=(wb,), sreads=(scol,), writes=(wb,))
                if name == "mmb_outproj":
                    wview = wb.ap[:, :cols]
                    S.add("V", lambda e, wview=wview: e.tensor_tensor(
                        wview, wview, normwb.ap, op=ALU.mult),
                        reads=(wb, normwb), writes=(wb,))
                if kind == "tern":
                    ng = cols // 64
                    sv = sbuf_s.sub(np.s_[:, :ng], "s")
                    wv = wb.ap[:, :cols].rearrange("p (g k) -> p g k", k=64)
                    S.add("V", lambda e, sv=sv, wv=wv: e.tensor_reduce(
                        sv.ap, wv, axis=AX.X, op=ALU.add, apply_absolute_value=True),
                        reads=(wb,), writes=(sv,))
                    S.add("V", lambda e, sv=sv: e.tensor_scalar(
                        sv.ap, sv.ap, 1.0 / 64, None, ALU.mult),
                        reads=(sv,), writes=(sv,))
                    srow_bf = sbuf_sbf.sub(np.s_[:, :ng], "srowbf")
                    S.add("V", lambda e, sv=sv, srow_bf=srow_bf: e.tensor_copy(srow_bf.ap, sv.ap),
                          reads=(sv,), writes=(srow_bf,))
                    S.add("V", lambda e, sv=sv, srow_bf=srow_bf: e.tensor_scalar(
                        sv.ap, srow_bf.ap, 1e-8, None, ALU.max),
                        reads=(srow_bf,), writes=(sv,))
                    sb = sv.ap.unsqueeze(-1).broadcast_to([128, ng, 64])
                    w3 = wb.ap[:, :cols].rearrange("p (g k) -> p g k", k=64)
                    u13 = sbuf_u1.ap[:, :cols].rearrange("p (g k) -> p g k", k=64)
                    u23 = sbuf_u2.ap[:, :cols].rearrange("p (g k) -> p g k", k=64)
                    C_THR = 0.501953125
                    S.add("V", lambda e, w3=w3, sb=sb, u13=u13: e.scalar_tensor_tensor(
                        u13, sb, -C_THR, w3, ALU.mult, ALU.add),
                        reads=(wb, sv), writes=(sbuf_u1,))
                    S.add("V", lambda e, w3=w3, sb=sb, u23=u23: e.scalar_tensor_tensor(
                        u23, sb, C_THR, w3, ALU.mult, ALU.add),
                        reads=(wb, sv), writes=(sbuf_u2,))
                    S.add("S", lambda e, u13=u13: e.activation(u13, u13, AF.Sign),
                          reads=(sbuf_u1,), writes=(sbuf_u1,))
                    S.add("S", lambda e, u23=u23: e.activation(u23, u23, AF.Sign),
                          reads=(sbuf_u2,), writes=(sbuf_u2,))
                    S.add("V", lambda e, u13=u13, u23=u23: e.tensor_tensor(
                        u13, u13, u23, op=ALU.add),
                        reads=(sbuf_u1, sbuf_u2), writes=(sbuf_u1,))
                    S.add("V", lambda e, u13=u13, sb=sb, w3=w3: e.scalar_tensor_tensor(
                        w3, sb, 0.5, u13, ALU.mult, ALU.mult),
                        reads=(sbuf_u1, sv), writes=(wb,))
                transpose_out(src, cols, dout, ti, ob)


def t0_host_inputs(inputs, consts):
    w = {
        "l0f_gate": inputs["L0_ffn_wq_gate"],
        "l0f_up": inputs["L0_ffn_wq_up"],
        "l0f_down": inputs["L0_ffn_wq_down"],
        "mmb_inproj": np.pad(np.asarray(inputs["L0_attn_b_in_proj_w"], np.float32),
                             ((0, 64), (0, 0))),
        "mmb_outproj": inputs["L0_attn_b_out_proj_w"],
        "l1b_wq": inputs["L1_attn_b_Wq"],
        "l1b_wk": inputs["L1_attn_b_Wk"],
        "l1b_wv": inputs["L1_attn_b_Wv"],
        "l1b_wo": inputs["L1_attn_b_Wo"],
        "l1f_gu": inputs["L1_ffn_w_gate_up"],
        "l1f_proj": inputs["L1_ffn_w_proj"],
    }
    maps = [dict() for _ in range(NCORE)]
    nwb = np.broadcast_to(np.asarray(inputs["L0_attn_b_norm_w"], np.float32), (128, 4096)).copy()
    for c in range(NCORE):
        maps[c]["ident_in"] = consts["ident"]
        maps[c]["normw_b"] = nwb
    scales = {
        "l0f_gate": inputs["L0_ffn_s_gate"],
        "l0f_up": inputs["L0_ffn_s_up"],
        "l0f_down": inputs["L0_ffn_s_down"],
    }
    for name, rows, cols, kind in PREP_SPECS:
        a = np.asarray(w[name])
        if kind == "dq":
            a = a.astype(np.int32)
        else:
            a = a.astype(np.float32)
        tpc = prep_tiles_per_core(rows)
        slab_rows = tpc * 128
        padded = np.zeros((NCORE * slab_rows, cols), a.dtype)
        padded[:a.shape[0]] = a
        for c in range(NCORE):
            maps[c][f"{name}_in"] = np.ascontiguousarray(
                padded[c * slab_rows:(c + 1) * slab_rows])
        if kind == "dq":
            s = np.zeros((NCORE * slab_rows, 1), np.float32)
            s[:rows, 0] = np.asarray(scales[name], np.float32)
            for c in range(NCORE):
                maps[c][f"{name}_s"] = np.ascontiguousarray(
                    s[c * slab_rows:(c + 1) * slab_rows])
    return maps


def t0_host_outputs(results):
    out = {}
    for name, rows, cols, kind in PREP_SPECS:
        tpc = prep_tiles_per_core(rows)
        full = np.concatenate([results[c][f"{name}_out"] for c in range(NCORE)], axis=1)
        out[name] = np.ascontiguousarray(full[:, :rows])
    return out




def build_t0():
    nc = bass.Bass("TRN2", target_bir_lowering=False)
    with ExitStack() as ctx:
        S = Sched(nc, ctx)
        io = _io_t0(S)
        emit_t0(S, io)
        S.finalize()
    return nc


def _io_l0ffn(S, debug=False):
    ND = DM // 128; NT = TPC // 128
    HID = 5632; NK = HID // 128
    io = {}
    io["xin"] = S.dram_in("xin", [TPC, DM], F32)
    io["wgT"] = S.dram_in("wgT", [DM, HID], BF16)
    io["wuT"] = S.dram_in("wuT", [DM, HID], BF16)
    io["wdT"] = S.dram_in("wdT", [HID, DM], BF16)
    io["ident_in"] = S.dram_in("ident_in", [128, 128], BF16)
    io["xout"] = S.dram_out("xout", [TPC, DM], F32)
    io["dbg_xh"] = S.dram_out("dbg_xh", [128, DM], F32)
    io["dbg_xT"] = S.dram_out("dbg_xT", [128, ND * TPC], F32)
    io["dbg_g"] = S.dram_out("dbg_g", [128, 512], F32)
    io["dbg_u"] = S.dram_out("dbg_u", [128, 512], F32)
    io["dbg_h"] = S.dram_out("dbg_h", [128, 512], F32)
    io["dbg_hT"] = S.dram_out("dbg_hT", [128, NK * TPC], F32)
    return io


def emit_l0ffn(S, io, debug=False):
    HID = 5632
    NT = TPC // 128
    NH_CH = HID // 512
    ND = DM // 128
    NK = HID // 128
    DC = 256
    with S.scope():
        xin = io["xin"]
        wgT = io["wgT"]
        wuT = io["wuT"]
        wdT = io["wdT"]
        identin = io["ident_in"]
        xout = io["xout"]

        ident = S.sbuf("ident", [128, 128], BF16)
        S.dma("Q", ident, identin)
        xT = S.sbuf("xT", [128, ND * TPC], BF16)
        hT = S.sbuf("hT", [128, NK * TPC], BF16)
        xtiles = [S.sbuf(f"xt{i}", [128, DM], F32) for i in range(1)]
        xh = S.sbuf("xh", [128, DM], BF16)
        junk = S.sbuf("junk", [128, DM], BF16)
        stats = S.sbuf("stats", [128, 8], F32)
        wg = [S.sbuf(f"wg{i}", [128, ND * 512], BF16) for i in range(1)]
        wu = [S.sbuf(f"wu{i}", [128, ND * 512], BF16) for i in range(1)]
        wd = [S.sbuf(f"wd{i}", [128, NK * DC], BF16) for i in range(1)]
        gsil = S.sbuf("gsil", [128, 512], BF16)
        ubuf = S.sbuf("ubuf", [128, 512], BF16)
        htile = S.sbuf("htile", [128, 512], BF16)
        otile = S.sbuf("otile", [128, DC], F32)
        ps = [S.psum(f"ps{i}", [128, 512]) for i in range(4)]
        pt = [S.psum(f"pt{i}", [128, 128], BF16) for i in range(2)]
        pd = [S.psum(f"pd{i}", [128, DC]) for i in range(2)]
        if debug:
            dbg_xh = io["dbg_xh"]
            dbg_xT = io["dbg_xT"]
            dbg_g = io["dbg_g"]
            dbg_u = io["dbg_u"]
            dbg_h = io["dbg_h"]
            dbg_hT = io["dbg_hT"]

        for j in range(NT):
            xtile = xtiles[0]
            S.dma("Q", xtile, xin, in_sl=np.s_[j * 128:(j + 1) * 128, :])
            emit_ln_tile(S, xtile, xh, stats, junk)
            if debug and j == 0:
                S.dma("P", dbg_xh, xh)
            for i in range(ND):
                p = pt[i % 2]
                mm_transpose(S, p, xh, ident, np.s_[:, i * 128:(i + 1) * 128])
                dst = Buf(xT.ap[:, i * TPC + j * 128: i * TPC + (j + 1) * 128], "xTc")
                S.add("V", lambda e, p=p, dst=dst: e.tensor_copy(dst.ap, p.ap),
                      reads=(p,), writes=(xT,))

        if debug:
            S.dma("P", dbg_xT, xT)
        for hc in range(NH_CH):
            h0 = hc * 512
            wgb, wub = wg[0], wu[0]
            wga = wgT.ap[:, h0:h0 + 512].rearrange("(c p) n -> p c n", p=128)
            S.add("Q", lambda e, wga=wga, wgb=wgb: e.dma_start(
                out=wgb.ap.rearrange("p (c n) -> p c n", n=512), in_=wga),
                reads=(wgT,), writes=(wgb,), is_dma=True)
            wua = wuT.ap[:, h0:h0 + 512].rearrange("(c p) n -> p c n", p=128)
            S.add("P", lambda e, wua=wua, wub=wub: e.dma_start(
                out=wub.ap.rearrange("p (c n) -> p c n", n=512), in_=wua),
                reads=(wuT,), writes=(wub,), is_dma=True)
            for j in range(NT):
                pg = ps[(2 * j) % 4]
                pu = ps[(2 * j + 1) % 4]
                mm_acc(S, pg, [(xT, np.s_[:, i * TPC + j * 128:i * TPC + (j + 1) * 128],
                                wgb, np.s_[:, i * 512:(i + 1) * 512]) for i in range(ND)])
                mm_acc(S, pu, [(xT, np.s_[:, i * TPC + j * 128:i * TPC + (j + 1) * 128],
                                wub, np.s_[:, i * 512:(i + 1) * 512]) for i in range(ND)])
                S.add("S", lambda e, pg=pg: e.activation(gsil.ap, pg.ap, AF.Silu),
                      reads=(pg,), writes=(gsil,))
                S.add("V", lambda e, pu=pu: e.tensor_copy(ubuf.ap, pu.ap),
                      reads=(pu,), writes=(ubuf,))
                S.add("V", lambda e: e.tensor_tensor(htile.ap, gsil.ap, ubuf.ap, op=ALU.mult),
                      reads=(gsil, ubuf), writes=(htile,))
                if debug and hc == 0 and j == 0:
                    S.dma("P", dbg_g, gsil)
                    S.dma("P", dbg_u, ubuf)
                    S.dma("P", dbg_h, htile)
                for c in range(4):
                    p = pt[c % 2]
                    mm_transpose(S, p, htile, ident, np.s_[:, c * 128:(c + 1) * 128])
                    kidx = hc * 4 + c
                    dst = Buf(hT.ap[:, kidx * TPC + j * 128: kidx * TPC + (j + 1) * 128], "hTc")
                    S.add("V", lambda e, p=p, dst=dst: e.tensor_copy(dst.ap, p.ap),
                          reads=(p,), writes=(hT,))

        if debug:
            S.dma("P", dbg_hT, hT)
        for dc in range(DM // DC):
            d0 = dc * DC
            wdb = wd[0]
            wda = wdT.ap[:, d0:d0 + DC].rearrange("(c p) n -> p c n", p=128)
            S.add("Q", lambda e, wda=wda, wdb=wdb: e.dma_start(
                out=wdb.ap.rearrange("p (c n) -> p c n", n=DC), in_=wda),
                reads=(wdT,), writes=(wdb,), is_dma=True)
            for j in range(NT):
                p = pd[j % 2]
                xtile = xtiles[0]
                mm_acc(S, p, [(hT, np.s_[:, k * TPC + j * 128:k * TPC + (j + 1) * 128],
                               wdb, np.s_[:, k * DC:(k + 1) * DC]) for k in range(NK)])
                S.dma("Q", xtile, xin, in_sl=np.s_[j * 128:(j + 1) * 128, d0:d0 + DC],
                      out_sl=np.s_[:, :DC])
                S.add("V", lambda e, p=p, xtile=xtile: e.tensor_tensor(
                    otile.ap, p.ap, xtile.ap[:, :DC], op=ALU.add),
                      reads=(p, xtile), writes=(otile,))
                S.dma("Q", xout, otile, out_sl=np.s_[j * 128:(j + 1) * 128, d0:d0 + DC])




def build_l0ffn(debug=False):
    nc = bass.Bass("TRN2", target_bir_lowering=False)
    with ExitStack() as ctx:
        S = Sched(nc, ctx)
        io = _io_l0ffn(S, debug)
        emit_l0ffn(S, io, debug)
        S.finalize()
    return nc


def _io_l1ffn(S):
    ND = DM // 128; NT = TPC // 128
    HID = 8192
    io = {}
    io["xin"] = S.dram_in("xin", [TPC, DM], F32)
    io["wguT"] = S.dram_in("wguT", [DM, 2 * HID], BF16)
    io["wpT"] = S.dram_in("wpT", [HID, DM], BF16)
    io["ident_in"] = S.dram_in("ident_in", [128, 128], BF16)
    io["xout"] = S.dram_out("xout", [TPC, DM], F32)
    return io


def emit_l1ffn(S, io):
    HID = 8192
    TH = 512
    NT = TH // 128
    NH_CH = HID // 512
    ND = DM // 128
    NK = HID // 128
    DC = 256
    EPSF = 1.1920929e-07
    with S.scope():
        xin = io["xin"]
        wguT = io["wguT"]
        wpT = io["wpT"]
        identin = io["ident_in"]
        xout = io["xout"]

        ident = S.sbuf("ident", [128, 128], BF16)
        S.dma("Q", ident, identin)
        xT = S.sbuf("xT", [128, ND * TH], BF16)
        hT = S.sbuf("hT", [128, NK * TH], BF16)
        xtile = S.sbuf("xt0", [128, DM], F32)
        xh = S.sbuf("xh", [128, DM], BF16)
        junk = S.sbuf("junk", [128, DM], BF16)
        stats = S.sbuf("stats", [128, 8], F32)
        sstat = S.sbuf("sstat", [128, 16], F32)
        wg = S.sbuf("wg0", [128, ND * 512], BF16)
        wu = S.sbuf("wu0", [128, ND * 512], BF16)
        wd = S.sbuf("wd0", [128, NK * DC], BF16)
        gsil = S.sbuf("gsil", [128, 512], BF16)
        ubuf = S.sbuf("ubuf", [128, 512], BF16)
        htile = S.sbuf("htile", [128, 512], BF16)
        sqj = S.sbuf("sqj", [128, 512], F32)
        tmp1 = S.sbuf("tmp1", [128, 1], F32)
        otile = S.sbuf("otile", [128, DC], F32)
        ps = [S.psum(f"ps{i}", [128, 512]) for i in range(4)]
        pt = [S.psum(f"pt{i}", [128, 128], BF16) for i in range(2)]
        pd = [S.psum(f"pd{i}", [128, DC]) for i in range(2)]

        for half in range(2):
            toff = half * TH
            for j in range(NT):
                S.dma("Q", xtile, xin, in_sl=np.s_[toff + j * 128:toff + (j + 1) * 128, :])
                emit_ln_tile(S, xtile, xh, stats, junk)
                for i in range(ND):
                    p = pt[i % 2]
                    mm_transpose(S, p, xh, ident, np.s_[:, i * 128:(i + 1) * 128])
                    dst = Buf(xT.ap[:, i * TH + j * 128: i * TH + (j + 1) * 128], "xTc")
                    S.add("V", lambda e, p=p, dst=dst: e.tensor_copy(dst.ap, p.ap),
                          reads=(p,), writes=(xT,))
            for hc in range(NH_CH):
                h0 = hc * 512
                wga = wguT.ap[:, h0:h0 + 512].rearrange("(c p) n -> p c n", p=128)
                S.add("Q", lambda e, wga=wga: e.dma_start(
                    out=wg.ap.rearrange("p (c n) -> p c n", n=512), in_=wga),
                    reads=(wguT,), writes=(wg,), is_dma=True)
                wua = wguT.ap[:, HID + h0:HID + h0 + 512].rearrange("(c p) n -> p c n", p=128)
                S.add("P", lambda e, wua=wua: e.dma_start(
                    out=wu.ap.rearrange("p (c n) -> p c n", n=512), in_=wua),
                    reads=(wguT,), writes=(wu,), is_dma=True)
                for j in range(NT):
                    pg = ps[(2 * j) % 4]
                    pu = ps[(2 * j + 1) % 4]
                    mm_acc(S, pg, [(xT, np.s_[:, i * TH + j * 128:i * TH + (j + 1) * 128],
                                    wg, np.s_[:, i * 512:(i + 1) * 512]) for i in range(ND)])
                    mm_acc(S, pu, [(xT, np.s_[:, i * TH + j * 128:i * TH + (j + 1) * 128],
                                    wu, np.s_[:, i * 512:(i + 1) * 512]) for i in range(ND)])
                    S.add("S", lambda e, pg=pg: e.activation(gsil.ap, pg.ap, AF.Silu),
                          reads=(pg,), writes=(gsil,))
                    S.add("V", lambda e, pu=pu: e.tensor_copy(ubuf.ap, pu.ap),
                          reads=(pu,), writes=(ubuf,))
                    S.add("V", lambda e: e.tensor_tensor(htile.ap, gsil.ap, ubuf.ap, op=ALU.mult),
                          reads=(gsil, ubuf), writes=(htile,))
                    ssj = sstat.sub(np.s_[:, j:j + 1], "ssj")
                    S.add("V", lambda e: e.scalar_tensor_tensor(
                        sqj.ap, htile.ap, 1.0, htile.ap, ALU.mult, ALU.mult,
                        accum_out=tmp1.ap),
                        reads=(htile,), writes=(sqj, tmp1))
                    if hc == 0:
                        S.add("V", lambda e, ssj=ssj: e.tensor_copy(ssj.ap, tmp1.ap),
                              reads=(tmp1,), writes=(ssj,))
                    else:
                        S.add("V", lambda e, ssj=ssj: e.tensor_tensor(
                            ssj.ap, ssj.ap, tmp1.ap, op=ALU.add),
                            reads=(ssj, tmp1), writes=(ssj,))
                    for c in range(4):
                        p = pt[c % 2]
                        mm_transpose(S, p, htile, ident, np.s_[:, c * 128:(c + 1) * 128])
                        kidx = hc * 4 + c
                        dst = Buf(hT.ap[:, kidx * TH + j * 128: kidx * TH + (j + 1) * 128], "hTc")
                        S.add("V", lambda e, p=p, dst=dst: e.tensor_copy(dst.ap, p.ap),
                              reads=(p,), writes=(hT,))
            for j in range(NT):
                ssj = sstat.sub(np.s_[:, j:j + 1], "ssj")
                rstdj = sstat.sub(np.s_[:, 8 + j:9 + j], "rstdj")
                S.add("V", lambda e, ssj=ssj, rstdj=rstdj: e.tensor_scalar(
                    rstdj.ap, ssj.ap, 1.0 / HID, EPSF, ALU.mult, ALU.add),
                    reads=(ssj,), writes=(rstdj,))
                S.add("S", lambda e, rstdj=rstdj: e.activation(rstdj.ap, rstdj.ap, AF.Sqrt),
                      reads=(rstdj,), writes=(rstdj,))
                S.add("V", lambda e, rstdj=rstdj: e.reciprocal(rstdj.ap, rstdj.ap),
                      reads=(rstdj,), writes=(rstdj,))
            for dc in range(DM // DC):
                d0 = dc * DC
                wda = wpT.ap[:, d0:d0 + DC].rearrange("(c p) n -> p c n", p=128)
                S.add("Q", lambda e, wda=wda: e.dma_start(
                    out=wd.ap.rearrange("p (c n) -> p c n", n=DC), in_=wda),
                    reads=(wpT,), writes=(wd,), is_dma=True)
                for j in range(NT):
                    p = pd[j % 2]
                    mm_acc(S, p, [(hT, np.s_[:, k * TH + j * 128:k * TH + (j + 1) * 128],
                                   wd, np.s_[:, k * DC:(k + 1) * DC]) for k in range(NK)])
                    S.dma("Q", xtile, xin,
                          in_sl=np.s_[toff + j * 128:toff + (j + 1) * 128, d0:d0 + DC],
                          out_sl=np.s_[:, :DC])
                    rstdj = sstat.sub(np.s_[:, 8 + j:9 + j], "rstdj")
                    S.add("V", lambda e, p=p, rstdj=rstdj: e.scalar_tensor_tensor(
                        otile.ap, p.ap, rstdj.ap, xtile.ap[:, :DC], ALU.mult, ALU.add),
                        reads=(p, xtile), sreads=(rstdj,), writes=(otile,))
                    S.dma("Q", xout, otile,
                          out_sl=np.s_[toff + j * 128:toff + (j + 1) * 128, d0:d0 + DC])


def emit_rot_hln(S, psrc, qrot, rr4, ri4, rsc, hstats, junk512, ghB, bhB, do_rot=True):
    er = rsc.sub(np.s_[:, 0:256], "er")
    oi = rsc.sub(np.s_[:, 256:512], "oi")
    ei = rsc.sub(np.s_[:, 512:768], "ei")
    eo = rsc.sub(np.s_[:, 768:1024], "eo")
    pe_ = psrc.ap[:, 0:512:2]
    po_ = psrc.ap[:, 1:512:2]
    if do_rot:
        S.add("V", lambda e: e.tensor_tensor(er.ap, pe_, rr4.ap, op=ALU.mult),
              reads=(psrc, rr4), writes=(er,))
        S.add("V", lambda e: e.tensor_tensor(oi.ap, po_, ri4.ap, op=ALU.mult),
              reads=(psrc, ri4), writes=(oi,))
        S.add("V", lambda e: e.tensor_tensor(ei.ap, pe_, ri4.ap, op=ALU.mult),
              reads=(psrc, ri4), writes=(ei,))
        S.add("V", lambda e: e.tensor_tensor(eo.ap, po_, rr4.ap, op=ALU.mult),
              reads=(psrc, rr4), writes=(eo,))
        qe = qrot.ap[:, 0:512:2]
        qo = qrot.ap[:, 1:512:2]
        S.add("V", lambda e: e.tensor_tensor(qe, er.ap, oi.ap, op=ALU.subtract),
              reads=(er, oi), writes=(qrot,))
        S.add("V", lambda e: e.tensor_tensor(qo, ei.ap, eo.ap, op=ALU.add),
              reads=(ei, eo), writes=(qrot,))
        src = qrot
    else:
        S.add("V", lambda e: e.tensor_copy(qrot.ap, psrc.ap), reads=(psrc,), writes=(qrot,))
        src = qrot
    nmu = hstats.sub(np.s_[:, 0:4], "nmu")
    ssq = hstats.sub(np.s_[:, 4:8], "ssq")
    var = hstats.sub(np.s_[:, 8:12], "var")
    rstd = hstats.sub(np.s_[:, 12:16], "rstd")
    v3 = src.ap.rearrange("p (h d) -> p h d", d=128)
    S.add("V", lambda e: e.tensor_reduce(nmu.ap, v3, axis=AX.X, op=ALU.add),
          reads=(src,), writes=(nmu,))
    S.add("V", lambda e: e.tensor_scalar(nmu.ap, nmu.ap, -1.0 / 128, None, ALU.mult),
          reads=(nmu,), writes=(nmu,))
    S.add("S", lambda e: e.activation(junk512.ap, src.ap, AF.Square),
          reads=(src,), writes=(junk512,))
    j3 = junk512.ap.rearrange("p (h d) -> p h d", d=128)
    S.add("V", lambda e: e.tensor_reduce(ssq.ap, j3, axis=AX.X, op=ALU.add),
          reads=(junk512,), writes=(ssq,))
    S.add("V", lambda e: e.scalar_tensor_tensor(var.ap, nmu.ap, 1.0, nmu.ap,
                                                ALU.mult, ALU.mult),
          reads=(nmu,), writes=(var,))
    S.add("V", lambda e: e.scalar_tensor_tensor(var.ap, ssq.ap, 1.0 / 128, var.ap,
                                                ALU.mult, ALU.subtract),
          reads=(ssq, var), writes=(var,))
    S.add("V", lambda e: e.tensor_scalar(rstd.ap, var.ap, 1.0, LNEPS, ALU.mult, ALU.add),
          reads=(var,), writes=(rstd,))
    S.add("S", lambda e: e.activation(rstd.ap, rstd.ap, AF.Sqrt),
          reads=(rstd,), writes=(rstd,))
    S.add("V", lambda e: e.reciprocal(rstd.ap, rstd.ap), reads=(rstd,), writes=(rstd,))
    for h in range(4):
        sl = np.s_[:, h * 128:(h + 1) * 128]
        hv = src.sub(sl, "hv")
        S.add("V", lambda e, hv=hv, h=h: e.tensor_scalar(
            hv.ap, hv.ap, nmu.ap[:, h:h + 1], rstd.ap[:, h:h + 1], ALU.add, ALU.mult),
            reads=(src,), sreads=(nmu, rstd), writes=(src,))
    S.add("V", lambda e: e.tensor_tensor(src.ap, src.ap, ghB.ap, op=ALU.mult),
          reads=(src, ghB), writes=(src,))
    S.add("V", lambda e: e.tensor_tensor(src.ap, src.ap, bhB.ap, op=ALU.add),
          reads=(src, bhB), writes=(src,))




def build_l1ffn():
    nc = bass.Bass("TRN2", target_bir_lowering=False)
    with ExitStack() as ctx:
        S = Sched(nc, ctx)
        io = _io_l1ffn(S)
        emit_l1ffn(S, io)
        S.finalize()
    return nc


def _io_l0a_pre(S, phases=3):
    ND = DM // 128; NT = TPC // 128
    io = {}
    io["xin"] = S.dram_in("xin", [TPC, DM], F32)
    io["wq"] = S.dram_in("wq", [DM, DM], F32)
    io["wkv"] = S.dram_in("wkv", [DM, 2 * DM], F32)
    io["bq"] = S.dram_in("bq", [1, DM], F32)
    io["bkv"] = S.dram_in("bkv", [1, 2 * DM], F32)
    io["gq_b"] = S.dram_in("gq_b", [128, DM], F32)
    io["bqln_b"] = S.dram_in("bqln_b", [128, DM], F32)
    io["gkv_b"] = S.dram_in("gkv_b", [128, DM], F32)
    io["bkvln_b"] = S.dram_in("bkvln_b", [128, DM], F32)
    io["gh_b"] = S.dram_in("gh_b", [128, 512], F32)
    io["bh_b"] = S.dram_in("bh_b", [128, 512], F32)
    io["cosg"] = S.dram_in("cosg", [TPC, 64], F32)
    io["sing"] = S.dram_in("sing", [TPC, 64], F32)
    io["ones_in"] = S.dram_in("ones_in", [1, 128], BF16)
    io["ident_in"] = S.dram_in("ident_in", [128, 128], BF16)
    io["qn"] = S.dram_out("qn", [TPC, DM], BF16)
    io["knT"] = S.dram_out("knT", [DM, TPC], BF16)
    io["v"] = S.dram_out("v", [TPC, DM], BF16)
    return io


def emit_l0a_pre(S, io, phases=3):
    NT = TPC // 128
    ND = DM // 128
    with S.scope():
        xin = io["xin"]
        wq_d = io["wq"]
        wkv_d = io["wkv"]
        bq_d = io["bq"]
        bkv_d = io["bkv"]
        gq_b = io["gq_b"]
        bqln_b = io["bqln_b"]
        gkv_b = io["gkv_b"]
        bkvln_b = io["bkvln_b"]
        gh_b = io["gh_b"]
        bh_b = io["bh_b"]
        cosg = io["cosg"]
        sing = io["sing"]
        onesin = io["ones_in"]
        identin = io["ident_in"]
        qn_o = io["qn"]
        knT_o = io["knT"]
        v_o = io["v"]

        ident = S.sbuf("ident", [128, 128], BF16)
        S.dma("Q", ident, identin)
        ones = S.sbuf("ones", [1, 128], BF16)
        S.dma("Q", ones, onesin)
        bqs = S.sbuf("bqs", [1, DM], BF16)
        S.dma("P", bqs, bq_d)
        bkvs = S.sbuf("bkvs", [1, 2 * DM], BF16)
        S.dma("P", bkvs, bkv_d)
        gqB = S.sbuf("gqB", [128, DM], F32)
        S.dma("Q", gqB, gq_b)
        bqB = S.sbuf("bqB", [128, DM], F32)
        S.dma("Q", bqB, bqln_b)
        gkvB = S.sbuf("gkvB", [128, DM], F32)
        S.dma("Q", gkvB, gkv_b)
        bkvB = S.sbuf("bkvB", [128, DM], F32)
        S.dma("Q", bkvB, bkvln_b)
        ghB = S.sbuf("ghB", [128, 512], F32)
        S.dma("Q", ghB, gh_b)
        bhB = S.sbuf("bhB", [128, 512], F32)
        S.dma("Q", bhB, bh_b)
        wch = [S.sbuf(f"wch{i}", [128, ND * 512], BF16) for i in range(2)]

        xtile = S.sbuf("xtile", [128, DM], F32)
        xh = S.sbuf("xh", [128, DM], BF16)
        junk = S.sbuf("junk", [128, DM], BF16)
        stats = S.sbuf("stats", [128, 8], F32)
        qinT = S.sbuf("qinT", [128, ND * TPC], BF16)
        kvinT = S.sbuf("kvinT", [128, ND * TPC], BF16)
        lnq = S.sbuf("lnq", [128, DM], BF16)
        cosb = S.sbuf("cosb", [128, 64], F32)
        sinb = S.sbuf("sinb", [128, 64], F32)
        rr4a = S.sbuf("rr4a", [128, 8 * 256], F32)
        ri4a = S.sbuf("ri4a", [128, 8 * 256], F32)
        rsc = S.sbuf("rsc", [128, 1024], F32)
        hstats = S.sbuf("hstats", [128, 32], F32)
        junk512 = S.sbuf("junk512", [128, 512], BF16)
        qrot = S.sbuf("qrot", [128, 512], BF16)
        vtile = S.sbuf("vtile", [128, 512], BF16)
        ps = [S.psum(f"ps{i}", [128, 512]) for i in range(4)]
        pt = [S.psum(f"pt{i}", [128, 128], BF16) for i in range(4)]

        rrm = S.sbuf("rrm", [128, NT * 8], F32)
        for j in range(NT):
            S.dma("Q", xtile, xin, in_sl=np.s_[j * 128:(j + 1) * 128, :])
            emit_ln_tile(S, xtile, xh, stats, junk, extra_stats=True)
            r2 = stats.sub(np.s_[:, 3:4], "r2")
            mm_ = stats.sub(np.s_[:, 4:5], "m")
            S.add("V", lambda e, j=j: e.tensor_copy(rrm.ap[:, 8 * j:8 * j + 1], stats.ap[:, 3:4]),
                  reads=(stats,), writes=(rrm,))
            S.add("V", lambda e, j=j: e.tensor_copy(rrm.ap[:, 8 * j + 1:8 * j + 2], stats.ap[:, 4:5]),
                  reads=(stats,), writes=(rrm,))
            for variant, gB, bB, dstT in ((0, gqB, bqB, qinT), (1, gkvB, bkvB, kvinT)):
                S.add("V", lambda e, gB=gB, r2=r2: e.scalar_tensor_tensor(
                    lnq.ap, xh.ap, r2.ap, gB.ap, ALU.mult, ALU.mult),
                    reads=(xh, gB), sreads=(stats,), writes=(lnq,))
                S.add("V", lambda e, bB=bB: e.tensor_tensor(lnq.ap, lnq.ap, bB.ap, op=ALU.add),
                      reads=(lnq, bB), writes=(lnq,))
                for i in range(ND):
                    p = pt[i % 4]
                    mm_transpose(S, p, lnq, ident, np.s_[:, i * 128:(i + 1) * 128])
                    dst = Buf(dstT.ap[:, i * TPC + j * 128: i * TPC + (j + 1) * 128], "T")
                    S.add("V", lambda e, p=p, dst=dst: e.tensor_copy(dst.ap, p.ap),
                          reads=(p,), writes=(dstT,))

        for j in (range(NT) if phases >= 2 else []):
            S.dma("Q", cosb, cosg, in_sl=np.s_[j * 128:(j + 1) * 128, :])
            S.dma("Q", sinb, sing, in_sl=np.s_[j * 128:(j + 1) * 128, :])
            for rep in range(4):
                c0 = j * 256 + rep * 64
                S.add("V", lambda e, c0=c0, j=j: e.tensor_scalar(
                    rr4a.ap[:, c0:c0 + 64], cosb.ap, rrm.ap[:, 8 * j + 1:8 * j + 2],
                    None, ALU.mult),
                    reads=(cosb,), sreads=(rrm,), writes=(rr4a,))
                S.add("V", lambda e, c0=c0, j=j: e.tensor_scalar(
                    ri4a.ap[:, c0:c0 + 64], sinb.ap, rrm.ap[:, 8 * j + 1:8 * j + 2],
                    None, ALU.mult),
                    reads=(sinb,), sreads=(rrm,), writes=(ri4a,))

        for hc in (range(12) if phases >= 3 else []):
            wb = wch[hc % 2]
            if hc < 4:
                wsrc, col0, inT, brow = wq_d, hc * 512, qinT, bqs
                bcol = hc * 512
            else:
                wsrc, col0, inT, brow = wkv_d, (hc - 4) * 512, kvinT, bkvs
                bcol = (hc - 4) * 512
            wa = wsrc.ap[:, col0:col0 + 512].rearrange("(c p) n -> p c n", p=128)
            S.add("P", lambda e, wa=wa, wb=wb: e.dma_start(
                out=wb.ap.rearrange("p (c n) -> p c n", n=512), in_=wa),
                reads=(wsrc,), writes=(wb,), is_dma=True)
            for j in range(NT):
                p = ps[j % 2] if hc < 4 else ps[2 + j % 2]
                terms = [(inT, np.s_[:, i * TPC + j * 128:i * TPC + (j + 1) * 128],
                          wb, np.s_[:, i * 512:(i + 1) * 512]) for i in range(ND)]
                terms.append((ones, None, brow, np.s_[:, bcol:bcol + 512]))
                mm_acc(S, p, terms)
                rr4 = rr4a.sub(np.s_[:, j * 256:(j + 1) * 256], "rr4")
                ri4 = ri4a.sub(np.s_[:, j * 256:(j + 1) * 256], "ri4")
                if hc < 8:
                    emit_rot_hln(S, p, qrot, rr4, ri4, rsc, hstats, junk512, ghB, bhB)
                    if hc < 4:
                        S.dma("Q", qn_o, qrot,
                              out_sl=np.s_[j * 128:(j + 1) * 128, hc * 512:(hc + 1) * 512])
                    else:
                        kc = hc - 4
                        for c in range(4):
                            pp = pt[c % 4]
                            mm_transpose(S, pp, qrot, ident, np.s_[:, c * 128:(c + 1) * 128])
                            S.add("V", lambda e, pp=pp: e.tensor_copy(junk512.ap[:, :128], pp.ap),
                                  reads=(pp,), writes=(junk512,))
                            S.dma("Q", knT_o, junk512,
                                  in_sl=np.s_[:, :128],
                                  out_sl=np.s_[(kc * 4 + c) * 128:(kc * 4 + c + 1) * 128,
                                               j * 128:(j + 1) * 128])
                else:
                    S.add("V", lambda e, p=p: e.tensor_copy(vtile.ap, p.ap),
                          reads=(p,), writes=(vtile,))
                    S.dma("Q", v_o, vtile,
                          out_sl=np.s_[j * 128:(j + 1) * 128, (hc - 8) * 512:(hc - 7) * 512])




def build_l0a_pre(phases=3):
    nc = bass.Bass("TRN2", target_bir_lowering=False)
    with ExitStack() as ctx:
        S = Sched(nc, ctx)
        io = _io_l0a_pre(S, phases)
        emit_l0a_pre(S, io, phases)
        S.finalize()
    return nc


def _io_l0a_attn(S, nheads=16, outproj=True, parts=5):
    ND = DM // 128; NT = TPC // 128
    io = {}
    io["xin"] = S.dram_in("xin", [TPC, DM], F32)
    io["qn"] = S.dram_in("qn", [TPC, DM], BF16)
    io["knT"] = S.dram_in("knT", [DM, SEQ], BF16)
    io["v"] = S.dram_in("v", [SEQ, DM], BF16)
    io["wo"] = S.dram_in("wo", [DM, DM], F32)
    io["bo"] = S.dram_in("bo", [1, DM], F32)
    io["ones_in"] = S.dram_in("ones_in", [1, 128], BF16)
    io["ident_in"] = S.dram_in("ident_in", [128, 128], BF16)
    io["xout"] = S.dram_out("xout", [TPC, DM], F32)
    return io


def emit_l0a_attn(S, io, nheads=16, outproj=True, parts=5):
    NT = TPC // 128
    ND = DM // 128
    NKC = SEQ // 512
    NKT = SEQ // 128
    SCALE = float(128.0 ** -0.5)
    with S.scope():
        xin = io["xin"]
        qn_d = io["qn"]
        knT_d = io["knT"]
        v_d = io["v"]
        wo_d = io["wo"]
        bo_d = io["bo"]
        onesin = io["ones_in"]
        identin = io["ident_in"]
        xout = io["xout"]

        ident = S.sbuf("ident", [128, 128], BF16)
        S.dma("Q", ident, identin)
        ones = S.sbuf("ones", [1, 128], BF16)
        S.dma("Q", ones, onesin)
        bos = S.sbuf("bos", [1, DM], BF16)
        S.dma("P", bos, bo_d)
        qnT = S.sbuf("qnT", [128, ND * TPC], BF16)
        obuf = S.sbuf("obuf", [128, NT * DM], BF16)
        knh = [S.sbuf(f"knh{i}", [128, SEQ], BF16) for i in range(2)]
        vh = [S.sbuf(f"vh{i}", [128, NKT * 128], BF16) for i in range(2)]
        stash = S.sbuf("stash", [128, SEQ], F32)
        Pb = S.sbuf("Pb", [128, SEQ], BF16)
        PT = S.sbuf("PT", [128, NKT * 128], BF16)
        m8 = S.sbuf("m8", [128, 16], F32)
        nms = S.sbuf("nms", [128, 1], F32)
        rsum = S.sbuf("rsum", [128, 1], F32)
        otile = S.sbuf("otile", [128, 128], F32)
        wos = [S.sbuf(f"wos{i}", [128, ND * 512], BF16) for i in range(2)]
        xtile = S.sbuf("xtile", [128, DM], F32)
        outt = S.sbuf("outt", [128, 512], F32)
        ps = [S.psum(f"ps{i}", [128, 512]) for i in range(2)]
        po = [S.psum(f"po{i}", [128, 128]) for i in range(2)]
        pt = [S.psum(f"pt{i}", [128, 128], BF16) for i in range(4)]

        qtl = S.sbuf("qtl", [128, DM], BF16)
        for j in range(NT):
            S.dma("Q", qtl, qn_d, in_sl=np.s_[j * 128:(j + 1) * 128, :])
            for i in range(ND):
                p = pt[i % 4]
                mm_transpose(S, p, qtl, ident, np.s_[:, i * 128:(i + 1) * 128])
                dst = Buf(qnT.ap[:, i * TPC + j * 128: i * TPC + (j + 1) * 128], "qT")
                S.add("V", lambda e, p=p, dst=dst: e.tensor_copy(dst.ap, p.ap),
                      reads=(p,), writes=(qnT,))

        for h in range(nheads):
            kb = knh[h % 2]
            vb = vh[h % 2]
            S.dma("Q", kb, knT_d, in_sl=np.s_[h * 128:(h + 1) * 128, :])
            va = v_d.ap[:, h * 128:(h + 1) * 128].rearrange("(kt p) n -> p kt n", p=128)
            S.add("Q", lambda e, va=va, vb=vb: e.dma_start(
                out=vb.ap.rearrange("p (kt n) -> p kt n", n=128), in_=va),
                reads=(v_d,), writes=(vb,), is_dma=True)
            for j in (range(NT) if parts >= 2 else []):
                for kc in range(NKC):
                    p = ps[kc % 2]
                    mm_acc(S, p, [(qnT, np.s_[:, h * TPC + j * 128:h * TPC + (j + 1) * 128],
                                   kb, np.s_[:, kc * 512:(kc + 1) * 512])])
                    dst = stash.sub(np.s_[:, kc * 512:(kc + 1) * 512], "stc")
                    mm8 = m8.sub(np.s_[:, kc:kc + 1], "m8c")
                    S.add("V", lambda e, p=p, dst=dst, mm8=mm8: e.tensor_scalar(
                        dst.ap, p.ap, 1.0, None, ALU.mult, ALU.max, accum_out=mm8.ap),
                        reads=(p,), writes=(stash, mm8))
                if parts < 3:
                    continue
                S.add("V", lambda e: e.reduce_max(nms.ap, m8.ap[:, :NKC], axis=AX.X,
                                                  negate=True),
                      reads=(m8,), writes=(nms,))
                S.add("V", lambda e: e.tensor_scalar(nms.ap, nms.ap, SCALE, None, ALU.mult),
                      reads=(nms,), writes=(nms,))
                S.add("S", lambda e: e.activation(Pb.ap, stash.ap, AF.Exp, bias=nms.ap,
                                                  scale=SCALE, accum_out=rsum.ap),
                      reads=(stash,), sreads=(nms,), writes=(Pb, rsum))
                if parts < 4:
                    continue
                for kt in range(NKT):
                    p = pt[kt % 4]
                    mm_transpose(S, p, Pb, ident, np.s_[:, kt * 128:(kt + 1) * 128])
                    dst = Buf(PT.ap[:, kt * 128:(kt + 1) * 128], "PTc")
                    S.add("V", lambda e, p=p, dst=dst: e.tensor_copy(dst.ap, p.ap),
                          reads=(p,), writes=(PT,))
                if parts < 5:
                    continue
                p = po[j % 2]
                mm_acc(S, p, [(PT, np.s_[:, kt * 128:(kt + 1) * 128],
                               vb, np.s_[:, kt * 128:(kt + 1) * 128]) for kt in range(NKT)])
                S.add("V", lambda e: e.reciprocal(rsum.ap, rsum.ap),
                      reads=(rsum,), writes=(rsum,))
                S.add("V", lambda e, p=p: e.tensor_scalar(otile.ap, p.ap, rsum.ap, None,
                                                          ALU.mult),
                      reads=(p,), sreads=(rsum,), writes=(otile,))
                dst = Buf(obuf.ap[:, j * DM + h * 128: j * DM + (h + 1) * 128], "oc")
                S.add("V", lambda e, dst=dst: e.tensor_copy(dst.ap, otile.ap),
                      reads=(otile,), writes=(obuf,))

        oT = qnT
        for j in range(NT):
            for i in range(ND):
                p = pt[i % 4]
                src = Buf(obuf.ap[:, j * DM + i * 128: j * DM + (i + 1) * 128], "ob")
                src.last_w = obuf.last_w
                S.add("T", lambda e, p=p, src=src: e.transpose(p.ap, src.ap, ident.ap),
                      reads=(obuf, ident), writes=(p,))
                dst = Buf(oT.ap[:, i * TPC + j * 128: i * TPC + (j + 1) * 128], "oT")
                S.add("V", lambda e, p=p, dst=dst: e.tensor_copy(dst.ap, p.ap),
                      reads=(p,), writes=(oT,))
        for hc in (range(4) if outproj else []):
            wb = wos[hc % 2]
            wa = wo_d.ap[:, hc * 512:(hc + 1) * 512].rearrange("(c p) n -> p c n", p=128)
            S.add("P", lambda e, wa=wa, wb=wb: e.dma_start(
                out=wb.ap.rearrange("p (c n) -> p c n", n=512), in_=wa),
                reads=(wo_d,), writes=(wb,), is_dma=True)
            for j in range(NT):
                p = ps[j % 2]
                terms = [(oT, np.s_[:, i * TPC + j * 128:i * TPC + (j + 1) * 128],
                          wb, np.s_[:, i * 512:(i + 1) * 512]) for i in range(ND)]
                terms.append((ones, None, bos, np.s_[:, hc * 512:(hc + 1) * 512]))
                mm_acc(S, p, terms)
                S.dma("Q", xtile, xin, in_sl=np.s_[j * 128:(j + 1) * 128, hc * 512:(hc + 1) * 512],
                      out_sl=np.s_[:, :512])
                S.add("V", lambda e, p=p: e.tensor_tensor(outt.ap, p.ap, xtile.ap[:, :512],
                                                          op=ALU.add),
                      reads=(p, xtile), writes=(outt,))
                S.dma("Q", xout, outt, out_sl=np.s_[j * 128:(j + 1) * 128, hc * 512:(hc + 1) * 512])




def build_l0a_attn(nheads=16, outproj=True, parts=5):
    nc = bass.Bass("TRN2", target_bir_lowering=False)
    with ExitStack() as ctx:
        S = Sched(nc, ctx)
        io = _io_l0a_attn(S, nheads, outproj, parts)
        emit_l0a_attn(S, io, nheads, outproj, parts)
        S.finalize()
    return nc


def _io_l1b_pre(S):
    ND = DM // 128; NT = TPC // 128
    io = {}
    io["xin"] = S.dram_in("xin", [TPC, DM], F32)
    io["wqT"] = S.dram_in("wqT", [DM, DM], BF16)
    io["wkT"] = S.dram_in("wkT", [DM, 512], BF16)
    io["wvT"] = S.dram_in("wvT", [DM, 512], BF16)
    io["bq"] = S.dram_in("bq", [1, DM], F32)
    io["bk"] = S.dram_in("bk", [1, 512], F32)
    io["bv"] = S.dram_in("bv", [1, 512], F32)
    io["Rq"] = S.dram_in("Rq", [TPC, 128], F32)
    io["Rk"] = S.dram_in("Rk", [TPC, 128], F32)
    io["ones_in"] = S.dram_in("ones_in", [1, 128], BF16)
    io["ident_in"] = S.dram_in("ident_in", [128, 128], BF16)
    io["q"] = S.dram_out("q", [TPC, DM], BF16)
    io["kT"] = S.dram_out("kT", [512, TPC], BF16)
    io["v"] = S.dram_out("v", [TPC, 512], BF16)
    return io


def emit_l1b_pre(S, io):
    NT = TPC // 128
    ND = DM // 128
    with S.scope():
        xin = io["xin"]
        wqT_d = io["wqT"]
        wkT_d = io["wkT"]
        wvT_d = io["wvT"]
        bq_d = io["bq"]
        bk_d = io["bk"]
        bv_d = io["bv"]
        Rq_d = io["Rq"]
        Rk_d = io["Rk"]
        onesin = io["ones_in"]
        identin = io["ident_in"]
        q_o = io["q"]
        kT_o = io["kT"]
        v_o = io["v"]

        ident = S.sbuf("ident", [128, 128], BF16)
        S.dma("Q", ident, identin)
        ones = S.sbuf("ones", [1, 128], BF16)
        S.dma("Q", ones, onesin)
        bqs = S.sbuf("bqs", [1, DM], BF16)
        S.dma("P", bqs, bq_d)
        bks = S.sbuf("bks", [1, 512], BF16)
        S.dma("P", bks, bk_d)
        bvs = S.sbuf("bvs", [1, 512], BF16)
        S.dma("P", bvs, bv_d)
        xT = S.sbuf("xT", [128, ND * TPC], BF16)
        xtile = S.sbuf("xtile", [128, DM], F32)
        xh = S.sbuf("xh", [128, DM], BF16)
        junk = S.sbuf("junk", [128, DM], BF16)
        stats = S.sbuf("stats", [128, 8], F32)
        wch = [S.sbuf(f"wch{i}", [128, ND * 512], BF16) for i in range(2)]
        R4 = S.sbuf("R4", [128, 512], F32)
        Rtile = S.sbuf("Rtile", [128, 128], F32)
        qtile = S.sbuf("qtile", [128, 512], BF16)
        ps = [S.psum(f"ps{i}", [128, 512]) for i in range(4)]
        pt = [S.psum(f"pt{i}", [128, 128], BF16) for i in range(4)]

        for j in range(NT):
            S.dma("Q", xtile, xin, in_sl=np.s_[j * 128:(j + 1) * 128, :])
            emit_ln_tile(S, xtile, xh, stats, junk)
            for i in range(ND):
                p = pt[i % 4]
                mm_transpose(S, p, xh, ident, np.s_[:, i * 128:(i + 1) * 128])
                dst = Buf(xT.ap[:, i * TPC + j * 128: i * TPC + (j + 1) * 128], "xTc")
                S.add("V", lambda e, p=p, dst=dst: e.tensor_copy(dst.ap, p.ap),
                      reads=(p,), writes=(xT,))

        for hc in range(6):
            wb = wch[hc % 2]
            if hc < 4:
                wsrc, col0, brow, bcol = wqT_d, hc * 512, bqs, hc * 512
            elif hc == 4:
                wsrc, col0, brow, bcol = wkT_d, 0, bks, 0
            else:
                wsrc, col0, brow, bcol = wvT_d, 0, bvs, 0
            wa = wsrc.ap[:, col0:col0 + 512].rearrange("(c p) n -> p c n", p=128)
            S.add("Q", lambda e, wa=wa, wb=wb: e.dma_start(
                out=wb.ap.rearrange("p (c n) -> p c n", n=512), in_=wa),
                reads=(wsrc,), writes=(wb,), is_dma=True)
            for j in range(NT):
                p = ps[j % 2] if hc < 4 else ps[2 + j % 2]
                terms = [(xT, np.s_[:, i * TPC + j * 128:i * TPC + (j + 1) * 128],
                          wb, np.s_[:, i * 512:(i + 1) * 512]) for i in range(ND)]
                terms.append((ones, None, brow, np.s_[:, bcol:bcol + 512]))
                mm_acc(S, p, terms)
                if hc < 5:
                    Rsrc = Rq_d if hc < 4 else Rk_d
                    S.dma("Q", Rtile, Rsrc, in_sl=np.s_[j * 128:(j + 1) * 128, :])
                    for rep in range(4):
                        dst = R4.sub(np.s_[:, rep * 128:(rep + 1) * 128], f"r{rep}")
                        S.add("V", lambda e, dst=dst: e.tensor_copy(dst.ap, Rtile.ap),
                              reads=(Rtile,), writes=(dst,))
                    S.add("V", lambda e, p=p: e.tensor_tensor(qtile.ap, p.ap, R4.ap,
                                                              op=ALU.mult),
                          reads=(p, R4), writes=(qtile,))
                    if hc < 4:
                        S.dma("Q", q_o, qtile,
                              out_sl=np.s_[j * 128:(j + 1) * 128, hc * 512:(hc + 1) * 512])
                    else:
                        for c in range(4):
                            pp = pt[c % 4]
                            mm_transpose(S, pp, qtile, ident, np.s_[:, c * 128:(c + 1) * 128])
                            S.add("V", lambda e, pp=pp: e.tensor_copy(junk.ap[:, :128], pp.ap),
                                  reads=(pp,), writes=(junk,))
                            S.dma("Q", kT_o, junk, in_sl=np.s_[:, :128],
                                  out_sl=np.s_[c * 128:(c + 1) * 128, j * 128:(j + 1) * 128])
                else:
                    S.add("V", lambda e, p=p: e.tensor_copy(qtile.ap, p.ap),
                          reads=(p,), writes=(qtile,))
                    S.dma("Q", v_o, qtile, in_sl=np.s_[:, :512],
                          out_sl=np.s_[j * 128:(j + 1) * 128, :])




def build_l1b_pre():
    nc = bass.Bass("TRN2", target_bir_lowering=False)
    with ExitStack() as ctx:
        S = Sched(nc, ctx)
        io = _io_l1b_pre(S)
        emit_l1b_pre(S, io)
        S.finalize()
    return nc


def _io_l1b_attn(S, cpos):
    ND = DM // 128; NT = TPC // 128
    io = {}
    io["xin"] = S.dram_in("xin", [TPC, DM], F32)
    io["q"] = S.dram_in("q", [TPC, DM], BF16)
    io["kT"] = S.dram_in("kT", [512, SEQ], BF16)
    io["v"] = S.dram_in("v", [SEQ, 512], BF16)
    io["woT"] = S.dram_in("woT", [DM, DM], BF16)
    io["bo"] = S.dram_in("bo", [1, DM], F32)
    io["cm4_in"] = S.dram_in("cm4_in", [4 * 128, 512], F32)
    io["ones_in"] = S.dram_in("ones_in", [1, 128], BF16)
    io["ident_in"] = S.dram_in("ident_in", [128, 128], BF16)
    io["xout"] = S.dram_out("xout", [TPC, DM], F32)
    return io


def emit_l1b_attn(S, io, cpos):
    NT = TPC // 128
    ND = DM // 128
    with S.scope():
        xin = io["xin"]
        q_d = io["q"]
        kT_d = io["kT"]
        v_d = io["v"]
        woT_d = io["woT"]
        bo_d = io["bo"]
        cm4_d = io["cm4_in"]
        onesin = io["ones_in"]
        identin = io["ident_in"]
        xout = io["xout"]

        ident = S.sbuf("ident", [128, 128], BF16)
        S.dma("Q", ident, identin)
        ones = S.sbuf("ones", [1, 128], BF16)
        S.dma("Q", ones, onesin)
        bos = S.sbuf("bos", [1, DM], BF16)
        S.dma("P", bos, bo_d)
        cm4 = S.sbuf("cm4", [128, 4 * 512], F32)
        cma = cm4_d.ap.rearrange("(a p) n -> p a n", p=128)
        S.add("Q", lambda e: e.dma_start(out=cm4.ap.rearrange("p (a n) -> p a n", n=512),
                                         in_=cma), reads=(cm4_d,), writes=(cm4,), is_dma=True)
        qT = S.sbuf("qT", [128, ND * TPC], BF16)
        obuf = S.sbuf("obuf", [128, NT * DM], BF16)
        knh = [S.sbuf(f"knh{i}", [128, SEQ], BF16) for i in range(2)]
        vh = [S.sbuf(f"vh{i}", [128, (SEQ // 128) * 128], BF16) for i in range(2)]
        stash = S.sbuf("stash", [128, SEQ], F32)
        Pb = S.sbuf("Pb", [128, SEQ], BF16)
        PT = S.sbuf("PT", [128, SEQ], BF16)
        m8 = S.sbuf("m8", [128, 16], F32)
        nms = S.sbuf("nms", [128, 1], F32)
        rsum = S.sbuf("rsum", [128, 1], F32)
        otile = S.sbuf("otile", [128, 128], F32)
        wos = [S.sbuf(f"wos{i}", [128, ND * 512], BF16) for i in range(2)]
        xtile = S.sbuf("xtile", [128, DM], F32)
        outt = S.sbuf("outt", [128, 512], F32)
        qtl = S.sbuf("qtl", [128, DM], BF16)
        if cpos is None:
            cmall_d = io["cmall"]
            cmt = S.sbuf("cmt", [128, SEQ], F32)
        ps = [S.psum(f"ps{i}", [128, 512]) for i in range(2)]
        po = [S.psum(f"po{i}", [128, 128]) for i in range(2)]
        pt = [S.psum(f"pt{i}", [128, 128], BF16) for i in range(4)]

        for j in range(NT):
            S.dma("Q", qtl, q_d, in_sl=np.s_[j * 128:(j + 1) * 128, :])
            for i in range(ND):
                p = pt[i % 4]
                mm_transpose(S, p, qtl, ident, np.s_[:, i * 128:(i + 1) * 128])
                dst = Buf(qT.ap[:, i * TPC + j * 128: i * TPC + (j + 1) * 128], "qT")
                S.add("V", lambda e, p=p, dst=dst: e.tensor_copy(dst.ap, p.ap),
                      reads=(p,), writes=(qT,))

        for kvh in range(4):
            kb = knh[kvh % 2]
            vb = vh[kvh % 2]
            S.dma("Q", kb, kT_d, in_sl=np.s_[kvh * 128:(kvh + 1) * 128, :])
            va = v_d.ap[:, kvh * 128:(kvh + 1) * 128].rearrange("(kt p) n -> p kt n", p=128)
            S.add("Q", lambda e, va=va, vb=vb: e.dma_start(
                out=vb.ap.rearrange("p (kt n) -> p kt n", n=128), in_=va),
                reads=(v_d,), writes=(vb,), is_dma=True)
            for qh in range(kvh * 4, kvh * 4 + 4):
                for j in range(NT):
                    if cpos is None:
                        nkc = 8
                        klen = nkc * 512
                        S.dma("Q", cmt, cmall_d, in_sl=np.s_[j * 128:(j + 1) * 128, :])
                    else:
                        Q0 = cpos * TPC + j * 128
                        nkc = (Q0 + 128 + 511) // 512
                        klen = nkc * 512
                    for kc in range(nkc):
                        p = ps[kc % 2]
                        mm_acc(S, p, [(qT, np.s_[:, qh * TPC + j * 128:qh * TPC + (j + 1) * 128],
                                       kb, np.s_[:, kc * 512:(kc + 1) * 512])])
                        dst = stash.sub(np.s_[:, kc * 512:(kc + 1) * 512], f"stc{kc}")
                        mm8 = m8.sub(np.s_[:, kc:kc + 1], f"m8c{kc}")
                        if cpos is None:
                            cmsl = cmt.sub(np.s_[:, kc * 512:(kc + 1) * 512], f"cmt{kc}")
                            S.add("V", lambda e, p=p, dst=dst, cmsl=cmsl: e.scalar_tensor_tensor(
                                dst.ap, p.ap, 1.0, cmsl.ap, ALU.mult, ALU.add),
                                reads=(p, cmsl), writes=(stash,))
                            S.add("V", lambda e, dst=dst, mm8=mm8: e.reduce_max(
                                mm8.ap, dst.ap, axis=AX.X),
                                reads=(stash,), writes=(mm8,))
                        elif kc == nkc - 1 and (Q0 + 128) % 512 != 0 or cpos is not None and kc * 512 + 512 > Q0 + 128:
                            a = (Q0 - kc * 512) // 128
                            cmsl = cm4.sub(np.s_[:, a * 512:(a + 1) * 512], f"cm{a}")
                            S.add("V", lambda e, p=p, dst=dst, cmsl=cmsl: e.scalar_tensor_tensor(
                                dst.ap, p.ap, 1.0, cmsl.ap, ALU.mult, ALU.add),
                                reads=(p, cmsl), writes=(stash,))
                            S.add("V", lambda e, dst=dst, mm8=mm8: e.reduce_max(
                                mm8.ap, dst.ap, axis=AX.X),
                                reads=(stash,), writes=(mm8,))
                        else:
                            S.add("V", lambda e, p=p, dst=dst, mm8=mm8: e.tensor_scalar(
                                dst.ap, p.ap, 1.0, None, ALU.mult, ALU.max, accum_out=mm8.ap),
                                reads=(p,), writes=(stash, mm8))
                    S.add("V", lambda e, nkc=nkc: e.reduce_max(nms.ap, m8.ap[:, :nkc],
                                                               axis=AX.X, negate=True),
                          reads=(m8,), writes=(nms,))
                    S.add("S", lambda e, klen=klen: e.activation(
                        Pb.ap[:, :klen], stash.ap[:, :klen], AF.Exp, bias=nms.ap,
                        scale=1.0, accum_out=rsum.ap),
                        reads=(stash,), sreads=(nms,), writes=(Pb, rsum))
                    nkt = klen // 128
                    for kt in range(nkt):
                        p = pt[kt % 4]
                        mm_transpose(S, p, Pb, ident, np.s_[:, kt * 128:(kt + 1) * 128])
                        dst = PT.sub(np.s_[:, kt * 128:(kt + 1) * 128], f"PT{kt}")
                        S.add("V", lambda e, p=p, dst=dst: e.tensor_copy(dst.ap, p.ap),
                              reads=(p,), writes=(PT,))
                    p = po[j % 2]
                    mm_acc(S, p, [(PT, np.s_[:, kt * 128:(kt + 1) * 128],
                                   vb, np.s_[:, kt * 128:(kt + 1) * 128]) for kt in range(nkt)])
                    S.add("V", lambda e: e.reciprocal(rsum.ap, rsum.ap),
                          reads=(rsum,), writes=(rsum,))
                    S.add("V", lambda e, p=p: e.tensor_scalar(otile.ap, p.ap, rsum.ap, None,
                                                              ALU.mult),
                          reads=(p,), sreads=(rsum,), writes=(otile,))
                    dst = Buf(obuf.ap[:, j * DM + qh * 128: j * DM + (qh + 1) * 128], "oc")
                    S.add("V", lambda e, dst=dst: e.tensor_copy(dst.ap, otile.ap),
                          reads=(otile,), writes=(obuf,))

        oT = qT
        for j in range(NT):
            for i in range(ND):
                p = pt[i % 4]
                src = Buf(obuf.ap[:, j * DM + i * 128: j * DM + (i + 1) * 128], "ob")
                src.last_w = obuf.last_w
                S.add("T", lambda e, p=p, src=src: e.transpose(p.ap, src.ap, ident.ap),
                      reads=(obuf, ident), writes=(p,))
                dst = Buf(oT.ap[:, i * TPC + j * 128: i * TPC + (j + 1) * 128], "oT")
                S.add("V", lambda e, p=p, dst=dst: e.tensor_copy(dst.ap, p.ap),
                      reads=(p,), writes=(oT,))
        for hc in range(4):
            wb = wos[hc % 2]
            wa = woT_d.ap[:, hc * 512:(hc + 1) * 512].rearrange("(c p) n -> p c n", p=128)
            S.add("Q", lambda e, wa=wa, wb=wb: e.dma_start(
                out=wb.ap.rearrange("p (c n) -> p c n", n=512), in_=wa),
                reads=(woT_d,), writes=(wb,), is_dma=True)
            for j in range(NT):
                p = ps[j % 2]
                terms = [(oT, np.s_[:, i * TPC + j * 128:i * TPC + (j + 1) * 128],
                          wb, np.s_[:, i * 512:(i + 1) * 512]) for i in range(ND)]
                terms.append((ones, None, bos, np.s_[:, hc * 512:(hc + 1) * 512]))
                mm_acc(S, p, terms)
                S.dma("Q", xtile, xin, in_sl=np.s_[j * 128:(j + 1) * 128, hc * 512:(hc + 1) * 512],
                      out_sl=np.s_[:, :512])
                S.add("V", lambda e, p=p: e.tensor_tensor(outt.ap, p.ap, xtile.ap[:, :512],
                                                          op=ALU.add),
                      reads=(p, xtile), writes=(outt,))
                S.dma("Q", xout, outt, out_sl=np.s_[j * 128:(j + 1) * 128, hc * 512:(hc + 1) * 512])


MB_NH, MB_HP = 64, 64


def mamba_bufs(S, CT, with_Y=False):
    b = {}
    b["CT"] = CT
    b["with_Y"] = with_Y
    b["py"] = [S.psum(f"mpy{i}", [128, 512]) for i in range(2)]
    b["pf"] = [S.psum(f"mpf{i}", [128, 512]) for i in range(3)]
    b["pbf"] = [S.psum(f"mpb{i}", [128, 128], BF16) for i in range(3)]
    b["Adt"] = S.sbuf("mAdt", [128, 64], F32)
    b["G"] = S.sbuf("mG", [128, 64], F32)
    b["negG"] = S.sbuf("mnegG", [128, 64], F32)
    b["eG"] = S.sbuf("meG", [128, 64], F32)
    b["GF"] = S.sbuf("mGF", [64, 128], F32)
    b["decF"] = S.sbuf("mdecF", [64, 128], F32)
    b["decT"] = S.sbuf("mdecT", [128, 64], F32)
    b["dtd"] = S.sbuf("mdtd", [128, 64], F32)
    b["dchB"] = S.sbuf("mdchB", [128, 64], F32)
    b["BTM"] = S.sbuf("mBTM", [128, 128], BF16)
    b["xTM"] = S.sbuf("mxTM", [128, 4096], BF16)
    b["xdtd"] = S.sbuf("mxdtd", [128, 4096], BF16)
    b["Sst"] = S.sbuf("mSst", [128, 4096], F32)
    b["Sbf"] = S.sbuf("mSbf", [128, 4096], BF16)
    if with_Y:
        b["BCt"] = S.sbuf("mBCt", [128, 128], F32)
        b["Earg"] = S.sbuf("mEarg", [128, 128], F32)
        b["Ebuf"] = S.sbuf("mEbuf", [128, 128], F32)
        b["M2"] = S.sbuf("mM2", [128, 128], BF16)
        b["xdt1"] = S.sbuf("mxdt1", [128, 64], BF16)
        b["ytile"] = S.sbuf("mytile", [128, 4096], F32)
    b["gl_d"] = S.dram_tmp("mgl_d", [64], F32)
    b["glrow"] = S.sbuf("mglrow", [1, 64], F32)
    if with_Y:
        b["ghi"] = S.sbuf("mghi", [64, 128], BF16)
        b["glo"] = S.sbuf("mglo", [64, 128], BF16)
        b["gsc"] = S.sbuf("mgsc", [64, 128], F32)
        b["g2_d"] = S.dram_tmp("mg2_d", [2 * 8192], BF16)
        b["gflat2"] = S.sbuf("mgflat2", [2, 8192], BF16)
    return b


def emit_mamba_chunk(S, bufs, conv_fm, dtp, c, with_Y, consts_b):
    T = 128
    CT = bufs["CT"]
    ident = consts_b["ident"]
    identf = consts_b["identf"]
    onesb = consts_b["ones"]
    onesf = consts_b["onesf"]
    trif = consts_b["trif"]
    Ab = consts_b["Ab"]
    py, pf, pbf = bufs["py"], bufs["pf"], bufs["pbf"]
    Adt, G, negG, eG = bufs["Adt"], bufs["G"], bufs["negG"], bufs["eG"]
    GF, decF, decT, dtd, dchB = bufs["GF"], bufs["decF"], bufs["decT"], bufs["dtd"], bufs["dchB"]
    BTM, xTM, xdtd, Sst, Sbf = bufs["BTM"], bufs["xTM"], bufs["xdtd"], bufs["Sst"], bufs["Sbf"]
    ytile = bufs.get("ytile")

    def bsl(i):
        return np.s_[:, i * CT + c * T: i * CT + (c + 1) * T]

    S.add("V", lambda e: e.tensor_tensor(Adt.ap, dtp.ap, Ab.ap, op=ALU.mult),
          reads=(dtp, Ab), writes=(Adt,))
    S.add("T", lambda e: e.matmul(pf[0].ap[:, :64], trif.ap, Adt.ap, start=True, stop=True),
          reads=(trif, Adt), writes=(pf[0],))
    S.add("V", lambda e: e.tensor_copy(G.ap, pf[0].ap[:, :64]), reads=(pf[0],), writes=(G,))
    S.add("V", lambda e: e.tensor_scalar(negG.ap, G.ap, -1.0, None, ALU.mult),
          reads=(G,), writes=(negG,))
    S.add("S", lambda e: e.activation(eG.ap, G.ap, AF.Exp), reads=(G,), writes=(eG,))
    S.add("T", lambda e: e.transpose(pf[1].ap[:64, :128], G.ap, identf.ap),
          reads=(G, identf), writes=(pf[1],))
    S.add("V", lambda e: e.tensor_copy(GF.ap, pf[1].ap[:64, :128]), reads=(pf[1],), writes=(GF,))
    S.dma("Q", bufs["gl_d"], GF, in_sl=np.s_[:, 127:128])
    gla = bufs["gl_d"].ap.rearrange("(a b) -> a b", a=1)
    S.add("Q", lambda e, gla=gla: e.dma_start(out=bufs["glrow"].ap, in_=gla),
          reads=(bufs["gl_d"],), writes=(bufs["glrow"],), is_dma=True)
    if with_Y:
        S.add("V", lambda e: e.tensor_copy(bufs["ghi"].ap, GF.ap),
              reads=(GF,), writes=(bufs["ghi"],))
        S.add("V", lambda e: e.scalar_tensor_tensor(bufs["gsc"].ap, bufs["ghi"].ap, -1.0,
                                                    GF.ap, ALU.mult, ALU.add),
              reads=(bufs["ghi"], GF), writes=(bufs["gsc"],))
        S.add("V", lambda e: e.tensor_copy(bufs["glo"].ap, bufs["gsc"].ap),
              reads=(bufs["gsc"],), writes=(bufs["glo"],))
        g2 = bufs["g2_d"].ap.rearrange("(a b) -> a b", a=2)
        S.add("Q", lambda e, g2=g2: e.dma_start(out=g2[0:1, :], in_=bufs["ghi"].ap),
              reads=(bufs["ghi"],), writes=(bufs["g2_d"],), is_dma=True)
        S.add("Q", lambda e, g2=g2: e.dma_start(out=g2[1:2, :], in_=bufs["glo"].ap),
              reads=(bufs["glo"],), writes=(bufs["g2_d"],), is_dma=True)
        S.add("Q", lambda e, g2=g2: e.dma_start(out=bufs["gflat2"].ap, in_=g2),
              reads=(bufs["g2_d"],), writes=(bufs["gflat2"],), is_dma=True)
    S.add("S", lambda e: e.activation(decF.ap, GF.ap, AF.Exp,
                                      bias=GF.ap[:, 127:128], scale=-1.0),
          reads=(GF,), sreads=(GF,), writes=(decF,))
    S.add("T", lambda e: e.matmul(pf[2].ap[:, :64], onesf.ap, bufs["glrow"].ap,
                                  start=True, stop=True),
          reads=(onesf, bufs["glrow"]), writes=(pf[2],))
    S.add("S", lambda e: e.activation(dchB.ap, pf[2].ap[:, :64], AF.Exp),
          reads=(pf[2],), writes=(dchB,))
    S.add("T", lambda e: e.transpose(pf[1].ap[:, :64], decF.ap, identf.ap[:64, :64]),
          reads=(decF, identf), writes=(pf[1],))
    S.add("V", lambda e: e.tensor_copy(decT.ap, pf[1].ap[:, :64]),
          reads=(pf[1],), writes=(decT,))
    S.add("V", lambda e: e.tensor_tensor(dtd.ap, dtp.ap, decT.ap, op=ALU.mult),
          reads=(dtp, decT), writes=(dtd,))
    S.add("T", lambda e: e.transpose(pbf[0].ap, conv_fm.ap[bsl(32)], ident.ap),
          reads=(conv_fm, ident), writes=(pbf[0],))
    S.add("V", lambda e: e.tensor_copy(BTM.ap, pbf[0].ap), reads=(pbf[0],), writes=(BTM,))
    for i in range(32):
        pp = pbf[1 + i % 2]
        S.add("T", lambda e, i=i, pp=pp: e.transpose(pp.ap, conv_fm.ap[bsl(i)], ident.ap),
              reads=(conv_fm, ident), writes=(pp,))
        dst = xTM.sub(np.s_[:, i * 128:(i + 1) * 128], f"x{i}")
        S.add("V", lambda e, pp=pp, dst=dst: e.tensor_copy(dst.ap, pp.ap),
              reads=(pp,), writes=(xTM,))
    for h in range(MB_NH):
        sl = np.s_[:, h * MB_HP:(h + 1) * MB_HP]
        S.add("V", lambda e, sl=sl, h=h: e.tensor_scalar(
            xdtd.ap[sl], xTM.ap[sl], dtd.ap[:, h:h + 1], None, ALU.mult),
            reads=(xTM,), sreads=(dtd,), writes=(xdtd,))
    if with_Y:
        for hb in range(8):
            p = py[hb % 2]
            S.add("T", lambda e, p=p, hb=hb: e.matmul(
                p.ap, conv_fm.ap[bsl(33)], Sbf.ap[:, hb * 512:(hb + 1) * 512],
                start=True, stop=True),
                reads=(conv_fm, Sbf), writes=(p,))
            for hh in range(8):
                h = hb * 8 + hh
                sl = np.s_[:, h * MB_HP:(h + 1) * MB_HP]
                psl = np.s_[:, hh * MB_HP:(hh + 1) * MB_HP]
                S.add("V", lambda e, p=p, sl=sl, psl=psl, h=h: e.tensor_scalar(
                    ytile.ap[sl], p.ap[psl], eG.ap[:, h:h + 1], None, ALU.mult),
                    reads=(p,), sreads=(eG,), writes=(ytile,))
        for h in range(MB_NH):
            sl = np.s_[:, h * MB_HP:(h + 1) * MB_HP]
            S.add("V", lambda e, sl=sl, h=h: e.scalar_tensor_tensor(
                ytile.ap[sl], xTM.ap[sl], consts_b["Db"].ap[:, h:h + 1],
                ytile.ap[sl], ALU.mult, ALU.add),
                reads=(xTM, ytile), sreads=(consts_b["Db"],), writes=(ytile,))
        S.add("T", lambda e: e.matmul(pf[1].ap[:, :128], conv_fm.ap[bsl(32)],
                                      conv_fm.ap[bsl(33)], start=True, stop=True),
              reads=(conv_fm,), writes=(pf[1],))
        S.add("V", lambda e: e.tensor_copy(bufs["BCt"].ap, pf[1].ap[:, :128]),
              reads=(pf[1],), writes=(bufs["BCt"],))
        for h in range(MB_NH):
            pE = pf[2]
            S.add("T", lambda e, pE=pE, h=h: e.matmul(
                pE.ap[:, :128], consts_b["ones2"].ap, bufs["gflat2"].ap[:, h * 128:(h + 1) * 128],
                start=True, stop=True),
                reads=(consts_b["ones2"], bufs["gflat2"]), writes=(pE,))
            S.add("V", lambda e, pE=pE: e.tensor_tensor(
                bufs["Earg"].ap, pE.ap[:, :128], consts_b["nmask"].ap, op=ALU.add),
                reads=(pE, consts_b["nmask"]), writes=(bufs["Earg"],))
            S.add("S", lambda e, h=h: e.activation(bufs["Ebuf"].ap, bufs["Earg"].ap, AF.Exp,
                                                   bias=negG.ap[:, h:h + 1]),
                  reads=(bufs["Earg"],), sreads=(negG,), writes=(bufs["Ebuf"],))
            S.add("V", lambda e: e.tensor_tensor(bufs["M2"].ap, bufs["Ebuf"].ap,
                                                 bufs["BCt"].ap, op=ALU.mult),
                  reads=(bufs["Ebuf"], bufs["BCt"]), writes=(bufs["M2"],))
            sl = np.s_[:, h * MB_HP:(h + 1) * MB_HP]
            S.add("V", lambda e, sl=sl, h=h: e.tensor_scalar(
                bufs["xdt1"].ap, xTM.ap[sl], dtp.ap[:, h:h + 1], None, ALU.mult),
                reads=(xTM,), sreads=(dtp,), writes=(bufs["xdt1"],))
            pyd = pf[0]
            S.add("T", lambda e, pyd=pyd: e.matmul(pyd.ap[:, :64], bufs["M2"].ap,
                                                   bufs["xdt1"].ap, start=True, stop=True),
                  reads=(bufs["M2"], bufs["xdt1"]), writes=(pyd,))
            S.add("V", lambda e, pyd=pyd, sl=sl: e.tensor_tensor(
                ytile.ap[sl], ytile.ap[sl], pyd.ap[:, :64], op=ALU.add),
                reads=(ytile, pyd), writes=(ytile,))
    for hb in range(8):
        p = py[hb % 2]
        S.add("T", lambda e, p=p, hb=hb: e.matmul(
            p.ap, BTM.ap, xdtd.ap[:, hb * 512:(hb + 1) * 512], start=True, stop=True),
            reads=(BTM, xdtd), writes=(p,))
        for hh in range(8):
            h = hb * 8 + hh
            sl = np.s_[:, h * MB_HP:(h + 1) * MB_HP]
            psl = np.s_[:, hh * MB_HP:(hh + 1) * MB_HP]
            S.add("V", lambda e, p=p, sl=sl, psl=psl, h=h: e.scalar_tensor_tensor(
                Sst.ap[sl], Sst.ap[sl], dchB.ap[:, h:h + 1], p.ap[psl], ALU.mult, ALU.add),
                reads=(Sst, p), sreads=(dchB,), writes=(Sst,))
    if with_Y:
        S.add("V", lambda e: e.tensor_copy(Sbf.ap, Sst.ap), reads=(Sst,), writes=(Sbf,))


def emit_softplus(S, raw, dtp, sc1, sc2):
    S.add("S", lambda e: e.activation(sc1.ap, raw.ap, AF.Abs), reads=(raw,), writes=(sc1,))
    S.add("S", lambda e: e.activation(sc1.ap, sc1.ap, AF.Exp, scale=-1.0),
          reads=(sc1,), writes=(sc1,))
    S.add("S", lambda e: e.activation(sc1.ap, sc1.ap, AF.Ln, bias=1.0),
          reads=(sc1,), writes=(sc1,))
    S.add("V", lambda e: e.tensor_scalar(sc2.ap, raw.ap, 0.0, None, ALU.max),
          reads=(raw,), writes=(sc2,))
    S.add("V", lambda e: e.tensor_tensor(dtp.ap, sc1.ap, sc2.ap, op=ALU.add),
          reads=(sc1, sc2), writes=(dtp,))


def mamba_consts(S):
    if getattr(S, "_mamba_cb", None) is not None:
        return S._mamba_cb
    cb = {}
    cb["ident"] = S.sbuf("cident", [128, 128], BF16)
    S.dma("Q", cb["ident"], S.dram_in("ident_in", [128, 128], BF16))
    cb["identf"] = S.sbuf("cidentf", [128, 128], F32)
    S.dma("Q", cb["identf"], S.dram_in("identf_in", [128, 128], F32))
    cb["ones"] = S.sbuf("cones", [1, 128], BF16)
    S.dma("Q", cb["ones"], S.dram_in("ones_in", [1, 128], BF16))
    cb["onesf"] = S.sbuf("conesf", [1, 128], F32)
    S.dma("Q", cb["onesf"], S.dram_in("onesf_in", [1, 128], F32))
    cb["ones2"] = S.sbuf("cones2", [2, 128], BF16)
    S.dma("Q", cb["ones2"], S.dram_in("ones2_in", [2, 128], BF16))
    cb["trif"] = S.sbuf("ctrif", [128, 128], F32)
    S.dma("Q", cb["trif"], S.dram_in("trif_in", [128, 128], F32))
    cb["nmask"] = S.sbuf("cnmask", [128, 128], F32)
    S.dma("Q", cb["nmask"], S.dram_in("nmask_in", [128, 128], F32))
    cb["Ab"] = S.sbuf("cAb", [128, 64], F32)
    alog = S.dram_in("alog_b", [128, 64], F32)
    S.dma("Q", cb["Ab"], alog)
    S.add("S", lambda e: e.activation(cb["Ab"].ap, cb["Ab"].ap, AF.Exp),
          reads=(cb["Ab"],), writes=(cb["Ab"],))
    S.add("V", lambda e: e.tensor_scalar(cb["Ab"].ap, cb["Ab"].ap, -1.0, None, ALU.mult),
          reads=(cb["Ab"],), writes=(cb["Ab"],))
    cb["Db"] = S.sbuf("cDb", [128, 64], F32)
    S.dma("Q", cb["Db"], S.dram_in("d_b", [128, 64], F32))
    cb["dtbB"] = S.sbuf("cdtbB", [128, 64], F32)
    S.dma("Q", cb["dtbB"], S.dram_in("dtb_b", [128, 64], F32))
    S._mamba_cb = cb
    return cb




def build_l1b_attn(cpos):
    nc = bass.Bass("TRN2", target_bir_lowering=False)
    with ExitStack() as ctx:
        S = Sched(nc, ctx)
        io = _io_l1b_attn(S, cpos)
        emit_l1b_attn(S, io, cpos)
        S.finalize()
    return nc


def _io_mmb_pre(S):
    ND = DM // 128; NT = TPC // 128
    TT_ = 9 * 128
    io = {}
    io["xin"] = S.dram_in("xin", [TT_, DM], F32)
    io["wT"] = S.dram_in("wT", [DM, 8576], BF16)
    io["convw"] = S.dram_in("convw", [34 * 128, 4], F32)
    io["convb"] = S.dram_in("convb", [34 * 128, 1], F32)
    io["conv"] = S.dram_out("conv", [34 * 128, TPC], BF16)
    io["z"] = S.dram_out("z", [TPC, 4096], BF16)
    io["dtp"] = S.dram_out("dtp", [TPC, 64], F32)
    io["sloc"] = S.dram_out("sloc", [128, 4096], F32)
    io["dloc"] = S.dram_out("dloc", [1, 64], F32)
    return io


def emit_mmb_pre(S, io):
    NTH = 9
    TT_ = NTH * 128
    ND = DM // 128
    with S.scope():
        xin = io["xin"]
        wT_d = io["wT"]
        convw_d = io["convw"]
        convb_d = io["convb"]
        cb = mamba_consts(S)
        conv_o = io["conv"]
        z_o = io["z"]
        dt_o = io["dtp"]
        sloc_o = io["sloc"]
        dloc_o = io["dloc"]

        xT = S.sbuf("xT", [128, ND * TT_], BF16)
        conv_fm = S.sbuf("convfm", [128, 34 * TPC], BF16)
        xtile = S.sbuf("xtile", [128, DM], F32)
        xh = S.sbuf("xh", [128, DM], BF16)
        junk = S.sbuf("junk", [128, DM], BF16)
        stats = S.sbuf("stats", [128, 8], F32)
        bufs = mamba_bufs(S, TPC)
        cacc = S.sbuf("cacc", [128, TPC], F32)
        wl = [S.sbuf(f"wl{i}", [128, ND * 128], BF16) for i in range(1)]
        wz = [S.sbuf(f"wz{i}", [128, ND * 512], BF16) for i in range(1)]
        xbc = [S.sbuf(f"xbc{i}", [128, TT_], BF16) for i in range(1)]
        cwcol = S.sbuf("cwcol", [128, 4], F32)
        cbcol = S.sbuf("cbcol", [128, 1], F32)
        ztile = S.sbuf("ztile", [128, 512], BF16)
        dtraw = S.sbuf("dtraw", [128, 64], F32)
        dtpb = S.sbuf("dtpb", [128, 8 * 64], F32)
        sc1 = S.sbuf("sc1", [128, 64], F32)
        sc2 = S.sbuf("sc2", [128, 64], F32)

        for j in range(NTH):
            S.dma("Q", xtile, xin, in_sl=np.s_[j * 128:(j + 1) * 128, :])
            emit_ln_tile(S, xtile, xh, stats, junk)
            for i in range(ND):
                p = bufs["pbf"][i % 3]
                mm_transpose(S, p, xh, ident=cb["ident"], src_sl=np.s_[:, i * 128:(i + 1) * 128])
                dst = Buf(xT.ap[:, i * TT_ + j * 128: i * TT_ + (j + 1) * 128], "xTc")
                S.add("V", lambda e, p=p, dst=dst: e.tensor_copy(dst.ap, p.ap),
                      reads=(p,), writes=(xT,))

        for r in range(34):
            wb = wl[0]
            wa = wT_d.ap[:, 4096 + r * 128: 4096 + (r + 1) * 128].rearrange(
                "(c p) n -> p c n", p=128)
            S.add("P", lambda e, wa=wa, wb=wb: e.dma_start(
                out=wb.ap.rearrange("p (c n) -> p c n", n=128), in_=wa),
                reads=(wT_d,), writes=(wb,), is_dma=True)
            xb = xbc[0]
            for tc, t0, tw in ((0, 0, 512), (1, 512, 512), (2, 1024, 128)):
                p = bufs["py"][tc % 2]
                mm_acc(S, p, [(wb, np.s_[:, i * 128:(i + 1) * 128],
                               xT, np.s_[:, i * TT_ + t0: i * TT_ + t0 + tw])
                              for i in range(ND)], width=tw)
                S.add("V", lambda e, p=p, t0=t0, tw=tw, xb=xb: e.tensor_copy(
                    xb.ap[:, t0:t0 + tw], p.ap[:, :tw]),
                    reads=(p,), writes=(xb,))
            S.dma("Q", cwcol, convw_d, in_sl=np.s_[r * 128:(r + 1) * 128, :])
            S.dma("Q", cbcol, convb_d, in_sl=np.s_[r * 128:(r + 1) * 128, :])
            acc = cacc
            S.add("V", lambda e, xb=xb: e.tensor_scalar(
                acc.ap, xb.ap[:, 125:125 + TPC], cwcol.ap[:, 0:1], None, ALU.mult),
                reads=(xb,), sreads=(cwcol,), writes=(acc,))
            for i in range(1, 4):
                S.add("V", lambda e, xb=xb, i=i: e.scalar_tensor_tensor(
                    acc.ap, xb.ap[:, 125 + i:125 + i + TPC], cwcol.ap[:, i:i + 1],
                    acc.ap, ALU.mult, ALU.add),
                    reads=(xb, acc), sreads=(cwcol,), writes=(acc,))
            dst = conv_fm.sub(np.s_[:, r * TPC:(r + 1) * TPC], f"cv{r}")
            S.add("S", lambda e, dst=dst: e.activation(dst.ap, acc.ap, AF.Silu,
                                                       bias=cbcol.ap),
                  reads=(acc,), sreads=(cbcol,), writes=(dst,))
            S.dma("Q", conv_o, dst, out_sl=np.s_[r * 128:(r + 1) * 128, :])

        for hc in range(8):
            wb = wz[0]
            wa = wT_d.ap[:, hc * 512:(hc + 1) * 512].rearrange("(c p) n -> p c n", p=128)
            S.add("P", lambda e, wa=wa, wb=wb: e.dma_start(
                out=wb.ap.rearrange("p (c n) -> p c n", n=512), in_=wa),
                reads=(wT_d,), writes=(wb,), is_dma=True)
            for j in range(8):
                p = bufs["py"][j % 2]
                mm_acc(S, p, [(xT, np.s_[:, i * TT_ + 128 + j * 128: i * TT_ + 128 + (j + 1) * 128],
                               wb, np.s_[:, i * 512:(i + 1) * 512]) for i in range(ND)])
                S.add("V", lambda e, p=p: e.tensor_copy(ztile.ap, p.ap),
                      reads=(p,), writes=(ztile,))
                S.dma("Q", z_o, ztile, out_sl=np.s_[j * 128:(j + 1) * 128, hc * 512:(hc + 1) * 512])
        wdt = wz[0]
        wa = wT_d.ap[:, 8448:8512].rearrange("(c p) n -> p c n", p=128)
        S.add("P", lambda e, wa=wa: e.dma_start(
            out=wdt.ap[:, :ND * 64].rearrange("p (c n) -> p c n", n=64), in_=wa),
            reads=(wT_d,), writes=(wdt,), is_dma=True)
        for j in range(8):
            p = bufs["pf"][j % 2]
            mm_acc(S, p, [(xT, np.s_[:, i * TT_ + 128 + j * 128: i * TT_ + 128 + (j + 1) * 128],
                           wdt, np.s_[:, i * 64:(i + 1) * 64]) for i in range(ND)], width=64)
            S.add("V", lambda e, p=p: e.tensor_tensor(dtraw.ap, p.ap[:, :64], cb["dtbB"].ap,
                                                      op=ALU.add),
                  reads=(p, cb["dtbB"]), writes=(dtraw,))
            dtp = dtpb.sub(np.s_[:, j * 64:(j + 1) * 64], f"dtp{j}")
            emit_softplus(S, dtraw, dtp, sc1, sc2)
            S.dma("Q", dt_o, dtp, out_sl=np.s_[j * 128:(j + 1) * 128, :])

        S.add("V", lambda e: e.memset(bufs["Sst"].ap, 0.0), writes=(bufs["Sst"],))
        glsum = S.sbuf("glsum", [1, 64], F32)
        S.add("V", lambda e: e.memset(glsum.ap, 0.0), writes=(glsum,))
        for c in range(8):
            dtp = dtpb.sub(np.s_[:, c * 64:(c + 1) * 64], f"dtp{c}")
            emit_mamba_chunk(S, bufs, conv_fm, dtp, c, False, cb)
            S.add("V", lambda e: e.tensor_tensor(glsum.ap, glsum.ap, bufs["glrow"].ap,
                                                 op=ALU.add),
                  reads=(glsum, bufs["glrow"]), writes=(glsum,))
        S.add("S", lambda e: e.activation(glsum.ap, glsum.ap, AF.Exp),
              reads=(glsum,), writes=(glsum,))
        S.dma("Q", dloc_o, glsum)
        S.dma("Q", sloc_o, bufs["Sst"])




def build_mmb_pre():
    nc = bass.Bass("TRN2", target_bir_lowering=False)
    with ExitStack() as ctx:
        S = Sched(nc, ctx)
        io = _io_mmb_pre(S)
        emit_mmb_pre(S, io)
        S.finalize()
    return nc


def _io_mmb_post(S, cpos, debug=False):
    ND = DM // 128; NT = TPC // 128
    io = {}
    io["xin"] = S.dram_in("xin", [TPC, DM], F32)
    io["conv"] = S.dram_in("conv", [34 * 128, TPC], BF16)
    io["z"] = S.dram_in("z", [TPC, 4096], BF16)
    io["dtp"] = S.dram_in("dtp", [TPC, 64], F32)
    io["slocs"] = S.dram_in("slocs", [4 * 128, 4096], F32)
    io["dlocx"] = S.dram_in("dlocx", [1, 4 * 4096], F32)
    io["wpT"] = S.dram_in("wpT", [4096, DM], BF16)
    io["xout"] = S.dram_out("xout", [TPC, DM], F32)
    io["dbg_y"] = S.dram_out("dbg_y", [128, 4096], F32)
    io["dbg_E"] = S.dram_out("dbg_E", [128, 128], F32)
    io["dbg_G"] = S.dram_out("dbg_G", [128, 64], F32)
    io["dbg_M2"] = S.dram_out("dbg_M2", [128, 128], F32)
    io["dbg_BCt"] = S.dram_out("dbg_BCt", [128, 128], F32)
    return io


def emit_mmb_post(S, io, cpos, debug=False):
    ND = DM // 128
    with S.scope():
        xin = io["xin"]
        conv_d = io["conv"]
        z_d = io["z"]
        dtp_d = io["dtp"]
        slocs_d = io["slocs"]
        dlocx_d = io["dlocx"]
        wpT_d = io["wpT"]
        cb = mamba_consts(S)
        xout = io["xout"]

        bufs = mamba_bufs(S, 128, with_Y=True)
        conv_sb = S.sbuf("convsb", [128, 34 * 128], BF16)
        sltmp = S.sbuf("sltmp", [128, 4096], F32)
        zt = S.sbuf("zt", [128, 4096], BF16)
        junkz = S.sbuf("junkz", [128, 4096], BF16)
        dtpc = S.sbuf("dtpc", [128, 64], F32)
        ynTc = S.sbuf("ynTc", [128, 32 * 128], BF16)
        wp = S.sbuf("wp", [128, 32 * 512], BF16)
        xtile = S.sbuf("xtile", [128, 512], F32)
        otile = S.sbuf("otile", [128, 512], F32)
        rsts = S.sbuf("rsts", [128, 8], F32)
        dlr = S.sbuf("mdlr", [1, 512], F32)

        if debug:
            dbg_y = io["dbg_y"]
            dbg_E = io["dbg_E"]
            dbg_G = io["dbg_G"]
            dbg_M2 = io["dbg_M2"]
            dbg_BCt = io["dbg_BCt"]
        S.add("V", lambda e: e.memset(bufs["Sst"].ap, 0.0), writes=(bufs["Sst"],))
        nprev = 3 if cpos is None else cpos
        if cpos is None:
            cmb_d = io["cmb"]
            cmbs = S.sbuf("cmbs", [128, 16], F32)
            S.dma("Q", cmbs, cmb_d)
            dl4 = io["dloc4"]
            dlsb = S.sbuf("dlsb", [1, 256], F32)
            dla = dl4.ap.rearrange("a b -> (a b)").rearrange("(a b) -> a b", a=1)
            S.add("Q", lambda e, dla=dla: e.dma_start(out=dlsb.ap, in_=dla),
                  reads=(dl4,), writes=(dlsb,), is_dma=True)
            dlocB = S.sbuf("dlocB", [128, 64], F32)
        for i in range(nprev):
            S.dma("Q", sltmp, slocs_d, in_sl=np.s_[i * 128:(i + 1) * 128, :])
            if cpos is None:
                p = bufs["pf"][2]
                S.add("T", lambda e, p=p, i=i: e.matmul(
                    p.ap[:, :64], cb["onesf"].ap, dlsb.ap[:, i * 64:(i + 1) * 64],
                    start=True, stop=True),
                    reads=(cb["onesf"], dlsb), writes=(p,))
                S.add("V", lambda e, p=p, i=i: e.tensor_scalar(
                    dlocB.ap, p.ap[:, :64], cmbs.ap[:, 3 * i:3 * i + 1],
                    cmbs.ap[:, 3 * i + 1:3 * i + 2], ALU.mult, ALU.add),
                    reads=(p,), sreads=(cmbs,), writes=(dlocB,))
                for c8 in range(8):
                    sl = np.s_[:, c8 * 512:(c8 + 1) * 512]
                    s3 = bufs["Sst"].ap[sl].rearrange("p (h q) -> p h q", q=64)
                    db3 = dlocB.ap[:, c8 * 8:(c8 + 1) * 8].unsqueeze(-1).broadcast_to([128, 8, 64])
                    S.add("V", lambda e, s3=s3, db3=db3: e.tensor_tensor(
                        s3, s3, db3, op=ALU.mult),
                        reads=(bufs["Sst"], dlocB), writes=(bufs["Sst"],))
                    S.add("V", lambda e, sl=sl, i=i: e.tensor_scalar(
                        sltmp.ap[sl], sltmp.ap[sl], cmbs.ap[:, 3 * i + 2:3 * i + 3],
                        None, ALU.mult),
                        reads=(sltmp,), sreads=(cmbs,), writes=(sltmp,))
                    S.add("V", lambda e, sl=sl: e.tensor_tensor(
                        bufs["Sst"].ap[sl], bufs["Sst"].ap[sl], sltmp.ap[sl], op=ALU.add),
                        reads=(bufs["Sst"], sltmp), writes=(bufs["Sst"],))
                continue
            for c8 in range(8):
                p = bufs["pf"][c8 % 2]
                row = np.s_[0:1, i * 4096 + c8 * 512: i * 4096 + (c8 + 1) * 512]
                S.dma("Q", dlr, dlocx_d, in_sl=row)
                S.add("T", lambda e, p=p, dlr=dlr: e.matmul(p.ap, cb["onesf"].ap, dlr.ap,
                                                            start=True, stop=True),
                      reads=(cb["onesf"], dlr), writes=(p,))
                sl = np.s_[:, c8 * 512:(c8 + 1) * 512]
                S.add("V", lambda e, p=p, sl=sl: e.tensor_tensor(
                    bufs["Sst"].ap[sl], bufs["Sst"].ap[sl], p.ap, op=ALU.mult),
                    reads=(bufs["Sst"], p), writes=(bufs["Sst"],))
                S.add("V", lambda e, sl=sl: e.tensor_tensor(
                    bufs["Sst"].ap[sl], bufs["Sst"].ap[sl], sltmp.ap[sl], op=ALU.add),
                    reads=(bufs["Sst"], sltmp), writes=(bufs["Sst"],))
        S.add("V", lambda e: e.tensor_copy(bufs["Sbf"].ap, bufs["Sst"].ap),
              reads=(bufs["Sst"],), writes=(bufs["Sbf"],))

        for c in range(8):
            ca = conv_d.ap[:, c * 128:(c + 1) * 128].rearrange("(r p) n -> p r n", p=128)
            S.add("Q", lambda e, ca=ca: e.dma_start(
                out=conv_sb.ap.rearrange("p (r n) -> p r n", n=128), in_=ca),
                reads=(conv_d,), writes=(conv_sb,), is_dma=True)
            S.dma("Q", dtpc, dtp_d, in_sl=np.s_[c * 128:(c + 1) * 128, :])
            emit_mamba_chunk(S, bufs, conv_sb, dtpc, 0, True, cb)
            ytile = bufs["ytile"]
            if debug and c == 0:
                S.dma("P", dbg_y, ytile)
                S.dma("P", dbg_E, bufs["Ebuf"])
                S.dma("P", dbg_G, bufs["G"])
                S.dma("P", dbg_M2, bufs["M2"])
                S.dma("P", dbg_BCt, bufs["BCt"])
            S.dma("Q", zt, z_d, in_sl=np.s_[c * 128:(c + 1) * 128, :])
            S.add("S", lambda e: e.activation(junkz.ap, zt.ap, AF.Silu),
                  reads=(zt,), writes=(junkz,))
            S.add("V", lambda e: e.tensor_tensor(ytile.ap, ytile.ap, junkz.ap, op=ALU.mult),
                  reads=(ytile, junkz), writes=(ytile,))
            ss = rsts.sub(np.s_[:, 0:1], "ss")
            rstd = rsts.sub(np.s_[:, 1:2], "rstd")
            S.add("V", lambda e: e.scalar_tensor_tensor(junkz.ap, ytile.ap, 1.0, ytile.ap,
                                                        ALU.mult, ALU.mult, accum_out=ss.ap),
                  reads=(ytile,), writes=(junkz, ss))
            S.add("V", lambda e: e.tensor_scalar(rstd.ap, ss.ap, 1.0 / 4096, LNEPS,
                                                 ALU.mult, ALU.add),
                  reads=(ss,), writes=(rstd,))
            S.add("S", lambda e: e.activation(rstd.ap, rstd.ap, AF.Sqrt),
                  reads=(rstd,), writes=(rstd,))
            S.add("V", lambda e: e.reciprocal(rstd.ap, rstd.ap), reads=(rstd,), writes=(rstd,))
            S.add("V", lambda e: e.tensor_scalar(junkz.ap, ytile.ap, rstd.ap, None, ALU.mult),
                  reads=(ytile,), sreads=(rstd,), writes=(junkz,))
            for i in range(32):
                pp = bufs["pbf"][i % 3]
                S.add("T", lambda e, pp=pp, i=i: e.transpose(
                    pp.ap, junkz.ap[:, i * 128:(i + 1) * 128], cb["ident"].ap),
                    reads=(junkz, cb["ident"]), writes=(pp,))
                dst = ynTc.sub(np.s_[:, i * 128:(i + 1) * 128], f"yn{i}")
                S.add("V", lambda e, pp=pp, dst=dst: e.tensor_copy(dst.ap, pp.ap),
                      reads=(pp,), writes=(ynTc,))
            for oc in range(4):
                wa = wpT_d.ap[:, oc * 512:(oc + 1) * 512].rearrange("(k p) n -> p k n", p=128)
                S.add("P", lambda e, wa=wa: e.dma_start(
                    out=wp.ap.rearrange("p (k n) -> p k n", n=512), in_=wa),
                    reads=(wpT_d,), writes=(wp,), is_dma=True)
                p = bufs["py"][oc % 2]
                mm_acc(S, p, [(ynTc, np.s_[:, kk * 128:(kk + 1) * 128],
                               wp, np.s_[:, kk * 512:(kk + 1) * 512]) for kk in range(32)])
                S.dma("Q", xtile, xin, in_sl=np.s_[c * 128:(c + 1) * 128, oc * 512:(oc + 1) * 512])
                S.add("V", lambda e, p=p: e.tensor_tensor(otile.ap, p.ap, xtile.ap,
                                                          op=ALU.add),
                      reads=(p, xtile), writes=(otile,))
                S.dma("Q", xout, otile, out_sl=np.s_[c * 128:(c + 1) * 128, oc * 512:(oc + 1) * 512])




def build_mmb_post(cpos, debug=False):
    nc = bass.Bass("TRN2", target_bir_lowering=False)
    with ExitStack() as ctx:
        S = Sched(nc, ctx)
        io = _io_mmb_post(S, cpos, debug)
        emit_mmb_post(S, io, cpos, debug)
        S.finalize()
    return nc


def _io_gsa_pre(S):
    ND = DM // 128; NT = TPC // 128
    TT_ = 9 * 128
    io = {}
    io["xin"] = S.dram_in("xin", [TT_, DM], F32)
    io["wq"] = S.dram_in("wq", [DM, DM], F32)
    io["wk"] = S.dram_in("wk", [DM, DM], F32)
    io["wv"] = S.dram_in("wv", [DM, DM], F32)
    io["ww"] = S.dram_in("ww", [DM, DM], F32)
    io["cq"] = S.dram_in("cq", [DM, 4], F32)
    io["ck"] = S.dram_in("ck", [DM, 4], F32)
    io["cv"] = S.dram_in("cv", [DM, 4], F32)
    io["wf1"] = S.dram_in("wf1", [DM, 128], F32)
    io["wf2"] = S.dram_in("wf2", [128, DM], F32)
    io["wb"] = S.dram_in("wb", [DM, 16], F32)
    io["wg1"] = S.dram_in("wg1", [DM, 128], F32)
    io["wg2"] = S.dram_in("wg2", [128, DM], F32)
    io["bg2"] = S.dram_in("bg2", [1, DM], F32)
    io["ident_in"] = S.dram_in("ident_in", [128, 128], BF16)
    io["ones_in"] = S.dram_in("ones_in", [1, 128], BF16)
    io["qfm"] = S.dram_out("qfm", [DM, TPC], BF16)
    io["kfm"] = S.dram_out("kfm", [DM, TPC], BF16)
    io["vfm"] = S.dram_out("vfm", [DM, TPC], BF16)
    io["wfm"] = S.dram_out("wfm", [DM, TPC], BF16)
    io["g"] = S.dram_out("g", [TPC, DM], F32)
    io["beta"] = S.dram_out("beta", [TPC, 16], F32)
    io["sgate"] = S.dram_out("sgate", [TPC, DM], BF16)
    return io


def emit_gsa_pre(S, io):
    NTH = 9
    TT_ = NTH * 128
    ND = DM // 128
    SCALE = float(128.0 ** -0.5)
    with S.scope():
        xin = io["xin"]
        wq_d = io["wq"]
        wk_d = io["wk"]
        wv_d = io["wv"]
        ww_d = io["ww"]
        cq_d = io["cq"]
        ck_d = io["ck"]
        cv_d = io["cv"]
        wf1_d = io["wf1"]
        wf2_d = io["wf2"]
        wb_d = io["wb"]
        wg1_d = io["wg1"]
        wg2_d = io["wg2"]
        bg2_d = io["bg2"]
        identin = io["ident_in"]
        onesin = io["ones_in"]
        q_o = io["qfm"]
        k_o = io["kfm"]
        v_o = io["vfm"]
        w_o = io["wfm"]
        g_o = io["g"]
        beta_o = io["beta"]
        sg_o = io["sgate"]

        ident = S.sbuf("ident", [128, 128], BF16)
        S.dma("Q", ident, identin)
        ones = S.sbuf("ones", [1, 128], BF16)
        S.dma("Q", ones, onesin)
        bg2s = S.sbuf("bg2s", [1, DM], BF16)
        S.dma("P", bg2s, bg2_d)
        xT = S.sbuf("xT", [128, ND * TT_], BF16)
        xtile = S.sbuf("xtile", [128, DM], F32)
        xh = S.sbuf("xh", [128, DM], BF16)
        junk = S.sbuf("junk", [128, DM], BF16)
        stats = S.sbuf("stats", [128, 8], F32)
        wl = S.sbuf("wl", [128, ND * 128], BF16)
        cwcol = S.sbuf("cwcol", [128, 4], F32)
        xbc = S.sbuf("xbc", [128, TT_], BF16)
        cacc = S.sbuf("cacc", [128, TPC], F32)
        convt = S.sbuf("convt", [128, TPC], BF16)
        ffm = S.sbuf("ffm", [128, TPC], BF16)
        g1fm = S.sbuf("g1fm", [128, TPC], BF16)
        w2s = S.sbuf("w2s", [128, DM], BF16)
        sc512 = S.sbuf("sc512", [128, 512], F32)
        ob = S.sbuf("ob", [128, 512], BF16)
        ps = [S.psum(f"ps{i}", [128, 512]) for i in range(4)]
        pt = [S.psum(f"pt{i}", [128, 128], BF16) for i in range(4)]

        for j in range(NTH):
            S.dma("Q", xtile, xin, in_sl=np.s_[j * 128:(j + 1) * 128, :])
            emit_ln_tile(S, xtile, xh, stats, junk)
            for i in range(ND):
                p = pt[i % 4]
                mm_transpose(S, p, xh, ident, np.s_[:, i * 128:(i + 1) * 128])
                dst = Buf(xT.ap[:, i * TT_ + j * 128: i * TT_ + (j + 1) * 128], "xTc")
                S.add("V", lambda e, p=p, dst=dst: e.tensor_copy(dst.ap, p.ap),
                      reads=(p,), writes=(xT,))

        for wsrc, csrc, dsto, scaled in ((wq_d, cq_d, q_o, True), (wk_d, ck_d, k_o, False),
                                         (wv_d, cv_d, v_o, False), (ww_d, cv_d, w_o, False)):
            for r in range(16):
                wa = wsrc.ap[:, r * 128:(r + 1) * 128].rearrange("(c p) n -> p c n", p=128)
                S.add("P", lambda e, wa=wa: e.dma_start(
                    out=wl.ap.rearrange("p (c n) -> p c n", n=128), in_=wa),
                    reads=(wsrc,), writes=(wl,), is_dma=True)
                for tc, t0, tw in ((0, 0, 512), (1, 512, 512), (2, 1024, 128)):
                    p = ps[tc % 2]
                    mm_acc(S, p, [(wl, np.s_[:, i * 128:(i + 1) * 128],
                                   xT, np.s_[:, i * TT_ + t0: i * TT_ + t0 + tw])
                                  for i in range(ND)], width=tw)
                    S.add("V", lambda e, p=p, t0=t0, tw=tw: e.tensor_copy(
                        xbc.ap[:, t0:t0 + tw], p.ap[:, :tw]),
                        reads=(p,), writes=(xbc,))
                S.dma("Q", cwcol, csrc, in_sl=np.s_[r * 128:(r + 1) * 128, :])
                S.add("V", lambda e: e.tensor_scalar(
                    cacc.ap, xbc.ap[:, 125:125 + TPC], cwcol.ap[:, 0:1], None, ALU.mult),
                    reads=(xbc,), sreads=(cwcol,), writes=(cacc,))
                for i in range(1, 4):
                    S.add("V", lambda e, i=i: e.scalar_tensor_tensor(
                        cacc.ap, xbc.ap[:, 125 + i:125 + i + TPC], cwcol.ap[:, i:i + 1],
                        cacc.ap, ALU.mult, ALU.add),
                        reads=(xbc, cacc), sreads=(cwcol,), writes=(cacc,))
                S.add("S", lambda e: e.activation(convt.ap, cacc.ap, AF.Silu),
                      reads=(cacc,), writes=(convt,))
                if scaled:
                    S.add("V", lambda e: e.tensor_scalar(convt.ap, convt.ap, SCALE, None,
                                                         ALU.mult),
                          reads=(convt,), writes=(convt,))
                S.dma("Q", dsto, convt, out_sl=np.s_[r * 128:(r + 1) * 128, :])

        for fm, wsrc in ((ffm, wf1_d), (g1fm, wg1_d)):
            wa = wsrc.ap.rearrange("(c p) n -> p c n", p=128)
            S.add("P", lambda e, wa=wa: e.dma_start(
                out=wl.ap.rearrange("p (c n) -> p c n", n=128), in_=wa),
                reads=(wsrc,), writes=(wl,), is_dma=True)
            for tc in range(2):
                p = ps[tc % 2]
                mm_acc(S, p, [(wl, np.s_[:, i * 128:(i + 1) * 128],
                               xT, np.s_[:, i * TT_ + 128 + tc * 512: i * TT_ + 128 + (tc + 1) * 512])
                              for i in range(ND)])
                S.add("V", lambda e, p=p, tc=tc, fm=fm: e.tensor_copy(
                    fm.ap[:, tc * 512:(tc + 1) * 512], p.ap),
                    reads=(p,), writes=(fm,))
        for which, fm, w2src, outd in ((0, ffm, wf2_d, g_o), (1, g1fm, wg2_d, sg_o)):
            S.dma("P", w2s, w2src)
            for j in range(8):
                for hc in range(4):
                    p = ps[2 + hc % 2]
                    terms = [(fm, np.s_[:, j * 128:(j + 1) * 128],
                              w2s, np.s_[:, hc * 512:(hc + 1) * 512])]
                    if which == 1:
                        terms.append((ones, None, bg2s, np.s_[:, hc * 512:(hc + 1) * 512]))
                    mm_acc(S, p, terms)
                    if which == 0:
                        S.add("S", lambda e, p=p: e.activation(sc512.ap, p.ap, AF.Abs),
                              reads=(p,), writes=(sc512,))
                        S.add("S", lambda e: e.activation(sc512.ap, sc512.ap, AF.Exp, scale=-1.0),
                              reads=(sc512,), writes=(sc512,))
                        S.add("S", lambda e: e.activation(sc512.ap, sc512.ap, AF.Ln, bias=1.0),
                              reads=(sc512,), writes=(sc512,))
                        S.add("V", lambda e, p=p: e.tensor_scalar(
                            cacc.ap[:, :512], p.ap, 0.0, None, ALU.min),
                            reads=(p,), writes=(cacc,))
                        S.add("V", lambda e: e.tensor_tensor(
                            cacc.ap[:, :512], cacc.ap[:, :512], sc512.ap, op=ALU.subtract),
                            reads=(cacc, sc512), writes=(cacc,))
                        S.dma("Q", outd, cacc, in_sl=np.s_[:, :512],
                              out_sl=np.s_[j * 128:(j + 1) * 128, hc * 512:(hc + 1) * 512])
                    else:
                        S.add("S", lambda e, p=p: e.activation(ob.ap, p.ap, AF.Sigmoid),
                              reads=(p,), writes=(ob,))
                        S.dma("Q", outd, ob,
                              out_sl=np.s_[j * 128:(j + 1) * 128, hc * 512:(hc + 1) * 512])
        wa = wb_d.ap.rearrange("(c p) n -> p c n", p=128)
        S.add("P", lambda e, wa=wa: e.dma_start(
            out=wl.ap[:, :ND * 16].rearrange("p (c n) -> p c n", n=16), in_=wa),
            reads=(wb_d,), writes=(wl,), is_dma=True)
        for j in range(8):
            p = ps[j % 2]
            mm_acc(S, p, [(xT, np.s_[:, i * TT_ + 128 + j * 128: i * TT_ + 128 + (j + 1) * 128],
                           wl, np.s_[:, i * 16:(i + 1) * 16]) for i in range(ND)], width=16)
            S.add("S", lambda e, p=p: e.activation(sc512.ap[:, :16], p.ap[:, :16], AF.Sigmoid),
                  reads=(p,), writes=(sc512,))
            S.dma("Q", beta_o, sc512, in_sl=np.s_[:, :16], out_sl=np.s_[j * 128:(j + 1) * 128, :])




def build_gsa_pre():
    nc = bass.Bass("TRN2", target_bir_lowering=False)
    with ExitStack() as ctx:
        S = Sched(nc, ctx)
        io = _io_gsa_pre(S)
        emit_gsa_pre(S, io)
        S.finalize()
    return nc


def _io_gsa_scan(S, cpos, with_out, debug=False):
    ND = DM // 128; NT = TPC // 128
    GH, GM = 16, 128
    io = {}
    io["qfm"] = S.dram_in("qfm", [DM, TPC], BF16)
    io["kfm"] = S.dram_in("kfm", [DM, TPC], BF16)
    io["vfm"] = S.dram_in("vfm", [DM, TPC], BF16)
    io["wfm"] = S.dram_in("wfm", [DM, TPC], BF16)
    io["g"] = S.dram_in("g", [TPC, DM], F32)
    io["beta"] = S.dram_in("beta", [TPC, 16], F32)
    io["ident_in"] = S.dram_in("ident_in", [128, 128], BF16)
    io["identf_in"] = S.dram_in("identf_in", [128, 128], F32)
    io["onesf_in"] = S.dram_in("onesf_in", [1, 128], F32)
    io["tri64_in"] = S.dram_in("tri64_in", [64, 64], BF16)
    io["ctri_in"] = S.dram_in("ctri_in", [64, 64], F32)
    io["xin"] = S.dram_in("xin", [TPC, DM], F32)
    io["sgate"] = S.dram_in("sgate", [TPC, DM], BF16)
    io["sklocs"] = S.dram_in("sklocs", [4 * GH * 128, GM], F32)
    io["svlocs"] = S.dram_in("svlocs", [4 * GH * 128, GM], F32)
    io["dlocx"] = S.dram_in("dlocx", [1, 4 * DM], F32)
    io["wo"] = S.dram_in("wo", [DM, DM], F32)
    io["normw_b"] = S.dram_in("normw_b", [128, DM], F32)
    io["xout"] = S.dram_out("xout", [TPC, DM], F32)
    io["skloc"] = S.dram_out("skloc", [GH * 128, GM], F32)
    io["svloc"] = S.dram_out("svloc", [GH * 128, GM], F32)
    io["dloc"] = S.dram_out("dloc", [1, DM], F32)
    io["dbg_o"] = S.dram_out("dbg_o", [64, DM], F32)
    io["dbg_U"] = S.dram_out("dbg_U", [64, DM], F32)
    io["dbg_sst"] = S.dram_out("dbg_sst", [64, 128], F32)
    io["dbg_a"] = S.dram_out("dbg_a", [64, 128], F32)
    io["dbg_G"] = S.dram_out("dbg_G", [64, DM], F32)
    return io


def emit_gsa_scan(S, io, cpos, with_out, debug=False):
    GH, GM, LC, NC_ = 16, 128, 64, 16
    ND = DM // 128
    with S.scope():
        qfm_d = io["qfm"]
        kfm_d = io["kfm"]
        vfm_d = io["vfm"]
        wfm_d = io["wfm"]
        g_d = io["g"]
        beta_d = io["beta"]
        identin = io["ident_in"]
        identfin = io["identf_in"]
        onesfin = io["onesf_in"]
        tri64in = io["tri64_in"]
        ctriin = io["ctri_in"]
        if with_out:
            xin = io["xin"]
            sgate_d = io["sgate"]
            sklocs_d = io["sklocs"]
            svlocs_d = io["svlocs"]
            dlocx_d = io["dlocx"]
            wo_d = io["wo"]
            normw_b = io["normw_b"]
            xout = io["xout"]
        else:
            skloc_o = io["skloc"]
            svloc_o = io["svloc"]
            dloc_o = io["dloc"]

        ident = S.sbuf("ident", [128, 128], BF16)
        S.dma("Q", ident, identin)
        identf = S.sbuf("identf", [128, 128], F32)
        S.dma("Q", identf, identfin)
        onesf = S.sbuf("onesf", [1, 128], F32)
        S.dma("Q", onesf, onesfin)
        cb_onesf = onesf
        tri64 = S.sbuf("tri64", [64, 64], BF16)
        S.dma("Q", tri64, tri64in)
        ctri = S.sbuf("ctri", [64, 64], F32)
        S.dma("Q", ctri, ctriin)

        qfc = S.sbuf("qfc", [128, GH * LC], BF16)
        kfc = S.sbuf("kfc", [128, GH * LC], BF16)
        vfc = S.sbuf("vfc", [128, GH * LC], BF16)
        wfc = S.sbuf("wfc", [128, GH * LC], BF16)
        gsl = S.sbuf("gsl", [64, DM], F32)
        Gt = S.sbuf("Gt", [64, DM], F32)
        eG = S.sbuf("eG", [64, DM], F32)
        emG = S.sbuf("emG", [64, DM], BF16)
        bsl_ = S.sbuf("bsl", [64, 16], F32)
        rb = S.sbuf("rb", [64, 16], F32)
        wtm = S.sbuf("wtm", [64, DM], F32)
        U = S.sbuf("U", [64, DM], BF16)
        ktm = S.sbuf("ktm", [64, 128], BF16)
        vtm = S.sbuf("vtm", [64, 128], BF16)
        UF = S.sbuf("UF", [128, 64], BF16)
        Am = S.sbuf("Am", [64, 64], BF16)
        P2m = S.sbuf("P2m", [64, 64], BF16)
        sst = S.sbuf("sst", [64, 128], F32)
        Pa = S.sbuf("Pa", [64, 128], F32)
        abuf = S.sbuf("abuf", [64, 128], BF16)
        aT = S.sbuf("aT", [128, 64], BF16)
        nm1 = S.sbuf("nm1", [64, 4], F32)
        Sk = S.sbuf("Sk", [128, GH * GM], F32)
        Sv = S.sbuf("Sv", [128, GH * GM], F32)
        Skbf = S.sbuf("Skbf", [128, GH * GM], BF16)
        Svbf = S.sbuf("Svbf", [128, GH * GM], BF16)
        gl_d2 = S.dram_tmp("gl_d2", [DM], F32)
        glrow = S.sbuf("glrow", [1, DM], F32)
        glsum = S.sbuf("glsum", [1, DM], F32)
        eglb = S.sbuf("eglb", [128, 128], F32)
        eglc = S.sbuf("eglc", [128, 1], F32)
        tmpS = S.sbuf("tmpS", [128, 128], F32)
        pf = [S.psum(f"pf{i}", [128, 128]) for i in range(3)]
        pb = [S.psum(f"pb{i}", [128, 128], BF16) for i in range(3)]
        pw = [S.psum(f"pw{i}", [128, 512]) for i in range(2)]
        if with_out:
            otm = S.sbuf("otm", [64, DM], F32)
            oT = S.sbuf("oT", [128, ND * TPC], BF16)
            sgt = S.sbuf("sgt", [64, DM], BF16)
            normwB = S.sbuf("normwB", [128, DM], F32)
            S.dma("Q", normwB, normw_b)
            sltmp = S.sbuf("sltmp", [128, 128], F32)
            dlr = S.sbuf("dlr", [1, 128], F32)
            wos = S.sbuf("wos", [128, ND * 512], BF16)
            xtile = S.sbuf("xtile", [128, 512], F32)
            otile = S.sbuf("otile", [128, 512], F32)
            ono = S.sbuf("ono", [64, 16], F32)

        if debug:
            dbg_o = io["dbg_o"]
            dbg_U = io["dbg_U"]
            dbg_sst = io["dbg_sst"]
            dbg_a = io["dbg_a"]
            dbg_G = io["dbg_G"]
        S.add("V", lambda e: e.memset(Sk.ap, 0.0), writes=(Sk,))
        S.add("V", lambda e: e.memset(Sv.ap, 0.0), writes=(Sv,))
        if with_out and cpos is None:
            cmbs = S.sbuf("cmbs", [128, 16], F32)
            S.dma("Q", cmbs, io["cmb"])
            dl4 = io["dloc4"]
            dlsb = S.sbuf("dlsb", [1, 4 * DM], F32)
            dla = dl4.ap.rearrange("a b -> (a b)").rearrange("(a b) -> a b", a=1)
            S.add("Q", lambda e, dla=dla: e.dma_start(out=dlsb.ap, in_=dla),
                  reads=(dl4,), writes=(dlsb,), is_dma=True)
            mdB = S.sbuf("mdB", [128, 128], F32)
            mdc = S.sbuf("mdc", [128, 1], F32)
            for i in range(3):
                for h in range(GH):
                    for which, st, locd in ((0, Sk, sklocs_d), (1, Sv, svlocs_d)):
                        S.dma("Q", sltmp, locd,
                              in_sl=np.s_[i * GH * 128 + h * 128: i * GH * 128 + (h + 1) * 128, :])
                        msl = np.s_[:, h * GM:(h + 1) * GM]
                        p = pf[0]
                        if which == 0:
                            S.add("T", lambda e, p=p, i=i, h=h: e.matmul(
                                p.ap, cb_onesf.ap, dlsb.ap[:, i * DM + h * GM: i * DM + (h + 1) * GM],
                                start=True, stop=True),
                                reads=(cb_onesf, dlsb), writes=(p,))
                            S.add("V", lambda e, p=p, i=i: e.tensor_scalar(
                                mdB.ap, p.ap, cmbs.ap[:, 3 * i:3 * i + 1],
                                cmbs.ap[:, 3 * i + 1:3 * i + 2], ALU.mult, ALU.add),
                                reads=(p,), sreads=(cmbs,), writes=(mdB,))
                            S.add("V", lambda e, msl=msl, st=st: e.tensor_tensor(
                                st.ap[msl], st.ap[msl], mdB.ap, op=ALU.mult),
                                reads=(st, mdB), writes=(st,))
                        else:
                            S.add("T", lambda e, p=p, i=i, h=h: e.matmul(
                                p.ap[:, 0:1],
                                dlsb.ap[:, i * DM + h * GM: i * DM + (h + 1) * GM],
                                cb_onesf.ap[:, 0:1], start=True, stop=True),
                                reads=(dlsb, cb_onesf), writes=(p,))
                            S.add("V", lambda e, p=p, i=i: e.tensor_scalar(
                                mdc.ap, p.ap[:, 0:1], cmbs.ap[:, 3 * i:3 * i + 1],
                                cmbs.ap[:, 3 * i + 1:3 * i + 2], ALU.mult, ALU.add),
                                reads=(p,), sreads=(cmbs,), writes=(mdc,))
                            S.add("V", lambda e, msl=msl, st=st: e.tensor_scalar(
                                st.ap[msl], st.ap[msl], mdc.ap, None, ALU.mult),
                                reads=(st,), sreads=(mdc,), writes=(st,))
                        S.add("V", lambda e, i=i: e.tensor_scalar(
                            sltmp.ap, sltmp.ap, cmbs.ap[:, 3 * i + 2:3 * i + 3], None,
                            ALU.mult),
                            reads=(sltmp,), sreads=(cmbs,), writes=(sltmp,))
                        S.add("V", lambda e, msl=msl, st=st: e.tensor_tensor(
                            st.ap[msl], st.ap[msl], sltmp.ap, op=ALU.add),
                            reads=(st, sltmp), writes=(st,))
        elif with_out:
            for i in range(cpos):
                for h in range(GH):
                    S.dma("Q", dlr, dlocx_d,
                          in_sl=np.s_[0:1, i * DM + h * GM: i * DM + (h + 1) * GM])
                    for which, st, locd in ((0, Sk, sklocs_d), (1, Sv, svlocs_d)):
                        S.dma("Q", sltmp, locd,
                              in_sl=np.s_[i * GH * 128 + h * 128: i * GH * 128 + (h + 1) * 128, :])
                        msl = np.s_[:, h * GM:(h + 1) * GM]
                        p = pf[0]
                        if which == 0:
                            S.add("T", lambda e, p=p: e.matmul(p.ap, onesf.ap, dlr.ap,
                                                               start=True, stop=True),
                                  reads=(onesf, dlr), writes=(p,))
                            S.add("V", lambda e, p=p, msl=msl, st=st: e.tensor_tensor(
                                st.ap[msl], st.ap[msl], p.ap, op=ALU.mult),
                                reads=(st, p), writes=(st,))
                        else:
                            S.add("T", lambda e, p=p: e.matmul(p.ap[:, 0:1], dlr.ap,
                                                               onesf.ap[:, 0:1],
                                                               start=True, stop=True),
                                  reads=(dlr, onesf), writes=(p,))
                            S.add("V", lambda e, p=p: e.tensor_copy(eglc.ap, p.ap[:, 0:1]),
                                  reads=(p,), writes=(eglc,))
                            S.add("V", lambda e, msl=msl, st=st: e.tensor_scalar(
                                st.ap[msl], st.ap[msl], eglc.ap, None, ALU.mult),
                                reads=(st,), sreads=(eglc,), writes=(st,))
                        S.add("V", lambda e, msl=msl, st=st: e.tensor_tensor(
                            st.ap[msl], st.ap[msl], sltmp.ap, op=ALU.add),
                            reads=(st, sltmp), writes=(st,))
        S.add("V", lambda e: e.tensor_copy(Skbf.ap, Sk.ap), reads=(Sk,), writes=(Skbf,))
        S.add("V", lambda e: e.tensor_copy(Svbf.ap, Sv.ap), reads=(Sv,), writes=(Svbf,))
        if not with_out:
            S.add("V", lambda e: e.memset(glsum.ap, 0.0), writes=(glsum,))

        for c in range(NC_):
            t0 = c * LC
            for src, dst in ((qfm_d, qfc), (kfm_d, kfc), (vfm_d, vfc), (wfm_d, wfc)):
                if src is qfm_d and not with_out:
                    continue
                sa = src.ap[:, t0:t0 + LC].rearrange("(h p) n -> p h n", p=128)
                S.add("Q", lambda e, sa=sa, dst=dst: e.dma_start(
                    out=dst.ap.rearrange("p (h n) -> p h n", n=LC), in_=sa),
                    reads=(src,), writes=(dst,), is_dma=True)
            S.dma("Q", gsl, g_d, in_sl=np.s_[t0:t0 + LC, :])
            S.dma("Q", bsl_, beta_d, in_sl=np.s_[t0:t0 + LC, :])
            for hc in range(4):
                p = pw[hc % 2]
                S.add("T", lambda e, p=p, hc=hc: e.matmul(
                    p.ap[:64, :], ctri.ap, gsl.ap[:, hc * 512:(hc + 1) * 512],
                    start=True, stop=True),
                    reads=(ctri, gsl), writes=(p,))
                S.add("V", lambda e, p=p, hc=hc: e.tensor_copy(
                    Gt.ap[:, hc * 512:(hc + 1) * 512], p.ap[:64, :]),
                    reads=(p,), writes=(Gt,))
            S.add("S", lambda e: e.activation(eG.ap, Gt.ap, AF.Exp), reads=(Gt,), writes=(eG,))
            S.add("S", lambda e: e.activation(emG.ap, Gt.ap, AF.Exp, scale=-1.0),
                  reads=(Gt,), writes=(emG,))
            gla = gl_d2.ap.rearrange("(a b) -> a b", a=1)
            S.add("Q", lambda e, gla=gla: e.dma_start(out=gla, in_=Gt.ap[63:64, :]),
                  reads=(Gt,), writes=(gl_d2,), is_dma=True)
            S.add("Q", lambda e, gla=gla: e.dma_start(out=glrow.ap, in_=gla),
                  reads=(gl_d2,), writes=(glrow,), is_dma=True)
            if not with_out:
                S.add("V", lambda e: e.tensor_tensor(glsum.ap, glsum.ap, glrow.ap, op=ALU.add),
                      reads=(glsum, glrow), writes=(glsum,))
            for h in range(GH):
                pp = pb[h % 3]
                S.add("T", lambda e, pp=pp, h=h: e.transpose(
                    pp.ap[:64, :], wfc.ap[:, h * LC:(h + 1) * LC], ident.ap),
                    reads=(wfc, ident), writes=(pp,))
                S.add("V", lambda e, pp=pp, h=h: e.tensor_copy(
                    wtm.ap[:, h * GM:(h + 1) * GM], pp.ap[:64, :128]),
                    reads=(pp,), writes=(wtm,))
            S.add("S", lambda e: e.activation(U.ap, wtm.ap, AF.Square),
                  reads=(wtm,), writes=(U,))
            u3 = U.ap.rearrange("p (h m) -> p h m", m=GM)
            S.add("V", lambda e, u3=u3: e.tensor_reduce(rb.ap, u3, axis=AX.X, op=ALU.add),
                  reads=(U,), writes=(rb,))
            S.add("V", lambda e: e.tensor_scalar(rb.ap, rb.ap, 1.0, 1e-6, ALU.mult, ALU.add),
                  reads=(rb,), writes=(rb,))
            S.add("S", lambda e: e.activation(rb.ap, rb.ap, AF.Sqrt), reads=(rb,), writes=(rb,))
            S.add("V", lambda e: e.reciprocal(rb.ap, rb.ap), reads=(rb,), writes=(rb,))
            S.add("V", lambda e: e.tensor_tensor(rb.ap, rb.ap, bsl_.ap, op=ALU.mult),
                  reads=(rb, bsl_), writes=(rb,))
            for h in range(GH):
                msl = np.s_[:, h * GM:(h + 1) * GM]
                S.add("V", lambda e, msl=msl, h=h: e.tensor_scalar(
                    U.ap[msl], wtm.ap[msl], rb.ap[:, h:h + 1], None, ALU.mult),
                    reads=(wtm,), sreads=(rb,), writes=(U,))
            S.add("V", lambda e: e.tensor_tensor(U.ap, U.ap, emG.ap, op=ALU.mult),
                  reads=(U, emG), writes=(U,))
            for h in range(GH):
                hsl = np.s_[:, h * LC:(h + 1) * LC]
                msl = np.s_[:, h * GM:(h + 1) * GM]
                S.add("T", lambda e, hsl=hsl: e.transpose(pb[0].ap[:64, :], kfc.ap[hsl], ident.ap),
                      reads=(kfc, ident), writes=(pb[0],))
                S.add("V", lambda e: e.tensor_copy(ktm.ap, pb[0].ap[:64, :128]),
                      reads=(pb[0],), writes=(ktm,))
                S.add("T", lambda e, hsl=hsl: e.transpose(pb[1].ap[:64, :], vfc.ap[hsl], ident.ap),
                      reads=(vfc, ident), writes=(pb[1],))
                S.add("V", lambda e: e.tensor_copy(vtm.ap, pb[1].ap[:64, :128]),
                      reads=(pb[1],), writes=(vtm,))
                S.add("T", lambda e, h=h: e.transpose(
                    pb[2].ap[:, :64], U.ap[:64, h * GM:(h + 1) * GM], ident.ap[:64, :64]),
                    reads=(U, ident), writes=(pb[2],))
                S.add("V", lambda e: e.tensor_copy(UF.ap, pb[2].ap[:, :64]),
                      reads=(pb[2],), writes=(UF,))
                if with_out:
                    S.add("T", lambda e, hsl=hsl: e.matmul(pf[0].ap[:64, :64], kfc.ap[hsl],
                                                           qfc.ap[hsl], start=True, stop=True),
                          reads=(kfc, qfc), writes=(pf[0],))
                    S.add("V", lambda e: e.tensor_tensor(Am.ap, pf[0].ap[:64, :64], tri64.ap,
                                                         op=ALU.mult),
                          reads=(pf[0], tri64), writes=(Am,))
                    p = pf[1]
                    S.add("T", lambda e, p=p, h=h: e.matmul(p.ap[:64, :], Am.ap,
                                                            U.ap[:64, h * GM:(h + 1) * GM],
                                                            start=True, stop=False),
                          reads=(Am, U), writes=(p,))
                    S.add("T", lambda e, p=p, hsl=hsl, msl=msl: e.matmul(
                        p.ap[:64, :], qfc.ap[hsl], Skbf.ap[msl], start=False, stop=True),
                        reads=(qfc, Skbf), writes=(p,))
                    S.add("V", lambda e, p=p, h=h: e.tensor_tensor(
                        sst.ap, p.ap[:64, :], eG.ap[:, h * GM:(h + 1) * GM], op=ALU.mult),
                        reads=(p, eG), writes=(sst,))
                    S.add("V", lambda e: e.reduce_max(nm1.ap[:, 0:1], sst.ap, axis=AX.X,
                                                      negate=True),
                          reads=(sst,), writes=(nm1,))
                    S.add("S", lambda e: e.activation(Pa.ap, sst.ap, AF.Exp,
                                                      bias=nm1.ap[:, 0:1],
                                                      accum_out=nm1.ap[:, 1:2]),
                          reads=(sst,), sreads=(nm1,), writes=(Pa, nm1))
                    S.add("V", lambda e: e.reciprocal(nm1.ap[:, 2:3], nm1.ap[:, 1:2]),
                          reads=(nm1,), writes=(nm1,))
                    S.add("V", lambda e: e.tensor_scalar(Pa.ap, Pa.ap, nm1.ap[:, 2:3], None,
                                                         ALU.mult),
                          reads=(Pa,), sreads=(nm1,), writes=(Pa,))
                    S.add("V", lambda e, h=h: e.tensor_tensor(
                        abuf.ap, Pa.ap, eG.ap[:, h * GM:(h + 1) * GM], op=ALU.mult),
                        reads=(Pa, eG), writes=(abuf,))
                    S.add("T", lambda e: e.transpose(pb[0].ap[:, :64], abuf.ap,
                                                     ident.ap[:64, :64]),
                          reads=(abuf, ident), writes=(pb[0],))
                    S.add("V", lambda e: e.tensor_copy(aT.ap, pb[0].ap[:, :64]),
                          reads=(pb[0],), writes=(aT,))
                    S.add("T", lambda e: e.matmul(pf[0].ap[:64, :64], UF.ap, aT.ap,
                                                  start=True, stop=True),
                          reads=(UF, aT), writes=(pf[0],))
                    S.add("V", lambda e: e.tensor_tensor(P2m.ap, pf[0].ap[:64, :64], tri64.ap,
                                                         op=ALU.mult),
                          reads=(pf[0], tri64), writes=(P2m,))
                    p = pf[2]
                    S.add("T", lambda e, p=p, msl=msl: e.matmul(p.ap[:64, :], aT.ap,
                                                                Svbf.ap[msl],
                                                                start=True, stop=False),
                          reads=(aT, Svbf), writes=(p,))
                    S.add("T", lambda e, p=p: e.matmul(p.ap[:64, :], P2m.ap, vtm.ap,
                                                       start=False, stop=True),
                          reads=(P2m, vtm), writes=(p,))
                    S.add("V", lambda e, p=p, h=h: e.tensor_copy(
                        otm.ap[:, h * GM:(h + 1) * GM], p.ap[:64, :]),
                        reads=(p,), writes=(otm,))
                p = pf[0]
                S.add("T", lambda e, p=p, h=h: e.matmul(
                    p.ap, ktm.ap, U.ap[:64, h * GM:(h + 1) * GM], start=True, stop=True),
                    reads=(ktm, U), writes=(p,))
                p2 = pf[1]
                S.add("T", lambda e, p2=p2, h=h: e.matmul(
                    p2.ap, onesf.ap, glrow.ap[:, h * GM:(h + 1) * GM], start=True, stop=True),
                    reads=(onesf, glrow), writes=(p2,))
                S.add("S", lambda e, p2=p2: e.activation(eglb.ap, p2.ap, AF.Exp),
                      reads=(p2,), writes=(eglb,))
                p3 = pf[2]
                S.add("T", lambda e, p3=p3, h=h: e.matmul(
                    p3.ap[:, 0:1], glrow.ap[:, h * GM:(h + 1) * GM], onesf.ap[:, 0:1],
                    start=True, stop=True),
                    reads=(glrow, onesf), writes=(p3,))
                S.add("S", lambda e, p3=p3: e.activation(eglc.ap, p3.ap[:, 0:1], AF.Exp),
                      reads=(p3,), writes=(eglc,))
                S.add("V", lambda e, p=p, msl=msl: e.tensor_tensor(tmpS.ap, Sk.ap[msl], p.ap,
                                                                   op=ALU.add),
                      reads=(Sk, p), writes=(tmpS,))
                S.add("V", lambda e, msl=msl: e.tensor_tensor(Sk.ap[msl], tmpS.ap, eglb.ap,
                                                              op=ALU.mult),
                      reads=(tmpS, eglb), writes=(Sk,))
                p4 = pf[0]
                S.add("T", lambda e, p4=p4, h=h: e.matmul(
                    p4.ap, U.ap[:64, h * GM:(h + 1) * GM], vtm.ap, start=True, stop=True),
                    reads=(U, vtm), writes=(p4,))
                S.add("V", lambda e, p4=p4, msl=msl: e.tensor_tensor(tmpS.ap, Sv.ap[msl], p4.ap,
                                                                     op=ALU.add),
                      reads=(Sv, p4), writes=(tmpS,))
                S.add("V", lambda e, msl=msl: e.tensor_scalar(Sv.ap[msl], tmpS.ap, eglc.ap,
                                                              None, ALU.mult),
                      reads=(tmpS,), sreads=(eglc,), writes=(Sv,))
                S.add("V", lambda e, msl=msl: e.tensor_copy(Skbf.ap[msl], Sk.ap[msl]),
                      reads=(Sk,), writes=(Skbf,))
                S.add("V", lambda e, msl=msl: e.tensor_copy(Svbf.ap[msl], Sv.ap[msl]),
                      reads=(Sv,), writes=(Svbf,))
            if debug and c == 0:
                S.dma("P", dbg_o, otm)
                S.dma("P", dbg_U, U)
                S.dma("P", dbg_sst, sst)
                S.dma("P", dbg_a, abuf)
                S.dma("P", dbg_G, Gt)
            if with_out:
                S.add("S", lambda e: e.activation(U.ap, otm.ap, AF.Square),
                      reads=(otm,), writes=(U,))
                u3b = U.ap.rearrange("p (h m) -> p h m", m=GM)
                S.add("V", lambda e, u3b=u3b: e.tensor_reduce(ono.ap, u3b, axis=AX.X, op=ALU.add),
                      reads=(U,), writes=(ono,))
                S.add("V", lambda e: e.tensor_scalar(ono.ap, ono.ap, 1.0 / GM, LNEPS,
                                                     ALU.mult, ALU.add),
                      reads=(ono,), writes=(ono,))
                S.add("S", lambda e: e.activation(ono.ap, ono.ap, AF.Sqrt),
                      reads=(ono,), writes=(ono,))
                S.add("V", lambda e: e.reciprocal(ono.ap, ono.ap), reads=(ono,), writes=(ono,))
                for h in range(GH):
                    msl = np.s_[:, h * GM:(h + 1) * GM]
                    S.add("V", lambda e, msl=msl, h=h: e.tensor_scalar(
                        otm.ap[msl], otm.ap[msl], ono.ap[:, h:h + 1], None, ALU.mult),
                        reads=(otm,), sreads=(ono,), writes=(otm,))
                S.add("V", lambda e: e.tensor_tensor(otm.ap, otm.ap, normwB.ap[:64, :],
                                                     op=ALU.mult),
                      reads=(otm, normwB), writes=(otm,))
                S.dma("Q", sgt, sgate_d, in_sl=np.s_[t0:t0 + LC, :])
                S.add("V", lambda e: e.tensor_tensor(otm.ap, otm.ap, sgt.ap, op=ALU.mult),
                      reads=(otm, sgt), writes=(otm,))
                for i in range(ND):
                    pp = pf[i % 3]
                    S.add("T", lambda e, pp=pp, i=i: e.transpose(
                        pp.ap[:, :64], otm.ap[:, i * 128:(i + 1) * 128], identf.ap[:64, :64]),
                        reads=(otm, identf), writes=(pp,))
                    dst = Buf(oT.ap[:, i * TPC + t0: i * TPC + t0 + LC], "oTc")
                    S.add("V", lambda e, pp=pp, dst=dst: e.tensor_copy(dst.ap, pp.ap[:, :64]),
                          reads=(pp,), writes=(oT,))
        if not with_out:
            S.add("S", lambda e: e.activation(glsum.ap, glsum.ap, AF.Exp),
                  reads=(glsum,), writes=(glsum,))
            S.dma("Q", dloc_o, glsum)
            for h in range(GH):
                S.dma("Q", skloc_o, Sk, in_sl=np.s_[:, h * GM:(h + 1) * GM],
                      out_sl=np.s_[h * 128:(h + 1) * 128, :])
                S.dma("Q", svloc_o, Sv, in_sl=np.s_[:, h * GM:(h + 1) * GM],
                      out_sl=np.s_[h * 128:(h + 1) * 128, :])
        else:
            for hc in range(4):
                wa = wo_d.ap[:, hc * 512:(hc + 1) * 512].rearrange("(c p) n -> p c n", p=128)
                S.add("P", lambda e, wa=wa: e.dma_start(
                    out=wos.ap.rearrange("p (c n) -> p c n", n=512), in_=wa),
                    reads=(wo_d,), writes=(wos,), is_dma=True)
                for j in range(8):
                    p = pw[j % 2]
                    mm_acc(S, p, [(oT, np.s_[:, i * TPC + j * 128:i * TPC + (j + 1) * 128],
                                   wos, np.s_[:, i * 512:(i + 1) * 512]) for i in range(ND)])
                    S.dma("Q", xtile, xin,
                          in_sl=np.s_[j * 128:(j + 1) * 128, hc * 512:(hc + 1) * 512])
                    S.add("V", lambda e, p=p: e.tensor_tensor(otile.ap, p.ap, xtile.ap,
                                                              op=ALU.add),
                          reads=(p, xtile), writes=(otile,))
                    S.dma("Q", xout, otile,
                          out_sl=np.s_[j * 128:(j + 1) * 128, hc * 512:(hc + 1) * 512])


def build_gsa_scan(cpos, with_out, debug=False):
    nc = bass.Bass("TRN2", target_bir_lowering=False)
    with ExitStack() as ctx:
        S = Sched(nc, ctx)
        io = _io_gsa_scan(S, cpos, with_out, debug)
        emit_gsa_scan(S, io, cpos, with_out, debug)
        S.finalize()
    return nc




SHARED_IN = {"ident_in", "identf_in", "ones_in", "onesf_in", "ones2_in", "trif_in",
             "nmask_in", "tri64_in", "ctri_in", "cm4_in", "alog_b", "d_b", "dtb_b"}


def _wire(S, pfx, overrides):
    def mk_in(n, shape, dt):
        if n in overrides:
            return overrides[n]
        if n in SHARED_IN:
            return S.dram_in(n, shape, dt)
        return S.dram_in(pfx + n, shape, dt)

    def mk_out(n, shape, dt):
        if n in overrides:
            return overrides[n]
        return Buf(S.nc.dram_tensor(S.uname(pfx + n), list(shape), dt,
                                    kind="Internal").ap(), pfx + n)
    return mk_in, mk_out


def emit_halo(S, x_buf, xh_buf, hsel_buf, ag4):
    with S.scope():
        S.dma("Q", xh_buf, x_buf, out_sl=np.s_[128:1152, :])
        t = S.sbuf("halo_t", [128, DM], F32)
        acc = S.sbuf("halo_acc", [128, DM], F32)
        S.add("V", lambda e: e.memset(acc.ap, 0.0), writes=(acc,))
        for r in range(4):
            S.dma("Q", t, ag4, in_sl=np.s_[r * 128:(r + 1) * 128, :])
            S.add("V", lambda e, r=r: e.scalar_tensor_tensor(
                acc.ap, t.ap, hsel_buf.ap[:, r:r + 1], acc.ap, ALU.mult, ALU.add),
                reads=(t, acc), sreads=(hsel_buf,), writes=(acc,))
        S.dma("Q", xh_buf, acc, out_sl=np.s_[0:128, :])


def build_mono(upto=99):
    nc = bass.Bass("TRN2", target_bir_lowering=False)
    with ExitStack() as ctx:
        S = Sched(nc, ctx)
        RG4 = [[0, 1, 2, 3], [4, 5, 6, 7]]
        RG8 = [[0, 1, 2, 3, 4, 5, 6, 7]]
        AGB = mybir.AluOpType.bypass

        x0 = S.dram_in("x0", [TPC, DM], F32)
        xs = [Buf(nc.dram_tensor(f"xbuf{i}", [TPC, DM], F32, kind="Internal").ap(), f"x{i}")
              for i in range(8)]
        xout_ext = S.dram_out("xout_final", [TPC, DM], F32)
        hsel = S.dram_in("hsel", [128, 4], F32)
        hselb = S.sbuf("hselb", [128, 4], F32)
        S.dma("Q", hselb, hsel)
        mamba_consts(S)

        mi, mo = _wire(S, "t0_", {})
        io_t0 = {}
        io_t0["ident_in"] = mi("ident_in", [128, 128], BF16)
        io_t0["normw_b"] = mi("normw_b", [128, 4096], F32)
        for nm, rows, cols, kind in PREP_SPECS:
            tpc = prep_tiles_per_core(rows)
            io_t0[f"{nm}_in"] = mi(f"{nm}_in", [tpc * 128, cols], I32 if kind == "dq" else F32)
            io_t0[f"{nm}_out"] = mo(f"{nm}_out", [cols, tpc * 128], BF16)
            if kind == "dq":
                io_t0[f"{nm}_s"] = mi(f"{nm}_s", [tpc * 128, 1], F32)
        emit_t0(S, io_t0)
        S.fence()
        prepW = {}
        for nm, rows, cols, kind in PREP_SPECS:
            tpc = prep_tiles_per_core(rows)
            agout = Buf(nc.dram_tensor(S.uname(f"t0ag_{nm}"), [8 * cols, tpc * 128], BF16,
                                       kind="Internal").ap(), "ag")
            S.collective("AllGather", AGB, RG8, io_t0[f"{nm}_out"], agout)
            wfull = Buf(nc.dram_tensor(S.uname(f"w_{nm}"), [cols, rows], BF16,
                                       kind="Internal").ap(), nm)
            off = 0
            for r in range(8):
                valid = min(tpc * 128, rows - off)
                if valid <= 0:
                    break
                S.dma("Q", wfull, agout, in_sl=np.s_[r * cols:(r + 1) * cols, 0:valid],
                      out_sl=np.s_[:, off:off + valid])
                off += valid
            prepW[nm] = wfull
        S.fence()

        def partial_out(xbuf):
            S.dma("Q", xout_ext, xbuf)
            S.finalize()

        if upto == 0:
            S.add("P", lambda e: e.dma_start(out=xout_ext.ap,
                                             in_=prepW["l1f_gu"].ap[1024:2048, 14336:16384]),
                  reads=(prepW["l1f_gu"],), writes=(xout_ext,), is_dma=True)
            S.finalize()
            return nc
        mi, mo = _wire(S, "s1_", {"xin": x0})
        io = _io_l0a_pre.__wrapped__(S) if False else None
        io = {}
        io.update(xin=x0,
                  wq=mi("wq", [DM, DM], F32), wkv=mi("wkv", [DM, 2 * DM], F32),
                  bq=mi("bq", [1, DM], F32), bkv=mi("bkv", [1, 2 * DM], F32),
                  gq_b=mi("gq_b", [128, DM], F32), bqln_b=mi("bqln_b", [128, DM], F32),
                  gkv_b=mi("gkv_b", [128, DM], F32), bkvln_b=mi("bkvln_b", [128, DM], F32),
                  gh_b=mi("gh_b", [128, 512], F32), bh_b=mi("bh_b", [128, 512], F32),
                  cosg=mi("cosg", [TPC, 64], F32), sing=mi("sing", [TPC, 64], F32),
                  ones_in=mi("ones_in", [1, 128], BF16), ident_in=mi("ident_in", [128, 128], BF16),
                  qn=mo("qn", [TPC, DM], BF16), knT=mo("knT", [DM, TPC], BF16),
                  v=mo("v", [TPC, DM], BF16))
        emit_l0a_pre(S, io)
        S.fence()
        knT_ag = Buf(nc.dram_tensor(S.uname("s1_knT_ag"), [4 * DM, TPC], BF16,
                                    kind="Internal").ap(), "knT_ag")
        S.collective("AllGather", AGB, RG4, io["knT"], knT_ag)
        v_ag = Buf(nc.dram_tensor(S.uname("s1_v_ag"), [SEQ, DM], BF16,
                                  kind="Internal").ap(), "v_ag")
        S.collective("AllGather", AGB, RG4, io["v"], v_ag)
        knT_full = Buf(nc.dram_tensor(S.uname("s1_knT_full"), [DM, SEQ], BF16,
                                      kind="Internal").ap(), "knT_full")
        for r in range(4):
            S.dma("Q", knT_full, knT_ag, in_sl=np.s_[r * DM:(r + 1) * DM, :],
                  out_sl=np.s_[:, r * TPC:(r + 1) * TPC])
        S.fence()
        mi, mo = _wire(S, "s1b_", {})
        io2 = dict(xin=x0, qn=io["qn"], knT=knT_full, v=v_ag,
                   wo=mi("wo", [DM, DM], F32), bo=mi("bo", [1, DM], F32),
                   ones_in=mi("ones_in", [1, 128], BF16),
                   ident_in=mi("ident_in", [128, 128], BF16),
                   xout=xs[0])
        emit_l0a_attn(S, io2)
        S.fence()
        if upto == 1:
            partial_out(xs[0])
            return nc

        def do_l0ffn(xin_b, xout_b):
            mi, mo = _wire(S, "l0f_", {})
            io = dict(xin=xin_b, wgT=prepW["l0f_gate"], wuT=prepW["l0f_up"],
                      wdT=prepW["l0f_down"], ident_in=mi("ident_in", [128, 128], BF16),
                      xout=xout_b)
            emit_l0ffn(S, io)
            S.fence()
        do_l0ffn(xs[0], xs[1])
        if upto == 2:
            partial_out(xs[1])
            return nc

        mi, mo = _wire(S, "s3_", {})
        tail = Buf(nc.dram_tensor(S.uname("s3_tail"), [128, DM], F32,
                                  kind="Internal").ap(), "tail")
        S.dma("Q", tail, xs[1], in_sl=np.s_[TPC - 128:TPC, :])
        S.fence()
        tail_ag = Buf(nc.dram_tensor(S.uname("s3_tail_ag"), [4 * 128, DM], F32,
                                     kind="Internal").ap(), "tail_ag")
        S.collective("AllGather", AGB, RG4, tail, tail_ag)
        xh_t = Buf(nc.dram_tensor(S.uname("s3_xh"), [9 * 128, DM], F32,
                                  kind="Internal").ap(), "xh")
        emit_halo(S, xs[1], xh_t, hselb, tail_ag)
        S.fence()
        io = dict(xin=xh_t, wT=prepW["mmb_inproj"],
                  convw=mi("convw", [34 * 128, 4], F32), convb=mi("convb", [34 * 128, 1], F32),
                  conv=mo("conv", [34 * 128, TPC], BF16), z=mo("z", [TPC, 4096], BF16),
                  dtp=mo("dtp", [TPC, 64], F32), sloc=mo("sloc", [128, 4096], F32),
                  dloc=mo("dloc", [1, 64], F32))
        emit_mmb_pre(S, io)
        S.fence()
        sloc_ag = Buf(nc.dram_tensor(S.uname("s3_sloc_ag"), [4 * 128, 4096], F32,
                                     kind="Internal").ap(), "sloc_ag")
        S.collective("AllGather", AGB, RG4, io["sloc"], sloc_ag)
        dloc_ag = Buf(nc.dram_tensor(S.uname("s3_dloc_ag"), [4, 64], F32,
                                     kind="Internal").ap(), "dloc_ag")
        S.collective("AllGather", AGB, RG4, io["dloc"], dloc_ag)
        S.fence()
        io2 = dict(xin=xs[1], conv=io["conv"], z=io["z"], dtp=io["dtp"], slocs=sloc_ag,
                   dloc4=dloc_ag, cmb=mi("cmb", [128, 16], F32), dlocx=None,
                   wpT=prepW["mmb_outproj"], xout=xs[2])
        emit_mmb_post(S, io2, None)
        S.fence()
        if upto == 3:
            partial_out(xs[2])
            return nc

        do_l0ffn(xs[2], xs[3])
        if upto == 4:
            partial_out(xs[3])
            return nc

        mi, mo = _wire(S, "s5_", {})
        tail5 = Buf(nc.dram_tensor(S.uname("s5_tail"), [128, DM], F32,
                                   kind="Internal").ap(), "tail")
        S.dma("Q", tail5, xs[3], in_sl=np.s_[TPC - 128:TPC, :])
        S.fence()
        tail5_ag = Buf(nc.dram_tensor(S.uname("s5_tail_ag"), [4 * 128, DM], F32,
                                      kind="Internal").ap(), "tail_ag")
        S.collective("AllGather", AGB, RG4, tail5, tail5_ag)
        xh5 = Buf(nc.dram_tensor(S.uname("s5_xh"), [9 * 128, DM], F32,
                                 kind="Internal").ap(), "xh")
        emit_halo(S, xs[3], xh5, hselb, tail5_ag)
        S.fence()
        io = dict(xin=xh5,
                  wq=mi("wq", [DM, DM], F32), wk=mi("wk", [DM, DM], F32),
                  wv=mi("wv", [DM, DM], F32), ww=mi("ww", [DM, DM], F32),
                  cq=mi("cq", [DM, 4], F32), ck=mi("ck", [DM, 4], F32),
                  cv=mi("cv", [DM, 4], F32), wf1=mi("wf1", [DM, 128], F32),
                  wf2=mi("wf2", [128, DM], F32), wb=mi("wb", [DM, 16], F32),
                  wg1=mi("wg1", [DM, 128], F32), wg2=mi("wg2", [128, DM], F32),
                  bg2=mi("bg2", [1, DM], F32),
                  ident_in=mi("ident_in", [128, 128], BF16),
                  ones_in=mi("ones_in", [1, 128], BF16),
                  qfm=mo("qfm", [DM, TPC], BF16), kfm=mo("kfm", [DM, TPC], BF16),
                  vfm=mo("vfm", [DM, TPC], BF16), wfm=mo("wfm", [DM, TPC], BF16),
                  g=mo("g", [TPC, DM], F32), beta=mo("beta", [TPC, 16], F32),
                  sgate=mo("sgate", [TPC, DM], BF16))
        emit_gsa_pre(S, io)
        S.fence()
        comio = dict(ident_in=mi("ident_in", [128, 128], BF16),
                     identf_in=mi("identf_in", [128, 128], F32),
                     onesf_in=mi("onesf_in", [1, 128], F32),
                     tri64_in=mi("tri64_in", [64, 64], BF16),
                     ctri_in=mi("ctri_in", [64, 64], F32))
        io2 = dict(qfm=io["qfm"], kfm=io["kfm"], vfm=io["vfm"], wfm=io["wfm"],
                   g=io["g"], beta=io["beta"],
                   skloc=mo("skloc", [16 * 128, 128], F32),
                   svloc=mo("svloc", [16 * 128, 128], F32),
                   dloc=mo("dlocg", [1, DM], F32), **comio)
        emit_gsa_scan(S, io2, 0, False)
        S.fence()
        sk_ag = Buf(nc.dram_tensor(S.uname("s5_sk_ag"), [4 * 16 * 128, 128], F32,
                                   kind="Internal").ap(), "sk_ag")
        S.collective("AllGather", AGB, RG4, io2["skloc"], sk_ag)
        sv_ag = Buf(nc.dram_tensor(S.uname("s5_sv_ag"), [4 * 16 * 128, 128], F32,
                                   kind="Internal").ap(), "sv_ag")
        S.collective("AllGather", AGB, RG4, io2["svloc"], sv_ag)
        dlg_ag = Buf(nc.dram_tensor(S.uname("s5_dl_ag"), [4, DM], F32,
                                    kind="Internal").ap(), "dl_ag")
        S.collective("AllGather", AGB, RG4, io2["dloc"], dlg_ag)
        S.fence()
        io3 = dict(qfm=io["qfm"], kfm=io["kfm"], vfm=io["vfm"], wfm=io["wfm"],
                   g=io["g"], beta=io["beta"], xin=xs[3], sgate=io["sgate"],
                   sklocs=sk_ag, svlocs=sv_ag, dloc4=dlg_ag, dlocx=None,
                   cmb=mi("cmbg", [128, 16], F32), wo=mi("wo", [DM, DM], F32),
                   normw_b=mi("normw_b", [128, DM], F32), xout=xs[4], **comio)
        emit_gsa_scan(S, io3, None, True)
        S.fence()
        if upto == 5:
            partial_out(xs[4])
            return nc

        def do_l1ffn(xin_b, xout_b):
            mi2, _ = _wire(S, "l1f_", {})
            io = dict(xin=xin_b, wguT=prepW["l1f_gu"], wpT=prepW["l1f_proj"],
                      ident_in=mi2("ident_in", [128, 128], BF16), xout=xout_b)
            emit_l1ffn(S, io)
            S.fence()
        do_l1ffn(xs[4], xs[5])
        if upto == 6:
            partial_out(xs[5])
            return nc

        mi, mo = _wire(S, "s7_", {})
        io = dict(xin=xs[5], wqT=prepW["l1b_wq"], wkT=prepW["l1b_wk"], wvT=prepW["l1b_wv"],
                  bq=mi("bq", [1, DM], F32), bk=mi("bk", [1, 512], F32),
                  bv=mi("bv", [1, 512], F32), Rq=mi("Rq", [TPC, 128], F32),
                  Rk=mi("Rk", [TPC, 128], F32), ones_in=mi("ones_in", [1, 128], BF16),
                  ident_in=mi("ident_in", [128, 128], BF16),
                  q=mo("q", [TPC, DM], BF16), kT=mo("kT", [512, TPC], BF16),
                  v=mo("v", [TPC, 512], BF16))
        emit_l1b_pre(S, io)
        S.fence()
        kT_ag = Buf(nc.dram_tensor(S.uname("s7_kT_ag"), [4 * 512, TPC], BF16,
                                   kind="Internal").ap(), "kT_ag")
        S.collective("AllGather", AGB, RG4, io["kT"], kT_ag)
        v7_ag = Buf(nc.dram_tensor(S.uname("s7_v_ag"), [SEQ, 512], BF16,
                                   kind="Internal").ap(), "v_ag")
        S.collective("AllGather", AGB, RG4, io["v"], v7_ag)
        kT_full = Buf(nc.dram_tensor(S.uname("s7_kT_full"), [512, SEQ], BF16,
                                     kind="Internal").ap(), "kT_full")
        for r in range(4):
            S.dma("Q", kT_full, kT_ag, in_sl=np.s_[r * 512:(r + 1) * 512, :],
                  out_sl=np.s_[:, r * TPC:(r + 1) * TPC])
        S.fence()
        io2 = dict(xin=xs[5], q=io["q"], kT=kT_full, v=v7_ag,
                   woT=prepW["l1b_wo"], bo=mi("bo", [1, DM], F32),
                   cm4_in=mi("cm4_in", [4 * 128, 512], F32),
                   cmall=mi("cmall", [TPC, SEQ], F32),
                   ones_in=mi("ones_in", [1, 128], BF16),
                   ident_in=mi("ident_in", [128, 128], BF16), xout=xs[6])
        emit_l1b_attn(S, io2, None)
        S.fence()
        if upto == 7:
            partial_out(xs[6])
            return nc

        do_l1ffn(xs[6], xout_ext)
        S.finalize()
    return nc


def mono_in_maps(inputs):
    consts = host_consts()
    f32 = np.float32
    x = np.asarray(inputs["L0_attn_a_x"], f32)
    IN = lambda n: np.asarray(inputs[n], f32)

    t0maps = t0_host_inputs(inputs, consts)
    nwb_mmb = np.broadcast_to(IN("L0_attn_b_norm_w"), (128, 4096)).copy()

    shared = dict(
        ident_in=consts["ident"], identf_in=np.eye(128, dtype=f32),
        ones_in=consts["ones1x128"], onesf_in=np.ones((1, 128), f32),
        ones2_in=np.ones((2, 128), f32).astype(bfnp),
        trif_in=np.triu(np.ones((128, 128), f32)),
        nmask_in=consts["negmask128"],
        tri64_in=np.triu(np.ones((64, 64), f32)).astype(bfnp),
        ctri_in=np.triu(np.ones((64, 64), f32)),
        cm4_in=consts["cm4"],
        alog_b=np.broadcast_to(IN("L0_attn_b_A_log"), (128, 64)).copy(),
        d_b=np.broadcast_to(IN("L0_attn_b_D"), (128, 64)).copy(),
        dtb_b=np.broadcast_to(IN("L0_attn_b_dt_bias"), (128, 64)).copy(),
    )
    s1 = dict(
        s1_wq=IN("L0_attn_a_Wq"), s1_wkv=IN("L0_attn_a_Wkv"),
        s1_bq=IN("L0_attn_a_bq")[None], s1_bkv=IN("L0_attn_a_bkv")[None],
        s1_gq_b=np.broadcast_to(IN("L0_attn_a_ln_q_g"), (128, DM)).copy(),
        s1_bqln_b=np.broadcast_to(IN("L0_attn_a_ln_q_b"), (128, DM)).copy(),
        s1_gkv_b=np.broadcast_to(IN("L0_attn_a_ln_kv_g"), (128, DM)).copy(),
        s1_bkvln_b=np.broadcast_to(IN("L0_attn_a_ln_kv_b"), (128, DM)).copy(),
        s1_gh_b=np.broadcast_to(np.tile(IN("L0_attn_a_ln_h_g"), 4), (128, 512)).copy(),
        s1_bh_b=np.broadcast_to(np.tile(IN("L0_attn_a_ln_h_b"), 4), (128, 512)).copy(),
        s1b_wo=IN("L0_attn_a_Wo"), s1b_bo=IN("L0_attn_a_bo")[None],
    )
    s3 = dict(
        s3_convw=np.pad(IN("L0_attn_b_conv_w"), ((0, 34 * 128 - 4352), (0, 0))),
        s3_convb=np.pad(IN("L0_attn_b_conv_b")[:, None], ((0, 34 * 128 - 4352), (0, 0))),
    )
    s5 = dict(
        s5_wq=IN("L1_attn_a_Wq"), s5_wk=IN("L1_attn_a_Wk"), s5_wv=IN("L1_attn_a_Wv"),
        s5_ww=IN("L1_attn_a_Ww"), s5_cq=IN("L1_attn_a_cq"), s5_ck=IN("L1_attn_a_ck"),
        s5_cv=IN("L1_attn_a_cv"), s5_wf1=IN("L1_attn_a_Wf1"), s5_wf2=IN("L1_attn_a_Wf2"),
        s5_wb=IN("L1_attn_a_Wb"), s5_wg1=IN("L1_attn_a_Wg1"), s5_wg2=IN("L1_attn_a_Wg2"),
        s5_bg2=IN("L1_attn_a_bg2")[None], s5_wo=IN("L1_attn_a_Wo"),
        s5_normw_b=np.broadcast_to(np.tile(IN("L1_attn_a_norm_w"), 16), (128, DM)).copy(),
    )
    s7 = dict(
        s7_bq=IN("L1_attn_b_bq")[None], s7_bk=IN("L1_attn_b_bk")[None],
        s7_bv=IN("L1_attn_b_bv")[None], s7_bo=IN("L1_attn_b_bo")[None],
    )
    maps = []
    for c in range(NCORE):
        b, j = c // 4, c % 4
        m = dict(t0maps[c])
        m["t0_normw_b"] = nwb_mmb
        m["t0_ident_in"] = consts["ident"]
        m = {("t0_" + k if not k.startswith("t0_") else k): v for k, v in m.items()}
        m.pop("t0_ident_in", None)
        m.update(shared)
        m.update(s1)
        m.update(s3)
        m.update(s5)
        m.update(s7)
        m["x0"] = np.ascontiguousarray(x[b, j * 1024:(j + 1) * 1024])
        hs = np.zeros((128, 4), np.float32)
        if j > 0:
            hs[:, j - 1] = 1.0
        m["hsel"] = hs
        cmb = np.zeros((128, 16), np.float32)
        for i in range(3):
            a_, b_, c_ = (1.0, 0.0, 1.0) if i < j else (0.0, 1.0, 0.0)
            cmb[:, 3 * i] = a_
            cmb[:, 3 * i + 1] = b_
            cmb[:, 3 * i + 2] = c_
        m["s3_cmb"] = cmb
        m["s5_cmbg"] = cmb.copy()
        m["s1_cosg"] = np.ascontiguousarray(host_consts()["rot_cos"][j * 1024:(j + 1) * 1024]) if False else np.ascontiguousarray(consts["rot_cos"][j * 1024:(j + 1) * 1024])
        m["s1_sing"] = np.ascontiguousarray(consts["rot_sin"][j * 1024:(j + 1) * 1024])
        m["s7_Rq"] = np.ascontiguousarray(consts["Rgrid_q"][j * 1024:(j + 1) * 1024])
        m["s7_Rk"] = np.ascontiguousarray(consts["Rgrid"][j * 1024:(j + 1) * 1024])
        q0 = j * 1024
        rows = np.arange(1024)[:, None] + q0
        cols = np.arange(4096)[None, :]
        m["s7_cmall"] = np.where(cols <= rows, 0.0, -1e30).astype(np.float32)
        maps.append(m)
    return maps


_MONO_CACHE = {}


def kernel(**inputs):
    inputs = {k: np.asarray(v) for k, v in inputs.items()}
    _refs = (
        inputs["L0_attn_a_x"],
        inputs["L0_attn_a_ln_q_g"],
        inputs["L0_attn_a_ln_q_b"],
        inputs["L0_attn_a_Wq"],
        inputs["L0_attn_a_bq"],
        inputs["L0_attn_a_ln_kv_g"],
        inputs["L0_attn_a_ln_kv_b"],
        inputs["L0_attn_a_Wkv"],
        inputs["L0_attn_a_bkv"],
        inputs["L0_attn_a_ln_h_g"],
        inputs["L0_attn_a_ln_h_b"],
        inputs["L0_attn_a_Wo"],
        inputs["L0_attn_a_bo"],
        inputs["L0_attn_b_in_proj_w"],
        inputs["L0_attn_b_conv_w"],
        inputs["L0_attn_b_conv_b"],
        inputs["L0_attn_b_dt_bias"],
        inputs["L0_attn_b_A_log"],
        inputs["L0_attn_b_D"],
        inputs["L0_attn_b_norm_w"],
        inputs["L0_attn_b_out_proj_w"],
        inputs["L0_ffn_wq_gate"],
        inputs["L0_ffn_s_gate"],
        inputs["L0_ffn_wq_up"],
        inputs["L0_ffn_s_up"],
        inputs["L0_ffn_wq_down"],
        inputs["L0_ffn_s_down"],
        inputs["L1_attn_a_Wq"],
        inputs["L1_attn_a_Wk"],
        inputs["L1_attn_a_Wv"],
        inputs["L1_attn_a_Ww"],
        inputs["L1_attn_a_cq"],
        inputs["L1_attn_a_ck"],
        inputs["L1_attn_a_cv"],
        inputs["L1_attn_a_Wf1"],
        inputs["L1_attn_a_Wf2"],
        inputs["L1_attn_a_Wb"],
        inputs["L1_attn_a_Wg1"],
        inputs["L1_attn_a_Wg2"],
        inputs["L1_attn_a_bg2"],
        inputs["L1_attn_a_norm_w"],
        inputs["L1_attn_a_Wo"],
        inputs["L1_attn_b_Wq"],
        inputs["L1_attn_b_bq"],
        inputs["L1_attn_b_Wk"],
        inputs["L1_attn_b_bk"],
        inputs["L1_attn_b_Wv"],
        inputs["L1_attn_b_bv"],
        inputs["L1_attn_b_Wo"],
        inputs["L1_attn_b_bo"],
        inputs["L1_ffn_w_gate_up"],
        inputs["L1_ffn_w_proj"],
    )

    if "nc" not in _MONO_CACHE:
        _MONO_CACHE["nc"] = build_mono()
    nc = _MONO_CACHE["nc"]
    maps = mono_in_maps(inputs)
    res = run_bass_kernel_spmd(nc, maps, core_ids=list(range(NCORE)))
    out = np.zeros((2, SEQ, DM), np.float32)
    for c in range(NCORE):
        b, j = c // 4, c % 4
        out[b, j * 1024:(j + 1) * 1024] = res.results[c]["xout_final"]
    return out
